# version 23 (final)
# baseline (speedup 1.0000x reference)
import os
import numpy as np
import concourse.bass as bass
import concourse.mybir as mybir
from concourse.bass_utils import run_bass_kernel_spmd
from contextlib import ExitStack

F32 = mybir.dt.float32
BF16 = mybir.dt.bfloat16
AF = mybir.ActivationFunctionType
ALU = mybir.AluOpType
AX = mybir.AxisListType


class Res:
    __slots__ = ("writer", "writers", "readers", "name")

    def __init__(self, name=""):
        self.writer = None
        self.writers = []
        self.readers = []
        self.name = name


class Ev:
    __slots__ = ("fn", "eng", "idx", "need_inc", "snap", "blk", "sem", "kind", "seq")


class _Eng:
    def __init__(self, name, sem):
        self.name = name
        self.sem = sem
        self.events = []
        self.know = {}
        self.dknow = {}
        self.waits = {}


class DSem:
    def __init__(self, h, mode):
        self.h = h
        self.mode = mode
        self.count = 0
        self.events = []


class Prog:
    def __init__(self, nc):
        self.nc = nc
        self.blk = 0
        self.engs = {}
        self.csems = {}

    def add_engine(self, name, sem_handle):
        s = {"h": sem_handle, "events": []}
        self.csems[name] = s
        self.engs[name] = _Eng(name, s)

    def flush_block(self):
        self.blk += 1

    def _deps(self, eng, ev, reads, writes, wdisj=()):
        deps = []
        for r in reads:
            if r.writer is not None:
                deps.append(r.writer)
            deps.extend(r.writers)
        for w in writes:
            if w.writer is not None:
                deps.append(w.writer)
            deps.extend(w.writers)
            deps.extend(w.readers)
        for w in wdisj:
            if w.writer is not None:
                deps.append(w.writer)
            deps.extend(w.readers)
        seen = set()
        for d in deps:
            if id(d) in seen or d is ev:
                continue
            seen.add(id(d))
            if d.kind == "c":
                if d.eng is eng and eng.name == "t" and d.eng.name == "t":
                    continue
                if eng.know.get(d.eng.name, -1) >= d.idx:
                    continue
                d.need_inc = True
                eng.waits.setdefault(ev.idx, []).append(("c", d))
                eng.know[d.eng.name] = max(eng.know.get(d.eng.name, -1), d.idx)
                for k, v in d.snap.items():
                    eng.know[k] = max(eng.know.get(k, -1), v)
            else:
                key = "d%d" % id(d.sem)
                if d.sem.mode == "slot":
                    if eng.know.get(key, -1) >= d.seq:
                        continue
                    eng.know[key] = d.seq
                else:
                    if eng.know.get(key, -1) >= 0:
                        continue
                    eng.know[key] = 0
                eng.waits.setdefault(ev.idx, []).append(("d", d))
                for k, v in d.snap.items():
                    eng.know[k] = max(eng.know.get(k, -1), v)

    @staticmethod
    def _prune(lst, ev):
        nr = []
        for r0 in lst:
            if r0.eng is ev.eng and r0.idx <= ev.idx and r0.kind == "c" and ev.kind == "c":
                continue
            if r0.kind == "c" and ev.snap.get(r0.eng.name, -1) >= r0.idx:
                continue
            nr.append(r0)
        return nr

    def _register(self, ev, reads, writes, wdisj=()):
        for r in reads:
            r.readers = self._prune(r.readers, ev)
            r.readers.append(ev)
        for w in writes:
            w.writer = ev
            w.writers = []
            w.readers = []
        for w in wdisj:
            w.writers = self._prune(w.writers, ev)
            w.writers.append(ev)

    def emit(self, ename, fn, reads=(), writes=(), wdisj=()):
        eng = self.engs[ename]
        ev = Ev()
        ev.fn = fn
        ev.eng = eng
        ev.idx = len(eng.events)
        ev.need_inc = False
        ev.blk = self.blk
        ev.kind = "c"
        ev.sem = None
        ev.seq = 0
        ev.snap = {}
        self._deps(eng, ev, reads, writes, wdisj)
        ev.snap = dict(eng.know)
        ev.snap[eng.name] = ev.idx
        self._register(ev, reads, writes, wdisj)
        eng.events.append(ev)
        return ev

    def dma(self, ename, dsem, fn, reads=(), writes=(), wdisj=()):
        eng = self.engs[ename]
        ev = Ev()
        ev.fn = fn
        ev.eng = eng
        ev.idx = len(eng.events)
        ev.need_inc = True
        ev.blk = self.blk
        ev.kind = "d"
        ev.sem = dsem
        dsem.count += 1
        ev.seq = dsem.count
        dsem.events.append(ev)
        ev.snap = {}
        self._deps(eng, ev, reads, writes, wdisj)
        ev.snap = dict(eng.know)
        self._register(ev, reads, writes, wdisj)
        eng.events.append(ev)
        return ev

    def finalize(self, nblocks, final_waits):
        for eng in self.engs.values():
            cnt = 0
            for ev in eng.events:
                if ev.kind == "c" and ev.need_inc:
                    cnt += 1
                    ev.seq = cnt
        out = [{k: [] for k in self.engs} for _ in range(nblocks)]
        for name, eng in self.engs.items():
            hw = {}
            for ev in eng.events:
                ops = out[ev.blk][name]
                best = {}
                for kind, d in eng.waits.get(ev.idx, ()):
                    if kind == "c":
                        sem = d.eng.sem["h"]
                        val = d.seq
                    else:
                        sem = d.sem.h
                        val = 16 * (d.sem.count if d.sem.mode == "sum" else d.seq)
                    key = id(sem)
                    if val > best.get(key, (0, None))[0]:
                        best[key] = (val, sem)
                for key, (val, sem) in best.items():
                    if hw.get(key, 0) >= val:
                        continue
                    hw[key] = val
                    ops.append(lambda e, s=sem, v=val: e.wait_ge(s, v))
                if ev.kind == "d":
                    ops.append(lambda e, f=ev.fn, s=ev.sem.h: f(e).then_inc(s, 16))
                elif ev.need_inc:
                    def _mk(f, s, ev=ev):
                        def go(e):
                            r = f(e)
                            if r is None:
                                raise RuntimeError("event fn returned None: %s blk=%d idx=%d"
                                                   % (ev.eng.name, ev.blk, ev.idx))
                            return r.then_inc(s, 1)
                        return go
                    ops.append(_mk(ev.fn, eng.sem["h"]))
                else:
                    ops.append(ev.fn)
        for ename, dsem in final_waits:
            out[nblocks - 1][ename].append(
                lambda e, s=dsem.h, v=16 * dsem.count: e.wait_ge(s, v))
        return out


class CFG:
    def __init__(self, TOK=2048, SEQ=4096, H=4096, debug=False):
        self.TOK, self.SEQ, self.H = TOK, SEQ, H
        self.debug = debug
        self.D = 1024
        self.NP = 8
        self.ER = 7
        self.ND = self.D // 128
        self.NT = TOK // 128
        self.NTS = SEQ // 128
        self.QSW = min(512, TOK)
        self.NQS = TOK // self.QSW
        self.NKC = SEQ // 128
        self.NKV = SEQ // 512
        self.NHG = H // 512
        self.T4W = min(512, TOK)
        self.NT4 = TOK // self.T4W
        self.CAP = 128 * ((3 * TOK + 2047) // 2048)
        self.NSC = self.CAP // 128
        assert self.NSC <= 3


def build_nc(c: CFG):
    nc = bass.Bass("TRN2", target_bir_lowering=False, detect_race_conditions=True)
    TOK, SEQ, D, H = c.TOK, c.SEQ, c.D, c.H
    ND, NT, QSW, NQS, NKC, NKV, NHG, NT4, T4W = (
        c.ND, c.NT, c.QSW, c.NQS, c.NKC, c.NKV, c.NHG, c.NT4, c.T4W)
    TOKO = SEQ - TOK
    NTO = TOKO // 128

    xb = nc.dram_tensor("xb", [SEQ, D], F32, kind="ExternalInput").ap()
    wqkv = nc.dram_tensor("wqkv", [D, 3 * D], F32, kind="ExternalInput").ap()
    bqkv = nc.dram_tensor("bqkv", [3 * D], F32, kind="ExternalInput").ap()
    wout = nc.dram_tensor("wout", [D, D], F32, kind="ExternalInput").ap()
    bout = nc.dram_tensor("bout", [D], F32, kind="ExternalInput").ap()
    wrt = nc.dram_tensor("wrt", [D, 8], F32, kind="ExternalInput").ap()
    wup = nc.dram_tensor("wup", [c.ER, D, H], F32, kind="ExternalInput").ap()
    wgate = nc.dram_tensor("wgate", [c.ER, D, H], F32, kind="ExternalInput").ap()
    wdown = nc.dram_tensor("wdown", [c.ER, H, D], F32, kind="ExternalInput").ap()
    wsup = nc.dram_tensor("wsup", [D, H], F32, kind="ExternalInput").ap()
    wsgate = nc.dram_tensor("wsgate", [D, H], F32, kind="ExternalInput").ap()
    wsdown = nc.dram_tensor("wsdown", [H, D], F32, kind="ExternalInput").ap()
    out = nc.dram_tensor("out", [TOK, D], F32, kind="ExternalOutput").ap()
    xscr = nc.dram_tensor("xscr", [TOK, D], BF16, kind="Internal").ap()
    if c.debug:
        dbg_xf = nc.dram_tensor("dbg_xf", [128, ND * TOK], F32, kind="ExternalOutput").ap()
        dbg_xof = nc.dram_tensor("dbg_xof", [128, ND * TOKO], F32, kind="ExternalOutput").ap()
        dbg_nrm = nc.dram_tensor("dbg_nrm", [128, 3 * 512], F32, kind="ExternalOutput").ap()

    st0 = ExitStack()
    E = st0.enter_context

    P = Prog(nc)
    for en in "tvsg":
        P.add_engine(en, E(nc.semaphore("sem_" + en)))
    P.add_engine("y", E(nc.semaphore("sem_y")))
    XST = [DSem(E(nc.semaphore(f"xst{i}")), "slot") for i in range(4)]
    TPX = [DSem(E(nc.semaphore(f"tpx{i}")), "slot") for i in range(4)]
    TPV = [DSem(E(nc.semaphore(f"tpv{i}")), "slot") for i in range(2)]
    TPS = DSem(E(nc.semaphore("tps")), "slot")
    WQS = [[DSem(E(nc.semaphore(f"wqs{i}_{r}")), "slot") for r in range(3)] for i in range(2)]
    WUS = [DSem(E(nc.semaphore(f"wus{i}")), "slot") for i in range(2)]
    WGS = [DSem(E(nc.semaphore(f"wgs{i}")), "slot") for i in range(2)]
    WDS = [DSem(E(nc.semaphore(f"wds{i}")), "slot") for i in range(2)]
    MIG = DSem(E(nc.semaphore("mig")), "sum")
    MIY = DSem(E(nc.semaphore("miy")), "sum")
    XRS = DSem(E(nc.semaphore("xrs")), "slot")
    WOS = DSem(E(nc.semaphore("wos")), "slot")
    XSC = DSem(E(nc.semaphore("xsc")), "slot")
    STR = [DSem(E(nc.semaphore(f"str{i}")), "slot") for i in range(2)]
    STO = DSem(E(nc.semaphore("sto")), "sum")
    DBG = DSem(E(nc.semaphore("dbgs")), "sum")

    ident = E(nc.sbuf_tensor("ident", [128, 128], BF16))
    ones_bf = E(nc.sbuf_tensor("ones_bf", [128, 128], BF16))
    ones_fc = E(nc.sbuf_tensor("ones_fc", [128, 1], BF16))
    sel = E(nc.sbuf_tensor("sel", [8, 8 * 128], BF16))
    bq_sb = E(nc.sbuf_tensor("bq_sb", [128, 24], F32))
    bout_row = E(nc.sbuf_tensor("bout_row", [1, D], BF16))
    wrt_hi = E(nc.sbuf_tensor("wrt_hi", [128, ND * 8], BF16))
    wrt_lo = E(nc.sbuf_tensor("wrt_lo", [128, ND * 8], BF16))
    wrt32 = E(nc.sbuf_tensor("wrt32", [128, ND * 8], F32))
    m_f = E(nc.sbuf_tensor("m_f", [8, TOK], BF16))
    stat = [E(nc.sbuf_tensor(f"stat{i}", [128, 96], F32)) for i in range(2)]
    negc = E(nc.sbuf_tensor("negc", [128, 2], F32))
    csc = E(nc.sbuf_tensor("csc", [1, 16], F32))
    cbf = E(nc.sbuf_tensor("cbf", [1, 4], BF16))
    onesf32 = E(nc.sbuf_tensor("onesf32", [1, 128], F32))

    st_xf = ExitStack()
    x_f = st_xf.enter_context(nc.sbuf_tensor("x_f", [128, ND * TOK], BF16))

    R = lambda n: Res(n)
    r_xf = R("x_f")
    r_const = R("const")
    r_bq = R("bq")
    r_boutrow = R("boutrow")
    r_wrthi, r_wrtlo, r_wrt32 = R("wrthi"), R("wrtlo"), R("wrt32")
    r_mf = R("m_f")
    r_stat = [R("stat0"), R("stat1")]
    r_wub = [R("wub0"), R("wub1")]
    r_wgb = [R("wgb0"), R("wgb1")]
    r_wdb = [R("wdb0"), R("wdb1")]
    r_negc = [R("negc0"), R("negc1")]
    r_csc = R("csc")
    r_cbf = R("cbf")

    st_ab = ExitStack()
    O_all = st_ab.enter_context(nc.sbuf_tensor("O_all", [128, c.NP * TOK], BF16))
    r_oall, r_wout = R("O_all"), R("wout")

    VTW = 161
    st_a = ExitStack()
    Ea = st_a.enter_context
    xo_f = Ea(nc.sbuf_tensor("xo_f", [128, max(ND * TOKO, 2)], BF16))
    iota_s = Ea(nc.sbuf_tensor("iota_s", [128, 1280], F32))
    wq_sb = [Ea(nc.sbuf_tensor(f"wq_sb{i}", [128, ND * 384], BF16)) for i in range(2)]
    xstage = [Ea(nc.sbuf_tensor(f"xstage{i}", [128, D], BF16)) for i in range(4)]
    q_sb = [Ea(nc.sbuf_tensor(f"q_sb{i}", [128, TOK], BF16)) for i in range(2)]
    k_sb = [Ea(nc.sbuf_tensor(f"k_sb{i}", [128, SEQ], BF16)) for i in range(2)]
    vt_sb = [Ea(nc.sbuf_tensor(f"vt_sb{i}", [128, NKC * VTW], BF16)) for i in range(2)]
    p_sb = [Ea(nc.sbuf_tensor(f"p_sb{i}", [128, 2 * QSW], BF16)) for i in range(4)]
    sqbuf = Ea(nc.sbuf_tensor("sqbuf", [128, 512], BF16))
    sqw = Ea(nc.sbuf_tensor("sqw", [128, 2048], BF16))
    mstat = Ea(nc.sbuf_tensor("mstat", [128, 24], F32))
    rden = Ea(nc.sbuf_tensor("rden", [128, QSW], BF16))
    bc_sb = [Ea(nc.sbuf_tensor(f"bc_sb{i}", [128, QSW], BF16)) for i in range(2)]
    r_xof = R("xo_f")
    r_wq3 = [[R(f"wq{b}_{r}") for r in range(3)] for b in range(2)]
    r_xst = [R(f"xst{i}") for i in range(4)]
    r_q = [R("q0"), R("q1")]
    r_k = [R("k0"), R("k1")]
    r_vt = [R("vt0"), R("vt1")]
    r_p = [R(f"p{i}") for i in range(4)]
    r_sq, r_mst, r_rden = R("sq"), R("mst"), R("rden")
    r_sqw = R("sqw")
    r_bc = [R("bc0"), R("bc1")]

    pa0 = ExitStack()
    ps_tp = [pa0.enter_context(nc.psum_tensor(f"ps_tp{i}", [128, 1024], BF16))
             for i in range(2)]
    r_pstp = [R("pstp0"), R("pstp1")]

    xfv = x_f[:, :].rearrange("p (dc t) -> p dc t", t=TOK)
    xov = xo_f[:, :].rearrange("p (dc t) -> p dc t", t=TOKO) if TOKO else None

    P.emit("g", lambda e: e.iota(iota_s[:, 0:128], [[1, 128]], channel_multiplier=-1,
                                 allow_small_or_imprecise_dtypes=True), writes=[r_const])
    P.emit("g", lambda e: e.iota(
        iota_s[:, 128:128 + 8 * 128].rearrange("p (a b) -> p a b", b=128),
        [[1, 8], [0, 128]], channel_multiplier=-1,
        allow_small_or_imprecise_dtypes=True), reads=[r_const], writes=[r_const])
    P.emit("v", lambda e: e.tensor_single_scalar(ident[:, :], iota_s[:, 0:128], 0.0,
                                                 ALU.is_equal), reads=[r_const], writes=[r_const])
    P.emit("v", lambda e: e.tensor_single_scalar(sel[0:8, :], iota_s[0:8, 128:128 + 8 * 128],
                                                 0.0, ALU.is_equal), reads=[r_const], writes=[r_const])
    P.emit("v", lambda e: e.memset(ones_bf[:, :], 1.0), reads=[r_const], writes=[r_const])
    P.emit("v", lambda e: e.memset(ones_fc[:, :], 1.0), reads=[r_const], writes=[r_const])
    P.emit("v", lambda e: e.memset(onesf32[0:1, :], 1.0), reads=[r_const], writes=[r_const])
    for par in range(2):
        vtr = vt_sb[par][:, :].rearrange("p (kc w) -> p kc w", w=VTW)
        P.emit("v", lambda e, par=par: e.memset(vt_sb[par][:, :], 0.0), writes=[r_vt[par]])
        P.emit("v", lambda e, vtr=vtr, par=par: e.memset(vtr[:, :, 64:66], 1.0),
               reads=[r_vt[par]], writes=[r_vt[par]])

    P.dma("g", MIG, lambda e: e.dma_start(bq_sb[:, :].rearrange("p (r j) -> p r j", r=3),
                                          bqkv.rearrange("(r j p) -> p r j", p=128, r=3),
                                          allow_slow_non_contiguous=True),
          writes=[r_bq])
    P.dma("g", MIG, lambda e: e.dma_start(bout_row[0:1, :],
                                          bout.rearrange("(a d) -> a d", a=1)),
          writes=[r_boutrow])
    P.dma("g", MIG, lambda e: e.dma_start(wrt_hi[:, :].rearrange("p (dc j) -> p dc j", j=8),
                                          wrt.rearrange("(dc p) j -> p dc j", p=128)),
          writes=[r_wrthi])
    P.dma("y", MIY, lambda e: e.dma_start(wrt32[:, :].rearrange("p (dc j) -> p dc j", j=8),
                                          wrt.rearrange("(dc p) j -> p dc j", p=128)),
          writes=[r_wrt32])
    P.emit("v", lambda e: e.tensor_scalar_mul(bq_sb[:, 0:8], bq_sb[:, 0:8], 0.25),
           reads=[r_bq], writes=[r_bq])
    P.emit("v", lambda e: e.tensor_scalar_mul(bq_sb[:, 8:16], bq_sb[:, 8:16], 0.5),
           reads=[r_bq], writes=[r_bq])
    P.emit("v", lambda e: e.tensor_sub(wrt_lo[:, :], wrt32[:, :], wrt_hi[:, :]),
           reads=[r_wrt32, r_wrthi], writes=[r_wrtlo])

    def wq_load(p):
        buf = p % 2
        for r in range(3):
            P.dma("g", WQS[buf][r], lambda e, p=p, buf=buf, r=r: e.dma_start(
                wq_sb[buf][:, :].rearrange("p (dc r cc) -> p dc r cc", r=3, cc=128)[:, :, r, :],
                bass.AP(wqkv.tensor, r * 1024 + p * 128,
                        [[3072, 128], [128 * 3072, ND], [1, 128]])),
                writes=[r_wq3[buf][r]])

    wq_load(0)
    if c.NP > 1:
        wq_load(1)

    def emit_xtile(j):
        for dc in range(ND):
            P.emit("t", lambda e, j=j, dc=dc: e.transpose(
                ps_tp[j % 2][:, dc * 128:(dc + 1) * 128],
                xstage[j % 4][:, dc * 128:(dc + 1) * 128], ident[:, :]),
                reads=[r_xst[j % 4], r_const], writes=[r_pstp[j % 2]])
        if j < NT:
            P.emit("v", lambda e, j=j: e.tensor_copy(
                xfv[:, :, j * 128:(j + 1) * 128],
                ps_tp[j % 2][:, :].rearrange("p (dc t) -> p dc t", t=128)),
                reads=[r_pstp[j % 2]], writes=[r_xf])
        else:
            P.emit("v", lambda e, j=j: e.tensor_copy(
                xov[:, :, (j - NT) * 128:(j - NT + 1) * 128],
                ps_tp[j % 2][:, :].rearrange("p (dc t) -> p dc t", t=128)),
                reads=[r_pstp[j % 2]], writes=[r_xof])

    for i in range(c.NTS):
        P.dma("g", XST[i % 4], lambda e, i=i: e.dma_start(
            xstage[i % 4][:, :], xb[i * 128:(i + 1) * 128, :]), writes=[r_xst[i % 4]])
        if i >= 2:
            emit_xtile(i - 2)
    for j in range(max(0, c.NTS - 2), c.NTS):
        emit_xtile(j)

    if c.debug:
        P.dma("g", DBG, lambda e: e.dma_start(dbg_xf, x_f[:, :]), reads=[r_xf])
        P.dma("g", DBG, lambda e: e.dma_start(dbg_xof, xo_f[:, :]), reads=[r_xof])

    pa0.close()
    pa = ExitStack()
    Eap = pa.enter_context
    ps_pool = [Eap(nc.psum_tensor(f"ps_pool{i}", [128, 1024], F32)) for i in range(3)]
    ps_av = [Eap(nc.psum_tensor(f"ps_av{i}", [128, 512], F32)) for i in range(2)]
    r_pool = [R(f"pool{i}") for i in range(3)]
    r_psav = [R("psav0"), R("psav1")]
    pctr = [0]
    pend_slots = {}

    def slot():
        for _ in range(4):
            i = pctr[0] % 3
            pctr[0] += 1
            if pend_slots.get(i, 0) == 0:
                return i
        raise RuntimeError("no free psum slot")

    def slot_S():
        i = slot()
        pend_slots[i] = pend_slots.get(i, 0) + 1
        return i
    P.emit("t", lambda e: e.matmul(ps_pool[0][0:1, 0:1], ones_fc[0:1, 0:1], ones_fc[0:1, 0:1],
                                   start=True, stop=True),
           reads=[r_const], writes=r_pool + r_pstp + r_psav)

    vtview = [bass.AP(vt_sb[i], 0, [[NKC * VTW, 128], [VTW, 4], [97, 2], [1, 64]])
              for i in range(2)]

    def mk_proj_thunks(p):
        par = p % 2
        buf = p % 2
        th = []

        def proj_group(kind, j0, vsl=None):
            r = {"q": 0, "k": 1, "v": 2}[kind]
            lim = TOK if kind == "q" else SEQ

            def go():
                si_ = slot()
                prj = ps_pool[si_]
                r_prj = r_pool[si_]
                if kind == "v":
                    for k2 in range(2):
                        for kq in range(4):
                            t0 = (j0 + k2) * 512 + kq * 128
                            for dc in range(ND):
                                def mmv(e, dc=dc, k2=k2, kq=kq, t0=t0, prj=prj):
                                    lh = (xfv[:, dc, t0:t0 + 128] if t0 < TOK
                                          else xov[:, dc, t0 - TOK:t0 - TOK + 128])
                                    return e.matmul(
                                        prj[:, k2 * 512 + kq * 128:k2 * 512 + (kq + 1) * 128],
                                        lh, wq_sb[buf][:, dc * 384 + 256:dc * 384 + 384],
                                        start=(dc == 0), stop=(dc == ND - 1))
                                rsrc = r_xf if t0 < TOK else r_xof
                                P.emit("t", mmv, reads=[r_wq3[buf][r], rsrc], writes=[r_prj])
                    for k2 in range(2):
                        kcb = (j0 + k2) * 4
                        P.emit("v", lambda e, kcb=kcb, k2=k2, prj=prj, par=par: e.tensor_copy(
                            bass.AP(vt_sb[par], kcb * VTW,
                                    [[NKC * VTW, 128], [VTW, 4], [97, 2], [1, 64]]),
                            prj[:, k2 * 512:(k2 + 1) * 512].rearrange(
                                "p (kc h w) -> p kc h w", h=2, w=64)),
                            reads=[r_prj], wdisj=[r_vt[par]])
                    return
                for dc in range(ND):
                    for k2 in range(2):
                        if (j0 + k2) * 512 >= lim:
                            continue
                        def mm(e, dc=dc, k2=k2, j0=j0, r=r, prj=prj):
                            t0 = (j0 + k2) * 512
                            if kind == "q":
                                rhs = xfv[:, dc, t0:t0 + 512]
                            else:
                                rhs = (xfv[:, dc, t0:t0 + 512] if t0 < TOK
                                       else xov[:, dc, t0 - TOK:t0 - TOK + 512])
                            return e.matmul(
                                prj[:, k2 * 512:(k2 + 1) * 512],
                                wq_sb[buf][:, dc * 384 + r * 128:dc * 384 + (r + 1) * 128],
                                rhs, start=(dc == 0), stop=(dc == ND - 1))
                        rsrc = r_xf if (kind == "q" or (j0 + k2) * 512 < TOK) else r_xof
                        P.emit("t", mm, reads=[r_wq3[buf][r], rsrc], writes=[r_prj])
                for k2 in range(2):
                    if (j0 + k2) * 512 >= lim:
                        continue
                    if kind == "q":
                        P.emit("v", lambda e, k2=k2, j0=j0, prj=prj: e.tensor_scalar(
                            q_sb[par][:, (j0 + k2) * 512:(j0 + k2 + 1) * 512],
                            prj[:, k2 * 512:(k2 + 1) * 512], 0.25, bq_sb[:, p:p + 1],
                            ALU.mult, ALU.add),
                            reads=[r_prj, r_bq], writes=[r_q[par]])
                    else:
                        P.emit("v", lambda e, k2=k2, j0=j0, prj=prj: e.tensor_scalar(
                            k_sb[par][:, (j0 + k2) * 512:(j0 + k2 + 1) * 512],
                            prj[:, k2 * 512:(k2 + 1) * 512], 0.5, bq_sb[:, 8 + p:9 + p],
                            ALU.mult, ALU.add),
                            reads=[r_prj, r_bq], writes=[r_k[par]])
            return go

        for j0 in range(0, NQS, 2):
            th.append(proj_group("q", j0))
        for j0 in range(0, NKV, 2):
            th.append(proj_group("k", j0))

        def cchain():
            nq = max(1, TOK // 512)
            nk = SEQ // 512
            nall = nq + nk
            for j in range(nall):
                src = q_sb[par] if j < nq else k_sb[par]
                rr = r_q[par] if j < nq else r_k[par]
                jj = j if j < nq else j - nq
                P.emit("v", lambda e, src=src, jj=jj, j=j: e.tensor_mul(
                    sqw[:, (j % 4) * 512:(j % 4) * 512 + 512],
                    src[:, jj * 512:(jj + 1) * 512], src[:, jj * 512:(jj + 1) * 512]),
                    reads=[rr], writes=[r_sqw])
                if j % 4 == 3 or j == nall - 1:
                    base = j - (j % 4)
                    si_ = slot()
                    prj = ps_pool[si_]
                    for m in range(j % 4 + 1):
                        P.emit("t", lambda e, m=m, prj=prj: e.matmul(
                            prj[0:1, m * 256:m * 256 + 256],
                            ones_fc[:, 0:1],
                            sqw[:, m * 512:m * 512 + 512].rearrange(
                                "p (a b) -> p a b", a=2)[:, 0, :]
                            if False else sqw[:, m * 512:m * 512 + 256],
                            start=True, stop=True),
                            reads=[r_sqw, r_const], writes=[r_pool[si_]]) if False else None
                        P.emit("t", lambda e, m=m, prj=prj: e.matmul(
                            prj[0:1, m * 512:m * 512 + 512], ones_fc[:, 0:1],
                            sqw[:, m * 512:m * 512 + 512], start=True, stop=True),
                            reads=[r_sqw, r_const], writes=[r_pool[si_]]) if m < 2 else None
                        if m >= 2:
                            si2 = slot()
                            prj2 = ps_pool[si2]
                            P.emit("t", lambda e, m=m, prj2=prj2: e.matmul(
                                prj2[0:1, (m - 2) * 512:(m - 2) * 512 + 512], ones_fc[:, 0:1],
                                sqw[:, m * 512:m * 512 + 512], start=True, stop=True),
                                reads=[r_sqw, r_const], writes=[r_pool[si2]])
                            P.emit("v", lambda e, base=base, m=m, prj2=prj2: e.reduce_max(
                                csc[0:1, base + m:base + m + 1],
                                prj2[0:1, (m - 2) * 512:(m - 2) * 512 + 512], axis=AX.X),
                                reads=[r_pool[si2]], writes=[r_csc])
                        else:
                            P.emit("v", lambda e, base=base, m=m, prj=prj: e.reduce_max(
                                csc[0:1, base + m:base + m + 1],
                                prj[0:1, m * 512:m * 512 + 512], axis=AX.X),
                                reads=[r_pool[si_]], writes=[r_csc])
            P.emit("v", lambda e, nq=nq: e.reduce_max(csc[0:1, 12:13], csc[0:1, 0:nq],
                                                      axis=AX.X), reads=[r_csc], writes=[r_csc])
            P.emit("v", lambda e, nq=nq, nk=nk: e.reduce_max(csc[0:1, 13:14],
                                                             csc[0:1, nq:nq + nk], axis=AX.X),
                   reads=[r_csc], writes=[r_csc])
            P.emit("v", lambda e: e.tensor_add(csc[0:1, 14:15], csc[0:1, 12:13],
                                               csc[0:1, 13:14]), reads=[r_csc], writes=[r_csc])
            P.emit("v", lambda e: e.tensor_scalar_mul(csc[0:1, 15:16], csc[0:1, 14:15], -0.51),
                   reads=[r_csc], writes=[r_csc])
            P.emit("v", lambda e: e.tensor_copy(cbf[0:1, 0:1], csc[0:1, 15:16]),
                   reads=[r_csc], writes=[r_cbf])
            si_ = slot()
            prj = ps_pool[si_]
            P.emit("t", lambda e, prj=prj: e.matmul(prj[:, 1022:1023], ones_bf[0:1, :],
                                                    cbf[0:1, 0:1], start=True, stop=True),
                   reads=[r_cbf, r_const], writes=[r_pool[si_]])
            P.emit("s", lambda e, par=par, prj=prj: e.copy(negc[:, par:par + 1],
                                                           prj[:, 1022:1023]),
                   reads=[r_pool[si_]], writes=[r_negc[par]])
        th.append(cchain)
        for idx, j0 in enumerate(range(0, NKV, 2)):
            th.append(proj_group("v", j0, vsl=0))
        if p + 2 < c.NP:
            th.append(lambda: wq_load(p + 2))
        return th

    for t_ in mk_proj_thunks(0):
        t_()

    pend = []
    ilvd = [0]
    norm_pend = [None]
    for p in range(c.NP):
        par = p % 2
        if p + 1 < c.NP:
            pend = mk_proj_thunks(p + 1)
            ilvd[0] = 0

        smap = {}

        def emit_S(qs, kc, p=p, par=par):
            si_ = slot_S()
            smap[kc] = si_
            sab = ps_pool[si_]
            P.emit("t", lambda e, kc=kc, qs=qs, sab=sab: e.matmul(
                sab[:, 0:QSW], k_sb[par][0:64, kc * 128:(kc + 1) * 128],
                q_sb[par][0:64, qs * QSW:(qs + 1) * QSW], start=True, stop=True,
                tile_position=(0, 0)),
                reads=[r_k[par], r_q[par]], writes=[r_pool[si_]])
            P.emit("t", lambda e, kc=kc, qs=qs, sab=sab: e.matmul(
                sab[:, QSW:2 * QSW], k_sb[par][64:128, kc * 128:(kc + 1) * 128],
                q_sb[par][64:128, qs * QSW:(qs + 1) * QSW], start=True, stop=True,
                tile_position=(64, 0)),
                reads=[r_k[par], r_q[par]], writes=[r_pool[si_]])

        def emit_exp(qs, kc, p=p, par=par):
            sp = kc % 4
            si_ = smap.pop(kc)
            pend_slots[si_] -= 1
            sab = ps_pool[si_]
            P.emit("s", lambda e, sab=sab, sp=sp: e.activation(
                p_sb[sp][:, 0:2 * QSW], sab[:, 0:2 * QSW], AF.Exp,
                bias=negc[:, par:par + 1], scale=1.0),
                reads=[r_pool[si_], r_negc[par]], writes=[r_p[sp]])

        def emit_AV(qs, kc, p=p, par=par):
            sp = kc % 4
            P.emit("t", lambda e, kc=kc, sp=sp: e.matmul(
                ps_av[0][:, 0:QSW], vt_sb[par][:, kc * VTW:kc * VTW + 128],
                p_sb[sp][:, 0:QSW], start=(kc == 0), stop=(kc == NKC - 1)),
                reads=[r_p[sp], r_vt[par]], writes=[r_psav[0]])
            P.emit("t", lambda e, kc=kc, sp=sp: e.matmul(
                ps_av[1][:, 0:QSW], vt_sb[par][:, kc * VTW + 33:kc * VTW + 161],
                p_sb[sp][:, QSW:2 * QSW], start=(kc == 0), stop=(kc == NKC - 1)),
                reads=[r_p[sp], r_vt[par]], writes=[r_psav[1]])

        def emit_norm(p, qs):
            P.emit("s", lambda e: e.copy(rden[64:65, 0:QSW], ps_av[0][64:65, 0:QSW]),
                   reads=[r_psav[0]], writes=[r_rden])
            P.emit("s", lambda e: e.copy(rden[32:33, 0:QSW], ps_av[1][32:33, 0:QSW]),
                   reads=[r_psav[1], r_rden], writes=[r_rden])
            si_ = slot()
            prj = ps_pool[si_]
            r_prj = r_pool[si_]
            P.emit("t", lambda e, prj=prj: e.matmul(prj[:, 0:QSW], ones_bf[64:65, :],
                                                    rden[64:65, 0:QSW], start=True, stop=True),
                   reads=[r_rden, r_const], writes=[r_prj])
            P.emit("t", lambda e, prj=prj: e.matmul(prj[:, 512:512 + QSW], ones_bf[32:33, :],
                                                    rden[32:33, 0:QSW], start=True, stop=True),
                   reads=[r_rden, r_const], writes=[r_prj])

            def _recipA(e, prj=prj):
                with nc.allow_low_precision(reason="bf16 softmax denom (honest-bf16 path)"):
                    return e.reciprocal(bc_sb[0][:, 0:QSW], prj[:, 0:QSW])

            def _recipB(e, prj=prj):
                with nc.allow_low_precision(reason="bf16 softmax denom (honest-bf16 path)"):
                    return e.reciprocal(bc_sb[1][:, 0:QSW], prj[:, 512:512 + QSW])
            P.emit("v", _recipA, reads=[r_prj], writes=[r_bc[0]])
            P.emit("v", _recipB, reads=[r_prj], writes=[r_bc[1]])
            P.emit("v", lambda e, p=p, qs=qs: e.tensor_mul(
                O_all[0:64, p * TOK + qs * QSW:p * TOK + (qs + 1) * QSW],
                ps_av[0][0:64, 0:QSW], bc_sb[0][0:64, 0:QSW]),
                reads=[r_psav[0], r_bc[0]], writes=[r_oall])
            P.emit("v", lambda e, p=p, qs=qs: e.tensor_mul(
                O_all[64:128, p * TOK + qs * QSW:p * TOK + (qs + 1) * QSW],
                ps_av[1][64:128, 0:QSW], bc_sb[1][64:128, 0:QSW]),
                reads=[r_psav[1], r_bc[1]], writes=[r_oall])
            if c.debug and p == 0 and qs == 1:
                P.emit("v", lambda e: e.tensor_copy(x_f[:, 0:512], ps_av[0][:, 0:512]),
                       reads=[r_psav[0]], writes=[r_xf]) if False else None
                P.dma("g", DBG, lambda e: e.dma_start(dbg_nrm[:, 0:512], bc_sb[0][:, 0:512]),
                      reads=[r_bc[0]])
                P.dma("g", DBG, lambda e: e.dma_start(dbg_nrm[64:65, 512:1024],
                                                      rden[64:65, 0:512]), reads=[r_rden])
                P.dma("g", DBG, lambda e: e.dma_start(dbg_nrm[32:33, 512:1024],
                                                      rden[32:33, 0:512]), reads=[r_rden])
                P.dma("g", DBG, lambda e: e.dma_start(dbg_nrm[:, 1024:1026], negc[:, 0:2]),
                      reads=[r_negc[0], r_negc[1]])

        n_iters = NQS * (NKC // 2 + 3)
        pop_every = max(1, n_iters // (len(pend) + 2)) if pend else 1
        itc = 0
        for qs in range(NQS):
            for j in range(0, NKC + 6, 2):
                if 2 <= j <= NKC:
                    emit_exp(qs, j - 2)
                    emit_exp(qs, j - 1)
                if j < NKC:
                    emit_S(qs, j)
                    emit_S(qs, j + 1)
                if j == 0 and norm_pend[0] is not None:
                    emit_norm(*norm_pend[0])
                    norm_pend[0] = None
                if 4 <= j - 0 and j - 4 < NKC:
                    emit_AV(qs, j - 4)
                    emit_AV(qs, j - 3)
                if pend and itc % pop_every == 0 and os.environ.get("NOILV") is None:
                    pend.pop(0)()
                itc += 1
            norm_pend[0] = (p, qs)
        while pend:
            pend.pop(0)()
    if norm_pend[0] is not None:
        emit_norm(*norm_pend[0])
        norm_pend[0] = None

    P.flush_block()
    st_a.close()
    pa.close()

    st_bend = ExitStack()
    Ebe = st_bend.enter_context
    x_res = Ebe(nc.sbuf_tensor("x_res", [128, NT * 1024], F32, side="right"))
    r_xres = [R(f"xres{t}") for t in range(NT)]
    r_xscr = R("xscr")
    st_b = ExitStack()
    Eb = st_b.enter_context
    wout_sb = Eb(nc.sbuf_tensor("wout_sb", [128, ND * 1024], BF16))
    b_bc = Eb(nc.sbuf_tensor("b_bc", [128, D], BF16))
    bqv = Eb(nc.sbuf_tensor("bqv", [128, 8], BF16))
    bvw_row = Eb(nc.sbuf_tensor("bvw_row", [1, D], BF16))
    r_bqv, r_bvw = R("bqv"), R("bvw")
    xhi_t = [Eb(nc.sbuf_tensor("xhi_t0", [128, D], BF16))]
    xlo_t = [Eb(nc.sbuf_tensor("xlo_t0", [128, D], BF16))]
    xlo_f = [Eb(nc.sbuf_tensor("xlo_f0", [128, D], BF16))]
    xsq = Eb(nc.sbuf_tensor("xsq", [128, 8 * D], BF16))
    exq = Eb(nc.sbuf_tensor("exq", [128, 64], F32))
    eqb = Eb(nc.sbuf_tensor("eqb", [128, 8], F32))
    mtok = Eb(nc.sbuf_tensor("mtok", [128, 8], BF16))
    r_bbc = R("b_bc")
    r_xhi = [R("xhi0")]
    r_xlo = [R("xlo0")]
    r_xlof = [R("xlof0")]
    r_xsq, r_exq, r_eqb, r_mtok = R("xsq"), R("exq"), R("eqb"), R("mtok")

    pb = ExitStack()
    Ebp = pb.enter_context
    ps_u = [Ebp(nc.psum_tensor(f"ps_u{i}", [128, 1024], F32)) for i in range(2)]
    ps_lg = Ebp(nc.psum_tensor("ps_lg", [128, 512], F32))
    ps_tb = [Ebp(nc.psum_tensor(f"ps_tb{i}", [128, 1024], BF16)) for i in range(2)]
    r_psu = [R("psu0"), R("psu1")]
    r_pslg = R("pslg")
    r_pstb = [R("pstb0"), R("pstb1")]

    a_sbuf_res = ([r_xof, r_sq, r_mst, r_rden] + r_q + r_k + r_vt +
                  r_wq3[0] + r_wq3[1] + r_xst + r_p + r_bc)
    P.dma("y", XRS, lambda e: e.dma_start(
        bass.AP(x_res, 0, [[NT * 1024, 128], [1024, NT], [1, 1024]]),
        bass.AP(xb.tensor, 0, [[1024, 128], [128 * 1024, NT], [1, 1024]])),
        writes=r_xres + a_sbuf_res)
    P.dma("g", WOS, lambda e: e.dma_start(
        wout_sb[:, :].rearrange("p (dc j) -> p dc j", j=1024),
        wout.rearrange("(dc p) j -> p dc j", p=128)),
        writes=[r_wout] + a_sbuf_res)

    a_psum_res = r_pool + r_psav
    P.emit("v", lambda e: e.tensor_copy(bqv[:, 0:8], bq_sb[:, 16:24]),
           reads=[r_bq], writes=[r_bqv])
    for half in range(2):
        P.emit("t", lambda e, half=half: e.matmul(
            ps_u[0][:, half * 512:(half + 1) * 512], ones_bf[0:1, :],
            bout_row[0:1, half * 512:(half + 1) * 512], start=True, stop=False),
            reads=[r_boutrow, r_const], writes=[r_psu[0]] + (a_psum_res if half == 0 else []))
    for pr in range(c.NP):
        for half in range(2):
            P.emit("t", lambda e, pr=pr, half=half: e.matmul(
                ps_u[1][0:1, half * 512:(half + 1) * 512], bqv[:, pr:pr + 1],
                wout_sb[:, pr * 1024 + half * 512:pr * 1024 + (half + 1) * 512],
                start=(pr == 0), stop=(pr == c.NP - 1)),
                reads=[r_bqv, r_wout], writes=[r_psu[1]])
    P.emit("s", lambda e: e.copy(bvw_row[0:1, :], ps_u[1][0:1, :]),
           reads=[r_psu[1]], writes=[r_bvw])
    for half in range(2):
        P.emit("t", lambda e, half=half: e.matmul(
            ps_u[0][:, half * 512:(half + 1) * 512], ones_bf[0:1, :],
            bvw_row[0:1, half * 512:(half + 1) * 512], start=False, stop=True),
            reads=[r_bvw, r_const], writes=[r_psu[0]])
    P.emit("s", lambda e: e.copy(b_bc[:, :], ps_u[0][:, :]), reads=[r_psu[0]], writes=[r_bbc])
    for tt in range(NT):
        P.emit("v", lambda e, tt=tt: e.tensor_add(
            x_res[:, tt * 1024:(tt + 1) * 1024], x_res[:, tt * 1024:(tt + 1) * 1024],
            b_bc[:, :]), reads=[r_bbc, r_xres[tt]], writes=[r_xres[tt]])

    GB = min(8, NT)
    for g0 in range(0, NT, GB):
        gi = g0 // GB
        st = gi % 2
        rs = r_stat[st]
        sb_ = stat[st]
        for g in range(GB):
            tt = g0 + g
            for half in range(2):
                for pr in range(c.NP):
                    P.emit("t", lambda e, pr=pr, half=half, tt=tt: e.matmul(
                        ps_u[0][:, half * 512:(half + 1) * 512],
                        O_all[0:64, pr * TOK + tt * 128:pr * TOK + (tt + 1) * 128],
                        wout_sb[0:64, pr * 1024 + half * 512:pr * 1024 + (half + 1) * 512],
                        start=(pr == 0), stop=(pr == c.NP - 1), tile_position=(0, 0)),
                        reads=[r_oall, r_wout], writes=[r_psu[0]])
                    P.emit("t", lambda e, pr=pr, half=half, tt=tt: e.matmul(
                        ps_u[1][:, half * 512:(half + 1) * 512],
                        O_all[64:128, pr * TOK + tt * 128:pr * TOK + (tt + 1) * 128],
                        wout_sb[64:128, pr * 1024 + half * 512:pr * 1024 + (half + 1) * 512],
                        start=(pr == 0), stop=(pr == c.NP - 1), tile_position=(64, 0)),
                        reads=[r_oall, r_wout], writes=[r_psu[1]])
            P.emit("v", lambda e, tt=tt: e.tensor_add(
                x_res[:, tt * 1024:(tt + 1) * 1024], x_res[:, tt * 1024:(tt + 1) * 1024],
                ps_u[0][:, :]), reads=[r_psu[0], r_xres[tt]], writes=[r_xres[tt]])
            P.emit("v", lambda e, tt=tt: e.tensor_add(
                x_res[:, tt * 1024:(tt + 1) * 1024], x_res[:, tt * 1024:(tt + 1) * 1024],
                ps_u[1][:, :]), reads=[r_psu[1], r_xres[tt]], writes=[r_xres[tt]])
            P.emit("v", lambda e, tt=tt, g=g, sb_=sb_: e.reduce_sum(
                sb_[:, g:g + 1], x_res[:, tt * 1024:(tt + 1) * 1024], axis=AX.X),
                reads=[r_xres[tt]], writes=[rs])
            P.emit("v", lambda e, tt=tt, g=g: e.tensor_mul(
                xsq[:, g * D:(g + 1) * D], x_res[:, tt * 1024:(tt + 1) * 1024],
                x_res[:, tt * 1024:(tt + 1) * 1024]), reads=[r_xres[tt]], writes=[r_xsq])
            P.emit("v", lambda e, g=g, sb_=sb_: e.reduce_sum(
                sb_[:, 8 + g:9 + g], xsq[:, g * D:(g + 1) * D], axis=AX.X),
                reads=[r_xsq], writes=[rs])
        P.emit("v", lambda e, sb_=sb_: e.tensor_scalar_mul(sb_[:, 16:16 + GB], sb_[:, 0:GB],
                                                           1.0 / D), reads=[rs], writes=[rs])
        P.emit("v", lambda e, sb_=sb_: e.tensor_scalar_mul(sb_[:, 24:24 + GB], sb_[:, 8:8 + GB],
                                                           1.0 / D), reads=[rs], writes=[rs])
        P.emit("v", lambda e, sb_=sb_: e.tensor_mul(sb_[:, 32:32 + GB], sb_[:, 16:16 + GB],
                                                    sb_[:, 16:16 + GB]), reads=[rs], writes=[rs])
        P.emit("v", lambda e, sb_=sb_: e.tensor_sub(sb_[:, 40:40 + GB], sb_[:, 24:24 + GB],
                                                    sb_[:, 32:32 + GB]), reads=[rs], writes=[rs])
        P.emit("v", lambda e, sb_=sb_: e.tensor_scalar_add(sb_[:, 40:40 + GB],
                                                           sb_[:, 40:40 + GB], 1e-5),
               reads=[rs], writes=[rs])
        P.emit("s", lambda e, sb_=sb_: e.activation(sb_[:, 48:48 + GB], sb_[:, 40:40 + GB],
                                                    AF.Ln, bias=0.0, scale=1.0),
               reads=[rs], writes=[rs])
        P.emit("s", lambda e, sb_=sb_: e.activation(sb_[:, 56:56 + GB], sb_[:, 48:48 + GB],
                                                    AF.Exp, bias=0.0, scale=-0.5),
               reads=[rs], writes=[rs])
        for g in range(GB):
            tt = g0 + g
            P.emit("v", lambda e, tt=tt, g=g, sb_=sb_: e.tensor_scalar(
                x_res[:, tt * 1024:(tt + 1) * 1024], x_res[:, tt * 1024:(tt + 1) * 1024],
                sb_[:, 16 + g:17 + g], sb_[:, 56 + g:57 + g], ALU.subtract, ALU.mult),
                reads=[rs, r_xres[tt]], writes=[r_xres[tt]])
            P.emit("s", lambda e, tt=tt: e.copy(xhi_t[0][:, :],
                                                x_res[:, tt * 1024:(tt + 1) * 1024]),
                   reads=[r_xres[tt]], writes=[r_xhi[0]])
            P.dma("y", XSC, lambda e, tt=tt: e.dma_start(
                xscr[tt * 128:(tt + 1) * 128, :], xhi_t[0][:, :]),
                reads=[r_xhi[0]], writes=[r_xscr])
            P.emit("v", lambda e, tt=tt: e.tensor_sub(xlo_t[0][:, :],
                                                      x_res[:, tt * 1024:(tt + 1) * 1024],
                                                      xhi_t[0][:, :]),
                   reads=[r_xres[tt], r_xhi[0]], writes=[r_xlo[0]])
            for dc in range(ND):
                P.emit("t", lambda e, dc=dc, st=st: e.transpose(
                    ps_tb[st][:, dc * 128:(dc + 1) * 128], xhi_t[0][:, dc * 128:(dc + 1) * 128],
                    ident[:, :]), reads=[r_xhi[0], r_const], writes=[r_pstb[st]])
            P.emit("s", lambda e, tt=tt, st=st: e.copy(
                xfv[:, :, tt * 128:(tt + 1) * 128],
                ps_tb[st][:, :].rearrange("p (dc t) -> p dc t", t=128)),
                reads=[r_pstb[st]], writes=[r_xf])
            for dc in range(ND):
                P.emit("t", lambda e, dc=dc, st=st: e.transpose(
                    ps_tb[st][:, dc * 128:(dc + 1) * 128], xlo_t[0][:, dc * 128:(dc + 1) * 128],
                    ident[:, :]), reads=[r_xlo[0], r_const], writes=[r_pstb[st]])
            P.emit("s", lambda e, st=st: e.copy(xlo_f[0][:, :], ps_tb[st][:, :]),
                   reads=[r_pstb[st]], writes=[r_xlof[0]])
            n_mm = 3 * ND
            i_mm = 0
            for dc in range(ND):
                for (xs, ws, rws) in ((0, wrt_hi, r_wrthi), (0, wrt_lo, r_wrtlo),
                                      (1, wrt_hi, r_wrthi)):
                    i_mm += 1
                    P.emit("t", lambda e, dc=dc, xs=xs, ws=ws, tt=tt, g=g,
                           first=(i_mm == 1), last=(i_mm == n_mm): e.matmul(
                        ps_lg[:, g * 8:g * 8 + 8],
                        (xfv[:, dc, tt * 128:(tt + 1) * 128] if xs == 0
                         else xlo_f[0][:, dc * 128:(dc + 1) * 128]),
                        ws[:, dc * 8:(dc + 1) * 8], start=first, stop=last),
                        reads=[r_xf if xs == 0 else r_xlof[0], rws], writes=[r_pslg])
        P.emit("v", lambda e, sb_=sb_: e.reduce_max(
            sb_[:, 64:64 + GB],
            ps_lg[:, 0:GB * 8].rearrange("p (g c) -> p g c", c=8), axis=AX.X),
            reads=[r_pslg], writes=[rs])
        P.emit("v", lambda e, sb_=sb_: e.tensor_scalar_mul(sb_[:, 72:72 + GB],
                                                           sb_[:, 64:64 + GB], -1.0),
               reads=[rs], writes=[rs])
        for g in range(GB):
            P.emit("s", lambda e, g=g, sb_=sb_: e.activation(
                exq[:, g * 8:g * 8 + 8], ps_lg[:, g * 8:g * 8 + 8], AF.Exp,
                bias=sb_[:, 72 + g:73 + g], scale=1.0),
                reads=[r_pslg, rs], writes=[r_exq])
        P.emit("v", lambda e, sb_=sb_: e.reduce_sum(
            sb_[:, 80:80 + GB],
            exq[:, 0:GB * 8].rearrange("p (g c) -> p g c", c=8), axis=AX.X),
            reads=[r_exq], writes=[rs])
        P.emit("v", lambda e, sb_=sb_: e.reciprocal(sb_[:, 88:88 + GB], sb_[:, 80:80 + GB]),
               reads=[rs], writes=[rs])
        for g in range(GB):
            tt = g0 + g
            P.emit("v", lambda e, g=g, sb_=sb_: e.tensor_scalar(
                eqb[:, 0:8], ps_lg[:, g * 8:g * 8 + 8], sb_[:, 64 + g:65 + g],
                sb_[:, 88 + g:89 + g], ALU.is_equal, ALU.mult),
                reads=[r_pslg, rs], writes=[r_eqb])
            P.emit("v", lambda e: e.tensor_scalar_mul(mtok[:, 0:8], eqb[:, 0:8], 2.0),
                   reads=[r_eqb], writes=[r_mtok])
            P.emit("t", lambda e, st=st: e.transpose(ps_tb[st][0:8, 512:640], mtok[:, 0:8],
                                                     ident[:, :]),
                   reads=[r_mtok, r_const], writes=[r_pstb[st]])
            P.emit("s", lambda e, tt=tt, st=st: e.copy(m_f[0:8, tt * 128:(tt + 1) * 128],
                                                       ps_tb[st][0:8, 512:640]),
                   reads=[r_pstb[st]], writes=[r_mf])

    P.flush_block()
    st_b.close()
    pb.close()
    st_ab.close()

    CAP, NSC = c.CAP, c.NSC
    st_cA = ExitStack()
    Ec = st_cA.enter_context
    wubA = [Ec(nc.sbuf_tensor(f"wubA{i}", [128, 8 * 512], BF16)) for i in range(2)]
    wgbA = [Ec(nc.sbuf_tensor(f"wgbA{i}", [128, 8 * 512], BF16)) for i in range(2)]
    wdbA = [Ec(nc.sbuf_tensor(f"wdbA{i}", [128, 4 * 1024], BF16)) for i in range(2)]
    GW2 = min(2, NT4) * 512
    hidA = Ec(nc.sbuf_tensor("hidA", [128, 4 * GW2], BF16))
    siluA = Ec(nc.sbuf_tensor("siluA", [128, GW2], BF16))
    r_wubA = [R("wubA0"), R("wubA1")]
    r_wgbA = [R("wgbA0"), R("wgbA1")]
    r_wdbA = [R("wdbA0"), R("wdbA1")]
    r_hidA = [R(f"hidA{i}") for i in range(4)]
    r_siluA = R("siluA")

    pc = ExitStack()
    Ecp = pc.enter_context
    ps_up = Ecp(nc.psum_tensor("ps_up", [128, 1024], F32))
    ps_ga = Ecp(nc.psum_tensor("ps_ga", [128, 1024], F32))
    ps_dn = [Ecp(nc.psum_tensor(f"ps_dn{i}", [128, 1024], F32)) for i in range(2)]
    r_psup = R("psup")
    r_psga = R("psga")
    r_psdn = [R("psdn0"), R("psdn1")]

    r_gateB = R("gateB")
    b_alias = [r_oall, r_wout, r_bbc, r_xsq, r_exq, r_eqb, r_mtok] + \
        r_xhi + r_xlo + r_xlof + a_sbuf_res
    b_psum = [r_psu[0], r_psu[1], r_pslg] + r_pstb
    P.emit("t", lambda e: e.matmul(ps_up[0:1, 0:1], ones_fc[0:1, 0:1], ones_fc[0:1, 0:1],
                                   start=True, stop=True),
           writes=[r_gateB, r_psup] + b_alias + b_psum)

    experts = [-1] + list(range(c.ER))
    steps = [(e_, hg) for e_ in experts for hg in range(NHG)]

    def emit_wload(si):
        e_, hg = steps[si]
        buf = si % 2
        if e_ == -1:
            ups, gas, dns = wsup, wsgate, wsdown
            uo, do = hg * 512, hg * 512 * 1024
            ub, gb, db = wubA[buf], wgbA[buf], wdbA[buf]
            rub, rgb, rdb = r_wubA[buf], r_wgbA[buf], r_wdbA[buf]
            gate = []
        else:
            ups, gas, dns = wup, wgate, wdown
            uo, do = e_ * D * H + hg * 512, e_ * H * D + hg * 512 * 1024
            ub, gb, db = wub2[buf], wgb2[buf], wdb2[buf]
            rub, rgb, rdb = r_wub2[buf], r_wgb2[buf], r_wdb2[buf]
            gate = [r_gateC2] if si < NHG + 2 else []
        extra = (a_sbuf_res + [r_bbc, r_xhi[0], r_xlo[0], r_xlof[0], r_xsq,
                                r_exq, r_eqb, r_mtok, r_wout, r_oall]) if si < 2 else []
        P.dma("g", WUS[buf], lambda e, ups=ups, uo=uo, ub=ub: e.dma_start(
            ub[:, :].rearrange("p (dc j) -> p dc j", j=512),
            bass.AP(ups.tensor, uo, [[H, 128], [128 * H, ND], [1, 512]])),
            reads=gate, writes=[rub] + extra)
        P.dma("g", WGS[buf], lambda e, gas=gas, uo=uo, gb=gb: e.dma_start(
            gb[:, :].rearrange("p (dc j) -> p dc j", j=512),
            bass.AP(gas.tensor, uo, [[H, 128], [128 * H, ND], [1, 512]])),
            reads=gate, writes=[rgb])
        P.dma("g", WDS[buf], lambda e, dns=dns, do=do, db=db: e.dma_start(
            db[:, :].rearrange("p (hc j) -> p hc j", j=1024),
            bass.AP(dns.tensor, do, [[1024, 128], [128 * 1024, 4], [1, 1024]])),
            reads=gate, writes=[rdb])

    emit_wload(0)
    emit_wload(1)

    GS = min(2, NT4)
    groups = [tuple(range(g, min(g + GS, NT4))) for g in range(0, NT4, GS)]
    dn_ctr = [0]

    for si in range(NHG):
        hg = si
        buf = si % 2
        for grp in groups:
            gw = len(grp) * 512
            for hc in range(4):
                for dc in range(ND):
                    for ki, tt4 in enumerate(grp):
                        P.emit("t", lambda e, dc=dc, hc=hc, tt4=tt4, ki=ki, buf=buf: e.matmul(
                            ps_ga[:, ki * 512:ki * 512 + T4W],
                            wgbA[buf][:, dc * 512 + hc * 128:dc * 512 + hc * 128 + 128],
                            xfv[:, dc, tt4 * T4W:(tt4 + 1) * T4W],
                            start=(dc == 0), stop=(dc == ND - 1)),
                            reads=[r_wgbA[buf], r_xf], writes=[r_psga])
                P.emit("s", lambda e, gw=gw: e.activation(
                    siluA[:, 0:gw], ps_ga[:, 0:gw], AF.Silu),
                    reads=[r_psga, r_gateB], writes=[r_siluA])
                for dc in range(ND):
                    for ki, tt4 in enumerate(grp):
                        P.emit("t", lambda e, dc=dc, hc=hc, tt4=tt4, ki=ki, buf=buf: e.matmul(
                            ps_up[:, ki * 512:ki * 512 + T4W],
                            wubA[buf][:, dc * 512 + hc * 128:dc * 512 + hc * 128 + 128],
                            xfv[:, dc, tt4 * T4W:(tt4 + 1) * T4W],
                            start=(dc == 0), stop=(dc == ND - 1)),
                            reads=[r_wubA[buf], r_xf], writes=[r_psup])
                P.emit("v", lambda e, hc=hc, gw=gw: e.tensor_mul(
                    hidA[:, hc * GW2:hc * GW2 + gw], ps_up[:, 0:gw], siluA[:, 0:gw]),
                    reads=[r_psup, r_siluA, r_gateB], writes=[r_hidA[hc]])
            for ki, tt4 in enumerate(grp):
                for it in range(T4W // 128):
                    tt = tt4 * (T4W // 128) + it
                    par = dn_ctr[0] % 2
                    dn_ctr[0] += 1
                    for hc in range(4):
                        for half in range(2):
                            P.emit("t", lambda e, hc=hc, half=half, it=it, ki=ki, par=par,
                                   buf=buf: e.matmul(
                                ps_dn[par][:, half * 512:(half + 1) * 512],
                                hidA[:, hc * GW2 + ki * 512 + it * 128:
                                      hc * GW2 + ki * 512 + (it + 1) * 128],
                                wdbA[buf][:, hc * 1024 + half * 512:hc * 1024 + (half + 1) * 512],
                                start=(hc == 0), stop=(hc == 3)),
                                reads=[r_hidA[hc], r_wdbA[buf]], writes=[r_psdn[par]])
                    P.emit("v", lambda e, tt=tt, par=par: e.tensor_add(
                        x_res[:, tt * 1024:(tt + 1) * 1024],
                        x_res[:, tt * 1024:(tt + 1) * 1024],
                        ps_dn[par][:, :]), reads=[r_psdn[par], r_xres[tt]],
                        writes=[r_xres[tt]])
        if si + 2 < NHG:
            emit_wload(si + 2)

    st_cA.close()
    st_xf.close()
    st_c2 = ExitStack()
    Ec2 = st_c2.enter_context
    wub2 = [Ec2(nc.sbuf_tensor(f"wub2_{i}", [128, 8 * 512], BF16)) for i in range(2)]
    wgb2 = [Ec2(nc.sbuf_tensor(f"wgb2_{i}", [128, 8 * 512], BF16)) for i in range(2)]
    wdb2 = [Ec2(nc.sbuf_tensor(f"wdb2_{i}", [128, 4 * 1024], BF16)) for i in range(2)]
    stripe = [Ec2(nc.sbuf_tensor(f"stripe{i}", [128, TOK], BF16)) for i in range(2)]
    Pv = Ec2(nc.sbuf_tensor("Pv", [128, max(NT * CAP, 4 * D)], BF16))
    PTm = Ec2(nc.sbuf_tensor("PTm", [128, NSC * TOK], BF16))
    MBc = Ec2(nc.sbuf_tensor("MBc", [128, TOK], BF16))
    keyb = Ec2(nc.sbuf_tensor("keyb", [128, TOK], F32))
    xg = Ec2(nc.sbuf_tensor("xg", [128, ND * CAP], BF16))
    dg_bf = Ec2(nc.sbuf_tensor("dg_bf", [128, NSC * 1024], BF16))
    hid3 = Ec2(nc.sbuf_tensor("hid3", [128, 4 * CAP], BF16))
    silu_r = Ec2(nc.sbuf_tensor("silu_r", [128, CAP], BF16))
    mrow = Ec2(nc.sbuf_tensor("mrow", [1, TOK], BF16))
    maskr = Ec2(nc.sbuf_tensor("maskr", [1, TOK], F32))
    keyr = Ec2(nc.sbuf_tensor("keyr", [1, TOK], F32))
    kc_sb = Ec2(nc.sbuf_tensor("kc_sb", [128, NT], F32))
    iota1 = Ec2(nc.sbuf_tensor("iota1", [128, CAP], F32))
    iotaS = Ec2(nc.sbuf_tensor("iotaS", [128, NSC], F32))
    r_wub2 = [R("wub2_0"), R("wub2_1")]
    r_wgb2 = [R("wgb2_0"), R("wgb2_1")]
    r_wdb2 = [R("wdb2_0"), R("wdb2_1")]
    r_stripe = [R("stripe0"), R("stripe1")]
    r_Pv, r_PTm, r_MB, r_keyb = R("Pv"), R("PTm"), R("MB"), R("keyb")
    r_xg, r_dgbf = R("xg"), R("dgbf")
    r_hid3 = [R(f"hid3_{i}") for i in range(4)]
    r_silur, r_mrow, r_maskr, r_keyr = R("silur"), R("mrow"), R("maskr"), R("keyr")
    r_kc, r_iotac = R("kc"), R("iotac")

    r_gateC2 = R("gateC2")
    P.emit("t", lambda e: e.matmul(ps_up[0:1, 0:1], ones_fc[0:1, 0:1], ones_fc[0:1, 0:1],
                                   start=True, stop=True),
           reads=[r_const],
           writes=[r_gateC2, r_psup, r_xf, r_siluA] + r_wubA + r_wgbA + r_wdbA + r_hidA)
    P.emit("g", lambda e: e.iota(iota1[:, :], [[1, CAP]], base=1, channel_multiplier=0,
                                 allow_small_or_imprecise_dtypes=True),
           reads=[r_gateC2], writes=[r_iotac])
    P.emit("g", lambda e: e.iota(iotaS[:, :], [[128, NSC]], base=1, channel_multiplier=1,
                                 allow_small_or_imprecise_dtypes=True),
           reads=[r_gateC2], writes=[r_iotac])
    emit_wload(NHG)
    emit_wload(NHG + 1)

    all8 = [(ps_up, 0, r_psup), (ps_up, 512, r_psup),
            (ps_ga, 0, r_psga), (ps_ga, 512, r_psga),
            (ps_dn[0], 0, r_psdn[0]), (ps_dn[0], 512, r_psdn[0]),
            (ps_dn[1], 0, r_psdn[1]), (ps_dn[1], 512, r_psdn[1])]
    dn4 = all8[4:]
    ug4 = all8[:4]
    dnmap = [[all8[4], all8[5]], [all8[6], all8[7]], [all8[1], all8[3]]]

    def emit_stripe(dc):
        b = dc % 2
        P.dma("y", STR[b], lambda e, dc=dc, b=b: e.dma_start(
            stripe[b][:, :].rearrange("p (tc q) -> p tc q", q=128),
            bass.AP(xscr.tensor, dc * 128, [[1024, 128], [128 * 1024, NT], [1, 128]])),
            reads=[r_xscr, r_gateC2], writes=[r_stripe[b]])

    def emit_pro_mrow(e_):
        nch = TOK // 512
        for j in range(nch):
            pst, off, rp = ug4[j % 4]
            P.emit("t", lambda e, j=j, pst=pst, off=off, e_=e_: e.matmul(
                pst[0:1, off:off + 512], sel[0:8, e_ * 128:e_ * 128 + 1],
                m_f[0:8, j * 512:(j + 1) * 512], start=True, stop=True),
                reads=[r_const, r_mf, r_gateC2], writes=[rp])
            P.emit("s", lambda e, j=j, pst=pst, off=off: e.copy(
                mrow[0:1, j * 512:(j + 1) * 512], pst[0:1, off:off + 512]),
                reads=[rp, r_gateC2], writes=[r_mrow])
        P.emit("v", lambda e: e.tensor_single_scalar(
            maskr[0:1, :], mrow[0:1, :], 0.0, ALU.is_gt),
            reads=[r_mrow, r_gateC2], writes=[r_maskr])
        P.emit("v", lambda e: e.tensor_tensor_scan(
            keyr[0:1, :], maskr[0:1, :], maskr[0:1, :], 0.0, ALU.add, ALU.max),
            reads=[r_maskr, r_gateC2], writes=[r_keyr])
        P.emit("v", lambda e: e.tensor_mul(keyr[0:1, :], keyr[0:1, :], maskr[0:1, :]),
               reads=[r_keyr, r_maskr], writes=[r_keyr])

    def emit_pro_rest(e_):
        nch = TOK // 512
        for j in range(nch):
            pst, off, rp = ug4[j % 4]
            P.emit("t", lambda e, j=j, pst=pst, off=off: e.matmul(
                pst[:, off:off + 512], onesf32[0:1, :], keyr[0:1, j * 512:(j + 1) * 512],
                start=True, stop=True), reads=[r_const, r_keyr], writes=[rp])
            P.emit("v", lambda e, j=j, pst=pst, off=off: e.tensor_copy(
                keyb[:, j * 512:(j + 1) * 512], pst[:, off:off + 512]),
                reads=[rp, r_gateC2], writes=[r_keyb])
        for j in range(nch):
            pst, off, rp = ug4[j % 4]
            P.emit("t", lambda e, j=j, pst=pst, off=off: e.matmul(
                pst[:, off:off + 512], ones_bf[0:1, :], mrow[0:1, j * 512:(j + 1) * 512],
                start=True, stop=True), reads=[r_const, r_mrow], writes=[rp])
            P.emit("s", lambda e, j=j, pst=pst, off=off: e.copy(
                MBc[:, j * 512:(j + 1) * 512], pst[:, off:off + 512]),
                reads=[rp, r_gateC2], writes=[r_MB])
        for tc in range(NT):
            pst, off, rp = ug4[tc % 4]
            P.emit("t", lambda e, tc=tc, pst=pst, off=off: e.matmul(
                pst[:, off:off + 1], keyr[0:1, tc * 128:(tc + 1) * 128],
                onesf32[0:1, 0:1], start=True, stop=True),
                reads=[r_keyr, r_const], writes=[rp])
            P.emit("v", lambda e, tc=tc, pst=pst, off=off: e.tensor_copy(
                kc_sb[:, tc:tc + 1], pst[:, off:off + 1]),
                reads=[rp, r_gateC2], writes=[r_kc])
            P.emit("v", lambda e, tc=tc: e.tensor_single_scalar(
                Pv[:, tc * CAP:(tc + 1) * CAP], iota1[:, :], kc_sb[:, tc:tc + 1],
                ALU.is_equal), reads=[r_kc, r_iotac, r_gateC2], writes=[r_Pv])
        for sc in range(NSC):
            P.emit("v", lambda e, sc=sc: e.scalar_tensor_tensor(
                PTm[:, sc * TOK:(sc + 1) * TOK], keyb[:, :], iotaS[:, sc:sc + 1],
                MBc[:, :], ALU.is_equal, ALU.mult),
                reads=[r_keyb, r_MB, r_iotac, r_gateC2], writes=[r_PTm])

    def emit_gather(e_):
        emit_stripe(0)
        if ND > 1:
            emit_stripe(1)
        for dc in range(ND):
            pst, off, rp = all8[dc]
            b = dc % 2
            for tc in range(NT):
                P.emit("t", lambda e, dc=dc, tc=tc, pst=pst, off=off, b=b: e.matmul(
                    pst[:, off:off + CAP], stripe[b][:, tc * 128:(tc + 1) * 128],
                    Pv[:, tc * CAP:(tc + 1) * CAP], start=(tc == 0), stop=(tc == NT - 1)),
                    reads=[r_stripe[b], r_Pv], writes=[rp])
            P.emit("s", lambda e, dc=dc, pst=pst, off=off: e.copy(
                xg[:, dc * CAP:(dc + 1) * CAP], pst[:, off:off + CAP]),
                reads=[rp, r_gateC2], writes=[r_xg])
            if dc + 2 < ND:
                emit_stripe(dc + 2)

    def emit_rcompute(si):
        e_, hg = steps[si]
        buf = si % 2
        for hc in range(4):
            for dc in range(ND):
                P.emit("t", lambda e, hc=hc, dc=dc, buf=buf: e.matmul(
                    ps_ga[:, 0:CAP],
                    wgb2[buf][:, dc * 512 + hc * 128:dc * 512 + hc * 128 + 128],
                    xg[:, dc * CAP:(dc + 1) * CAP],
                    start=(dc == 0), stop=(dc == ND - 1)),
                    reads=[r_wgb2[buf], r_xg], writes=[r_psga])
            P.emit("s", lambda e: e.activation(silu_r[:, 0:CAP], ps_ga[:, 0:CAP], AF.Silu),
                   reads=[r_psga, r_gateC2], writes=[r_silur])
            for dc in range(ND):
                P.emit("t", lambda e, hc=hc, dc=dc, buf=buf: e.matmul(
                    ps_up[:, 0:CAP],
                    wub2[buf][:, dc * 512 + hc * 128:dc * 512 + hc * 128 + 128],
                    xg[:, dc * CAP:(dc + 1) * CAP],
                    start=(dc == 0), stop=(dc == ND - 1)),
                    reads=[r_wub2[buf], r_xg], writes=[r_psup])
            P.emit("v", lambda e, hc=hc: e.tensor_mul(
                hid3[:, hc * CAP:(hc + 1) * CAP], ps_up[:, 0:CAP], silu_r[:, 0:CAP]),
                reads=[r_psup, r_silur, r_gateC2], writes=[r_hid3[hc]])
            for sc in range(NSC):
                for half in range(2):
                    pst, off, rp = dnmap[sc][half]
                    P.emit("t", lambda e, hc=hc, sc=sc, half=half, buf=buf, pst=pst,
                           off=off, hg=hg: e.matmul(
                        pst[:, off:off + 512],
                        hid3[:, hc * CAP + sc * 128:hc * CAP + (sc + 1) * 128],
                        wdb2[buf][:, hc * 1024 + half * 512:hc * 1024 + (half + 1) * 512],
                        start=(hg == 0 and hc == 0), stop=(hg == NHG - 1 and hc == 3)),
                        reads=[r_hid3[hc], r_wdb2[buf]], writes=[rp])

    def emit_dgevac(e_):
        for sc in range(NSC):
            for half in range(2):
                pst, off, rp = dnmap[sc][half]
                P.emit("s", lambda e, sc=sc, half=half, pst=pst, off=off: e.copy(
                    dg_bf[:, sc * 1024 + half * 512:sc * 1024 + (half + 1) * 512],
                    pst[:, off:off + 512]), reads=[rp, r_gateC2], writes=[r_dgbf])

    def emit_scatter(e_):
        for tc in range(NT):
            par = tc % 2
            for sc in range(NSC):
                for half in range(2):
                    P.emit("t", lambda e, tc=tc, sc=sc, half=half, par=par: e.matmul(
                        ps_dn[par][:, half * 512:(half + 1) * 512],
                        PTm[:, sc * TOK + tc * 128:sc * TOK + (tc + 1) * 128],
                        dg_bf[:, sc * 1024 + half * 512:sc * 1024 + (half + 1) * 512],
                        start=(sc == 0), stop=(sc == NSC - 1)),
                        reads=[r_PTm, r_dgbf], writes=[r_psdn[par]])
            P.emit("v", lambda e, tc=tc, par=par: e.tensor_add(
                x_res[:, tc * 1024:(tc + 1) * 1024], x_res[:, tc * 1024:(tc + 1) * 1024],
                ps_dn[par][:, :]), reads=[r_psdn[par], r_xres[tc]], writes=[r_xres[tc]])

    emit_pro_mrow(0)
    emit_pro_rest(0)
    emit_gather(0)
    for e_ in range(c.ER):
        for hg in range(NHG):
            si = NHG + e_ * NHG + hg
            emit_rcompute(si)
            if si + 2 < len(steps):
                emit_wload(si + 2)
        emit_dgevac(e_)
        if e_ + 1 < c.ER:
            emit_pro_mrow(e_ + 1)
        emit_scatter(e_)
        if e_ + 1 < c.ER:
            emit_pro_rest(e_ + 1)
            emit_gather(e_ + 1)

    GB2 = min(4, NT)
    for g0 in range(0, NT, GB2):
        gi = g0 // GB2
        st = gi % 2
        rs = r_stat[st]
        sb_ = stat[st]
        for g in range(GB2):
            tt = g0 + g
            P.emit("v", lambda e, tt=tt, g=g, sb_=sb_: e.reduce_sum(
                sb_[:, g:g + 1], x_res[:, tt * 1024:(tt + 1) * 1024], axis=AX.X),
                reads=[r_xres[tt]], writes=[rs])
            P.emit("v", lambda e, tt=tt, g=g: e.tensor_mul(
                Pv[:, g * D:(g + 1) * D], x_res[:, tt * 1024:(tt + 1) * 1024],
                x_res[:, tt * 1024:(tt + 1) * 1024]),
                reads=[r_xres[tt], r_gateC2], writes=[r_Pv])
            P.emit("v", lambda e, g=g, sb_=sb_: e.reduce_sum(
                sb_[:, 8 + g:9 + g], Pv[:, g * D:(g + 1) * D], axis=AX.X),
                reads=[r_Pv], writes=[rs])
        P.emit("v", lambda e, sb_=sb_: e.tensor_scalar_mul(sb_[:, 16:16 + GB2], sb_[:, 0:GB2],
                                                           1.0 / D), reads=[rs], writes=[rs])
        P.emit("v", lambda e, sb_=sb_: e.tensor_scalar_mul(sb_[:, 24:24 + GB2],
                                                           sb_[:, 8:8 + GB2],
                                                           1.0 / D), reads=[rs], writes=[rs])
        P.emit("v", lambda e, sb_=sb_: e.tensor_mul(sb_[:, 32:32 + GB2], sb_[:, 16:16 + GB2],
                                                    sb_[:, 16:16 + GB2]), reads=[rs], writes=[rs])
        P.emit("v", lambda e, sb_=sb_: e.tensor_sub(sb_[:, 40:40 + GB2], sb_[:, 24:24 + GB2],
                                                    sb_[:, 32:32 + GB2]), reads=[rs], writes=[rs])
        P.emit("v", lambda e, sb_=sb_: e.tensor_scalar_add(sb_[:, 40:40 + GB2],
                                                           sb_[:, 40:40 + GB2], 1e-5),
               reads=[rs], writes=[rs])
        P.emit("s", lambda e, sb_=sb_: e.activation(sb_[:, 48:48 + GB2], sb_[:, 40:40 + GB2],
                                                    AF.Ln, bias=0.0, scale=1.0),
               reads=[rs], writes=[rs])
        P.emit("s", lambda e, sb_=sb_: e.activation(sb_[:, 56:56 + GB2], sb_[:, 48:48 + GB2],
                                                    AF.Exp, bias=0.0, scale=-0.5),
               reads=[rs], writes=[rs])
        for g in range(GB2):
            tt = g0 + g
            P.emit("v", lambda e, tt=tt, g=g, sb_=sb_: e.tensor_scalar(
                x_res[:, tt * 1024:(tt + 1) * 1024], x_res[:, tt * 1024:(tt + 1) * 1024],
                sb_[:, 16 + g:17 + g], sb_[:, 56 + g:57 + g], ALU.subtract, ALU.mult),
                reads=[rs, r_xres[tt]], writes=[r_xres[tt]])
            P.dma("y", STO, lambda e, tt=tt: e.dma_start(
                out[tt * 128:(tt + 1) * 128, :], x_res[:, tt * 1024:(tt + 1) * 1024]),
                reads=[r_xres[tt]])

    ops = P.finalize(3, [("y", STO)] + ([("g", DBG)] if c.debug else []))

    for ebd in ops:
        with nc.Block() as blk:
            blk.tensor(lambda e, d=ebd: [f(e) for f in d["t"]])
            blk.vector(lambda e, d=ebd: [f(e) for f in d["v"]])
            blk.scalar(lambda e, d=ebd: [f(e) for f in d["s"]])
            blk.gpsimd(lambda e, d=ebd: [f(e) for f in d["g"]])
            blk.sync(lambda e, d=ebd: [f(e) for f in d["y"]])

    pc.close()
    st_c2.close()
    st_bend.close()
    st0.close()
    return nc


def kernel(**inputs):
    c = CFG()
    x = np.asarray(inputs["L0_attn_data"])
    wm = {
        "wqkv": np.asarray(inputs["L0_attn_W_qkv"]),
        "bqkv": np.asarray(inputs["L0_attn_b_qkv"]),
        "wout": np.asarray(inputs["L0_attn_W_out"]),
        "bout": np.asarray(inputs["L0_attn_b_out"]),
        "wrt": np.asarray(inputs["L0_ffn_router_w"]),
        "wup": np.asarray(inputs["L0_ffn_up_w"]),
        "wgate": np.asarray(inputs["L0_ffn_gate_w"]),
        "wdown": np.asarray(inputs["L0_ffn_down_w"]),
        "wsup": np.asarray(inputs["L0_ffn_shared_up_w"]),
        "wsgate": np.asarray(inputs["L0_ffn_shared_gate_w"]),
        "wsdown": np.asarray(inputs["L0_ffn_shared_down_w"]),
    }
    nc = build_nc(c)
    in_maps = []
    for core in range(8):
        b, h = divmod(core, 2)
        own = x[b, h * 2048:(h + 1) * 2048]
        oth = x[b, (1 - h) * 2048:(2 - h) * 2048]
        m = {"xb": np.ascontiguousarray(np.concatenate([own, oth], axis=0))}
        m.update(wm)
        in_maps.append(m)
    res = run_bass_kernel_spmd(nc, in_maps, core_ids=list(range(8)))
    outp = np.empty((4, 4096, 1024), np.float32)
    for core in range(8):
        b, h = divmod(core, 2)
        outp[b, h * 2048:(h + 1) * 2048] = res.results[core]["out"]
    return outp


# version 24
# speedup vs baseline: 1.0240x; 1.0240x over previous
import os
import numpy as np
import concourse.bass as bass
import concourse.mybir as mybir
from concourse.bass_utils import run_bass_kernel_spmd
from contextlib import ExitStack

F32 = mybir.dt.float32
BF16 = mybir.dt.bfloat16
AF = mybir.ActivationFunctionType
ALU = mybir.AluOpType
AX = mybir.AxisListType


class Res:
    __slots__ = ("writer", "writers", "readers", "name")

    def __init__(self, name=""):
        self.writer = None
        self.writers = []
        self.readers = []
        self.name = name


class Ev:
    __slots__ = ("fn", "eng", "idx", "need_inc", "snap", "blk", "sem", "kind", "seq")


class _Eng:
    def __init__(self, name, sem):
        self.name = name
        self.sem = sem
        self.events = []
        self.know = {}
        self.dknow = {}
        self.waits = {}


class DSem:
    def __init__(self, h, mode):
        self.h = h
        self.mode = mode
        self.count = 0
        self.events = []


class Prog:
    def __init__(self, nc):
        self.nc = nc
        self.blk = 0
        self.engs = {}
        self.csems = {}

    def add_engine(self, name, sem_handle):
        s = {"h": sem_handle, "events": []}
        self.csems[name] = s
        self.engs[name] = _Eng(name, s)

    def flush_block(self):
        self.blk += 1

    def _deps(self, eng, ev, reads, writes, wdisj=()):
        deps = []
        for r in reads:
            if r.writer is not None:
                deps.append(r.writer)
            deps.extend(r.writers)
        for w in writes:
            if w.writer is not None:
                deps.append(w.writer)
            deps.extend(w.writers)
            deps.extend(w.readers)
        for w in wdisj:
            if w.writer is not None:
                deps.append(w.writer)
            deps.extend(w.readers)
        seen = set()
        for d in deps:
            if id(d) in seen or d is ev:
                continue
            seen.add(id(d))
            if d.kind == "c":
                if d.eng is eng and eng.name == "t" and d.eng.name == "t":
                    continue
                if eng.know.get(d.eng.name, -1) >= d.idx:
                    continue
                d.need_inc = True
                eng.waits.setdefault(ev.idx, []).append(("c", d))
                eng.know[d.eng.name] = max(eng.know.get(d.eng.name, -1), d.idx)
                for k, v in d.snap.items():
                    eng.know[k] = max(eng.know.get(k, -1), v)
            else:
                key = "d%d" % id(d.sem)
                if d.sem.mode == "slot":
                    if eng.know.get(key, -1) >= d.seq:
                        continue
                    eng.know[key] = d.seq
                else:
                    if eng.know.get(key, -1) >= 0:
                        continue
                    eng.know[key] = 0
                eng.waits.setdefault(ev.idx, []).append(("d", d))
                for k, v in d.snap.items():
                    eng.know[k] = max(eng.know.get(k, -1), v)

    @staticmethod
    def _prune(lst, ev):
        nr = []
        for r0 in lst:
            if r0.eng is ev.eng and r0.idx <= ev.idx and r0.kind == "c" and ev.kind == "c":
                continue
            if r0.kind == "c" and ev.snap.get(r0.eng.name, -1) >= r0.idx:
                continue
            nr.append(r0)
        return nr

    def _register(self, ev, reads, writes, wdisj=()):
        for r in reads:
            r.readers = self._prune(r.readers, ev)
            r.readers.append(ev)
        for w in writes:
            w.writer = ev
            w.writers = []
            w.readers = []
        for w in wdisj:
            w.writers = self._prune(w.writers, ev)
            w.writers.append(ev)

    def emit(self, ename, fn, reads=(), writes=(), wdisj=()):
        eng = self.engs[ename]
        ev = Ev()
        ev.fn = fn
        ev.eng = eng
        ev.idx = len(eng.events)
        ev.need_inc = False
        ev.blk = self.blk
        ev.kind = "c"
        ev.sem = None
        ev.seq = 0
        ev.snap = {}
        self._deps(eng, ev, reads, writes, wdisj)
        ev.snap = dict(eng.know)
        ev.snap[eng.name] = ev.idx
        self._register(ev, reads, writes, wdisj)
        eng.events.append(ev)
        return ev

    def dma(self, ename, dsem, fn, reads=(), writes=(), wdisj=()):
        eng = self.engs[ename]
        ev = Ev()
        ev.fn = fn
        ev.eng = eng
        ev.idx = len(eng.events)
        ev.need_inc = True
        ev.blk = self.blk
        ev.kind = "d"
        ev.sem = dsem
        dsem.count += 1
        ev.seq = dsem.count
        dsem.events.append(ev)
        ev.snap = {}
        self._deps(eng, ev, reads, writes, wdisj)
        ev.snap = dict(eng.know)
        self._register(ev, reads, writes, wdisj)
        eng.events.append(ev)
        return ev

    def finalize(self, nblocks, final_waits):
        for eng in self.engs.values():
            cnt = 0
            for ev in eng.events:
                if ev.kind == "c" and ev.need_inc:
                    cnt += 1
                    ev.seq = cnt
        out = [{k: [] for k in self.engs} for _ in range(nblocks)]
        for name, eng in self.engs.items():
            hw = {}
            for ev in eng.events:
                ops = out[ev.blk][name]
                best = {}
                for kind, d in eng.waits.get(ev.idx, ()):
                    if kind == "c":
                        sem = d.eng.sem["h"]
                        val = d.seq
                    else:
                        sem = d.sem.h
                        val = 16 * (d.sem.count if d.sem.mode == "sum" else d.seq)
                    key = id(sem)
                    if val > best.get(key, (0, None))[0]:
                        best[key] = (val, sem)
                for key, (val, sem) in best.items():
                    if hw.get(key, 0) >= val:
                        continue
                    hw[key] = val
                    ops.append(lambda e, s=sem, v=val: e.wait_ge(s, v))
                if ev.kind == "d":
                    ops.append(lambda e, f=ev.fn, s=ev.sem.h: f(e).then_inc(s, 16))
                elif ev.need_inc:
                    def _mk(f, s, ev=ev):
                        def go(e):
                            r = f(e)
                            if r is None:
                                raise RuntimeError("event fn returned None: %s blk=%d idx=%d"
                                                   % (ev.eng.name, ev.blk, ev.idx))
                            return r.then_inc(s, 1)
                        return go
                    ops.append(_mk(ev.fn, eng.sem["h"]))
                else:
                    ops.append(ev.fn)
        for ename, dsem in final_waits:
            out[nblocks - 1][ename].append(
                lambda e, s=dsem.h, v=16 * dsem.count: e.wait_ge(s, v))
        return out


class CFG:
    def __init__(self, TOK=2048, SEQ=4096, H=4096, debug=False):
        self.TOK, self.SEQ, self.H = TOK, SEQ, H
        self.debug = debug
        self.D = 1024
        self.NP = 8
        self.ER = 7
        self.ND = self.D // 128
        self.NT = TOK // 128
        self.NTS = SEQ // 128
        self.QSW = min(512, TOK)
        self.NQS = TOK // self.QSW
        self.NKC = SEQ // 128
        self.NKV = SEQ // 512
        self.NHG = H // 512
        self.T4W = min(512, TOK)
        self.NT4 = TOK // self.T4W
        self.CAP = 128 * ((3 * TOK + 2047) // 2048)
        self.NSC = self.CAP // 128
        assert self.NSC <= 3


def build_nc(c: CFG):
    nc = bass.Bass("TRN2", target_bir_lowering=False, detect_race_conditions=True)
    TOK, SEQ, D, H = c.TOK, c.SEQ, c.D, c.H
    ND, NT, QSW, NQS, NKC, NKV, NHG, NT4, T4W = (
        c.ND, c.NT, c.QSW, c.NQS, c.NKC, c.NKV, c.NHG, c.NT4, c.T4W)
    TOKO = SEQ - TOK
    NTO = TOKO // 128

    xb = nc.dram_tensor("xb", [SEQ, D], F32, kind="ExternalInput").ap()
    wqkv = nc.dram_tensor("wqkv", [D, 3 * D], F32, kind="ExternalInput").ap()
    bqkv = nc.dram_tensor("bqkv", [3 * D], F32, kind="ExternalInput").ap()
    wout = nc.dram_tensor("wout", [D, D], F32, kind="ExternalInput").ap()
    bout = nc.dram_tensor("bout", [D], F32, kind="ExternalInput").ap()
    wrt = nc.dram_tensor("wrt", [D, 8], F32, kind="ExternalInput").ap()
    wup = nc.dram_tensor("wup", [c.ER, D, H], F32, kind="ExternalInput").ap()
    wgate = nc.dram_tensor("wgate", [c.ER, D, H], F32, kind="ExternalInput").ap()
    wdown = nc.dram_tensor("wdown", [c.ER, H, D], F32, kind="ExternalInput").ap()
    wsup = nc.dram_tensor("wsup", [D, H], F32, kind="ExternalInput").ap()
    wsgate = nc.dram_tensor("wsgate", [D, H], F32, kind="ExternalInput").ap()
    wsdown = nc.dram_tensor("wsdown", [H, D], F32, kind="ExternalInput").ap()
    out = nc.dram_tensor("out", [TOK, D], F32, kind="ExternalOutput").ap()
    xscr = nc.dram_tensor("xscr", [TOK, D], BF16, kind="Internal").ap()
    if c.debug:
        dbg_xf = nc.dram_tensor("dbg_xf", [128, ND * TOK], F32, kind="ExternalOutput").ap()
        dbg_xof = nc.dram_tensor("dbg_xof", [128, ND * TOKO], F32, kind="ExternalOutput").ap()
        dbg_nrm = nc.dram_tensor("dbg_nrm", [128, 3 * 512], F32, kind="ExternalOutput").ap()

    st0 = ExitStack()
    E = st0.enter_context

    P = Prog(nc)
    for en in "tvsg":
        P.add_engine(en, E(nc.semaphore("sem_" + en)))
    P.add_engine("y", E(nc.semaphore("sem_y")))
    XST = [DSem(E(nc.semaphore(f"xst{i}")), "slot") for i in range(4)]
    TPX = [DSem(E(nc.semaphore(f"tpx{i}")), "slot") for i in range(4)]
    TPV = [DSem(E(nc.semaphore(f"tpv{i}")), "slot") for i in range(2)]
    TPS = DSem(E(nc.semaphore("tps")), "slot")
    WQS = [[DSem(E(nc.semaphore(f"wqs{i}_{r}")), "slot") for r in range(3)] for i in range(2)]
    WUS = [DSem(E(nc.semaphore(f"wus{i}")), "slot") for i in range(2)]
    WGS = [DSem(E(nc.semaphore(f"wgs{i}")), "slot") for i in range(2)]
    WDS = [DSem(E(nc.semaphore(f"wds{i}")), "slot") for i in range(2)]
    MIG = DSem(E(nc.semaphore("mig")), "sum")
    MIY = DSem(E(nc.semaphore("miy")), "sum")
    XRS = DSem(E(nc.semaphore("xrs")), "slot")
    WOS = DSem(E(nc.semaphore("wos")), "slot")
    XSC = DSem(E(nc.semaphore("xsc")), "slot")
    STR = [DSem(E(nc.semaphore(f"str{i}")), "slot") for i in range(2)]
    STO = DSem(E(nc.semaphore("sto")), "sum")
    DBG = DSem(E(nc.semaphore("dbgs")), "sum")

    ident = E(nc.sbuf_tensor("ident", [128, 128], BF16))
    ones_bf = E(nc.sbuf_tensor("ones_bf", [128, 128], BF16))
    ones_fc = E(nc.sbuf_tensor("ones_fc", [128, 1], BF16))
    sel = E(nc.sbuf_tensor("sel", [8, 8 * 128], BF16))
    bq_sb = E(nc.sbuf_tensor("bq_sb", [128, 24], F32))
    bout_row = E(nc.sbuf_tensor("bout_row", [1, D], BF16))
    wrt_hi = E(nc.sbuf_tensor("wrt_hi", [128, ND * 8], BF16))
    wrt_lo = E(nc.sbuf_tensor("wrt_lo", [128, ND * 8], BF16))
    wrt32 = E(nc.sbuf_tensor("wrt32", [128, ND * 8], F32))
    m_f = E(nc.sbuf_tensor("m_f", [8, TOK], BF16))
    stat = [E(nc.sbuf_tensor(f"stat{i}", [128, 96], F32)) for i in range(2)]
    negc = E(nc.sbuf_tensor("negc", [128, 2], F32))
    csc = E(nc.sbuf_tensor("csc", [1, 16], F32))
    cbf = E(nc.sbuf_tensor("cbf", [1, 4], BF16))
    onesf32 = E(nc.sbuf_tensor("onesf32", [1, 128], F32))

    st_xf = ExitStack()
    x_f = st_xf.enter_context(nc.sbuf_tensor("x_f", [128, ND * TOK], BF16))

    R = lambda n: Res(n)
    r_xf = R("x_f")
    r_const = R("const")
    r_bq = R("bq")
    r_boutrow = R("boutrow")
    r_wrthi, r_wrtlo, r_wrt32 = R("wrthi"), R("wrtlo"), R("wrt32")
    r_mf = R("m_f")
    r_stat = [R("stat0"), R("stat1")]
    r_wub = [R("wub0"), R("wub1")]
    r_wgb = [R("wgb0"), R("wgb1")]
    r_wdb = [R("wdb0"), R("wdb1")]
    r_negc = [R("negc0"), R("negc1")]
    r_csc = R("csc")
    r_cbf = R("cbf")

    st_ab = ExitStack()
    O_all = st_ab.enter_context(nc.sbuf_tensor("O_all", [128, c.NP * TOK], BF16))
    r_oall, r_wout = R("O_all"), R("wout")

    VTW = 161
    st_a = ExitStack()
    Ea = st_a.enter_context
    xo_f = Ea(nc.sbuf_tensor("xo_f", [128, max(ND * TOKO, 2)], BF16))
    iota_s = Ea(nc.sbuf_tensor("iota_s", [128, 1280], F32))
    wq_sb = [Ea(nc.sbuf_tensor(f"wq_sb{i}", [128, ND * 384], BF16)) for i in range(2)]
    xstage = [Ea(nc.sbuf_tensor(f"xstage{i}", [128, D], BF16)) for i in range(4)]
    q_sb = [Ea(nc.sbuf_tensor(f"q_sb{i}", [128, TOK], BF16)) for i in range(2)]
    k_sb = [Ea(nc.sbuf_tensor(f"k_sb{i}", [128, SEQ], BF16)) for i in range(2)]
    vt_sb = [Ea(nc.sbuf_tensor(f"vt_sb{i}", [128, NKC * VTW], BF16)) for i in range(2)]
    p_sb = [Ea(nc.sbuf_tensor(f"p_sb{i}", [128, 2 * QSW], BF16)) for i in range(4)]
    sqbuf = Ea(nc.sbuf_tensor("sqbuf", [128, 512], BF16))
    sqw = Ea(nc.sbuf_tensor("sqw", [128, 2048], BF16))
    mstat = Ea(nc.sbuf_tensor("mstat", [128, 24], F32))
    rden = Ea(nc.sbuf_tensor("rden", [128, QSW], BF16))
    bc_sb = [Ea(nc.sbuf_tensor(f"bc_sb{i}", [128, QSW], BF16)) for i in range(2)]
    r_xof = R("xo_f")
    r_wq3 = [[R(f"wq{b}_{r}") for r in range(3)] for b in range(2)]
    r_xst = [R(f"xst{i}") for i in range(4)]
    r_q = [R("q0"), R("q1")]
    r_k = [R("k0"), R("k1")]
    r_vt = [R("vt0"), R("vt1")]
    r_p = [R(f"p{i}") for i in range(4)]
    r_sq, r_mst, r_rden = R("sq"), R("mst"), R("rden")
    r_sqw = R("sqw")
    r_bc = [R("bc0"), R("bc1")]

    pa0 = ExitStack()
    ps_tp = [pa0.enter_context(nc.psum_tensor(f"ps_tp{i}", [128, 1024], BF16))
             for i in range(2)]
    r_pstp = [R("pstp0"), R("pstp1")]

    xfv = x_f[:, :].rearrange("p (dc t) -> p dc t", t=TOK)
    xov = xo_f[:, :].rearrange("p (dc t) -> p dc t", t=TOKO) if TOKO else None

    P.emit("g", lambda e: e.iota(iota_s[:, 0:128], [[1, 128]], channel_multiplier=-1,
                                 allow_small_or_imprecise_dtypes=True), writes=[r_const])
    P.emit("g", lambda e: e.iota(
        iota_s[:, 128:128 + 8 * 128].rearrange("p (a b) -> p a b", b=128),
        [[1, 8], [0, 128]], channel_multiplier=-1,
        allow_small_or_imprecise_dtypes=True), reads=[r_const], writes=[r_const])
    P.emit("v", lambda e: e.tensor_single_scalar(ident[:, :], iota_s[:, 0:128], 0.0,
                                                 ALU.is_equal), reads=[r_const], writes=[r_const])
    P.emit("v", lambda e: e.tensor_single_scalar(sel[0:8, :], iota_s[0:8, 128:128 + 8 * 128],
                                                 0.0, ALU.is_equal), reads=[r_const], writes=[r_const])
    P.emit("v", lambda e: e.memset(ones_bf[:, :], 1.0), reads=[r_const], writes=[r_const])
    P.emit("v", lambda e: e.memset(ones_fc[:, :], 1.0), reads=[r_const], writes=[r_const])
    P.emit("v", lambda e: e.memset(onesf32[0:1, :], 1.0), reads=[r_const], writes=[r_const])
    for par in range(2):
        vtr = vt_sb[par][:, :].rearrange("p (kc w) -> p kc w", w=VTW)
        P.emit("v", lambda e, par=par: e.memset(vt_sb[par][:, :], 0.0), writes=[r_vt[par]])
        P.emit("v", lambda e, vtr=vtr, par=par: e.memset(vtr[:, :, 64:66], 1.0),
               reads=[r_vt[par]], writes=[r_vt[par]])

    P.dma("g", MIG, lambda e: e.dma_start(bq_sb[:, :].rearrange("p (r j) -> p r j", r=3),
                                          bqkv.rearrange("(r j p) -> p r j", p=128, r=3),
                                          allow_slow_non_contiguous=True),
          writes=[r_bq])
    P.dma("g", MIG, lambda e: e.dma_start(bout_row[0:1, :],
                                          bout.rearrange("(a d) -> a d", a=1)),
          writes=[r_boutrow])
    P.dma("g", MIG, lambda e: e.dma_start(wrt_hi[:, :].rearrange("p (dc j) -> p dc j", j=8),
                                          wrt.rearrange("(dc p) j -> p dc j", p=128)),
          writes=[r_wrthi])
    P.dma("y", MIY, lambda e: e.dma_start(wrt32[:, :].rearrange("p (dc j) -> p dc j", j=8),
                                          wrt.rearrange("(dc p) j -> p dc j", p=128)),
          writes=[r_wrt32])
    P.emit("v", lambda e: e.tensor_scalar_mul(bq_sb[:, 0:8], bq_sb[:, 0:8], 0.25),
           reads=[r_bq], writes=[r_bq])
    P.emit("v", lambda e: e.tensor_scalar_mul(bq_sb[:, 8:16], bq_sb[:, 8:16], 0.5),
           reads=[r_bq], writes=[r_bq])
    P.emit("v", lambda e: e.tensor_sub(wrt_lo[:, :], wrt32[:, :], wrt_hi[:, :]),
           reads=[r_wrt32, r_wrthi], writes=[r_wrtlo])

    def wq_load(p):
        buf = p % 2
        for r in range(3):
            P.dma("g", WQS[buf][r], lambda e, p=p, buf=buf, r=r: e.dma_start(
                wq_sb[buf][:, :].rearrange("p (dc r cc) -> p dc r cc", r=3, cc=128)[:, :, r, :],
                bass.AP(wqkv.tensor, r * 1024 + p * 128,
                        [[3072, 128], [128 * 3072, ND], [1, 128]])),
                writes=[r_wq3[buf][r]])

    wq_load(0)
    if c.NP > 1:
        wq_load(1)

    def emit_xtile(j):
        for dc in range(ND):
            P.emit("t", lambda e, j=j, dc=dc: e.transpose(
                ps_tp[j % 2][:, dc * 128:(dc + 1) * 128],
                xstage[j % 4][:, dc * 128:(dc + 1) * 128], ident[:, :]),
                reads=[r_xst[j % 4], r_const], writes=[r_pstp[j % 2]])
        if j < NT:
            P.emit("v", lambda e, j=j: e.tensor_copy(
                xfv[:, :, j * 128:(j + 1) * 128],
                ps_tp[j % 2][:, :].rearrange("p (dc t) -> p dc t", t=128)),
                reads=[r_pstp[j % 2]], writes=[r_xf])
        else:
            P.emit("v", lambda e, j=j: e.tensor_copy(
                xov[:, :, (j - NT) * 128:(j - NT + 1) * 128],
                ps_tp[j % 2][:, :].rearrange("p (dc t) -> p dc t", t=128)),
                reads=[r_pstp[j % 2]], writes=[r_xof])

    for i in range(c.NTS):
        P.dma("g", XST[i % 4], lambda e, i=i: e.dma_start(
            xstage[i % 4][:, :], xb[i * 128:(i + 1) * 128, :]), writes=[r_xst[i % 4]])
        if i >= 2:
            emit_xtile(i - 2)
    for j in range(max(0, c.NTS - 2), c.NTS):
        emit_xtile(j)

    if c.debug:
        P.dma("g", DBG, lambda e: e.dma_start(dbg_xf, x_f[:, :]), reads=[r_xf])
        P.dma("g", DBG, lambda e: e.dma_start(dbg_xof, xo_f[:, :]), reads=[r_xof])

    pa0.close()
    pa = ExitStack()
    Eap = pa.enter_context
    ps_pool = [Eap(nc.psum_tensor(f"ps_pool{i}", [128, 1024], F32)) for i in range(3)]
    ps_av = [Eap(nc.psum_tensor(f"ps_av{i}", [128, 512], F32)) for i in range(2)]
    r_pool = [R(f"pool{i}") for i in range(3)]
    r_psav = [R("psav0"), R("psav1")]
    pctr = [0]
    pend_slots = {}

    def slot():
        for _ in range(4):
            i = pctr[0] % 3
            pctr[0] += 1
            if pend_slots.get(i, 0) == 0:
                return i
        raise RuntimeError("no free psum slot")

    def slot_S():
        i = slot()
        pend_slots[i] = pend_slots.get(i, 0) + 1
        return i
    P.emit("t", lambda e: e.matmul(ps_pool[0][0:1, 0:1], ones_fc[0:1, 0:1], ones_fc[0:1, 0:1],
                                   start=True, stop=True),
           reads=[r_const], writes=r_pool + r_pstp + r_psav)

    vtview = [bass.AP(vt_sb[i], 0, [[NKC * VTW, 128], [VTW, 4], [97, 2], [1, 64]])
              for i in range(2)]

    def mk_proj_thunks(p):
        par = p % 2
        buf = p % 2
        th = []

        def proj_group(kind, j0, vsl=None):
            r = {"q": 0, "k": 1, "v": 2}[kind]
            lim = TOK if kind == "q" else SEQ

            def go():
                si_ = slot()
                prj = ps_pool[si_]
                r_prj = r_pool[si_]
                if kind == "v":
                    for k2 in range(2):
                        for kq in range(4):
                            t0 = (j0 + k2) * 512 + kq * 128
                            for dc in range(ND):
                                def mmv(e, dc=dc, k2=k2, kq=kq, t0=t0, prj=prj):
                                    lh = (xfv[:, dc, t0:t0 + 128] if t0 < TOK
                                          else xov[:, dc, t0 - TOK:t0 - TOK + 128])
                                    return e.matmul(
                                        prj[:, k2 * 512 + kq * 128:k2 * 512 + (kq + 1) * 128],
                                        lh, wq_sb[buf][:, dc * 384 + 256:dc * 384 + 384],
                                        start=(dc == 0), stop=(dc == ND - 1))
                                rsrc = r_xf if t0 < TOK else r_xof
                                P.emit("t", mmv, reads=[r_wq3[buf][r], rsrc], writes=[r_prj])
                    for k2 in range(2):
                        kcb = (j0 + k2) * 4
                        P.emit("v", lambda e, kcb=kcb, k2=k2, prj=prj, par=par: e.tensor_copy(
                            bass.AP(vt_sb[par], kcb * VTW,
                                    [[NKC * VTW, 128], [VTW, 4], [97, 2], [1, 64]]),
                            prj[:, k2 * 512:(k2 + 1) * 512].rearrange(
                                "p (kc h w) -> p kc h w", h=2, w=64)),
                            reads=[r_prj], wdisj=[r_vt[par]])
                    return
                for dc in range(ND):
                    for k2 in range(2):
                        if (j0 + k2) * 512 >= lim:
                            continue
                        def mm(e, dc=dc, k2=k2, j0=j0, r=r, prj=prj):
                            t0 = (j0 + k2) * 512
                            if kind == "q":
                                rhs = xfv[:, dc, t0:t0 + 512]
                            else:
                                rhs = (xfv[:, dc, t0:t0 + 512] if t0 < TOK
                                       else xov[:, dc, t0 - TOK:t0 - TOK + 512])
                            return e.matmul(
                                prj[:, k2 * 512:(k2 + 1) * 512],
                                wq_sb[buf][:, dc * 384 + r * 128:dc * 384 + (r + 1) * 128],
                                rhs, start=(dc == 0), stop=(dc == ND - 1))
                        rsrc = r_xf if (kind == "q" or (j0 + k2) * 512 < TOK) else r_xof
                        P.emit("t", mm, reads=[r_wq3[buf][r], rsrc], writes=[r_prj])
                for k2 in range(2):
                    if (j0 + k2) * 512 >= lim:
                        continue
                    if kind == "q":
                        P.emit("v", lambda e, k2=k2, j0=j0, prj=prj: e.tensor_scalar(
                            q_sb[par][:, (j0 + k2) * 512:(j0 + k2 + 1) * 512],
                            prj[:, k2 * 512:(k2 + 1) * 512], 0.25, bq_sb[:, p:p + 1],
                            ALU.mult, ALU.add),
                            reads=[r_prj, r_bq], writes=[r_q[par]])
                    else:
                        P.emit("v", lambda e, k2=k2, j0=j0, prj=prj: e.tensor_scalar(
                            k_sb[par][:, (j0 + k2) * 512:(j0 + k2 + 1) * 512],
                            prj[:, k2 * 512:(k2 + 1) * 512], 0.5, bq_sb[:, 8 + p:9 + p],
                            ALU.mult, ALU.add),
                            reads=[r_prj, r_bq], writes=[r_k[par]])
            return go

        for j0 in range(0, NQS, 2):
            th.append(proj_group("q", j0))
        for j0 in range(0, NKV, 2):
            th.append(proj_group("k", j0))

        def cchain():
            nq = max(1, TOK // 512)
            nk = SEQ // 512
            nall = nq + nk
            for j in range(nall):
                src = q_sb[par] if j < nq else k_sb[par]
                rr = r_q[par] if j < nq else r_k[par]
                jj = j if j < nq else j - nq
                P.emit("v", lambda e, src=src, jj=jj, j=j: e.tensor_mul(
                    sqw[:, (j % 4) * 512:(j % 4) * 512 + 512],
                    src[:, jj * 512:(jj + 1) * 512], src[:, jj * 512:(jj + 1) * 512]),
                    reads=[rr], writes=[r_sqw])
                if j % 4 == 3 or j == nall - 1:
                    base = j - (j % 4)
                    si_ = slot()
                    prj = ps_pool[si_]
                    for m in range(j % 4 + 1):
                        P.emit("t", lambda e, m=m, prj=prj: e.matmul(
                            prj[0:1, m * 256:m * 256 + 256],
                            ones_fc[:, 0:1],
                            sqw[:, m * 512:m * 512 + 512].rearrange(
                                "p (a b) -> p a b", a=2)[:, 0, :]
                            if False else sqw[:, m * 512:m * 512 + 256],
                            start=True, stop=True),
                            reads=[r_sqw, r_const], writes=[r_pool[si_]]) if False else None
                        P.emit("t", lambda e, m=m, prj=prj: e.matmul(
                            prj[0:1, m * 512:m * 512 + 512], ones_fc[:, 0:1],
                            sqw[:, m * 512:m * 512 + 512], start=True, stop=True),
                            reads=[r_sqw, r_const], writes=[r_pool[si_]]) if m < 2 else None
                        if m >= 2:
                            si2 = slot()
                            prj2 = ps_pool[si2]
                            P.emit("t", lambda e, m=m, prj2=prj2: e.matmul(
                                prj2[0:1, (m - 2) * 512:(m - 2) * 512 + 512], ones_fc[:, 0:1],
                                sqw[:, m * 512:m * 512 + 512], start=True, stop=True),
                                reads=[r_sqw, r_const], writes=[r_pool[si2]])
                            P.emit("v", lambda e, base=base, m=m, prj2=prj2: e.reduce_max(
                                csc[0:1, base + m:base + m + 1],
                                prj2[0:1, (m - 2) * 512:(m - 2) * 512 + 512], axis=AX.X),
                                reads=[r_pool[si2]], writes=[r_csc])
                        else:
                            P.emit("v", lambda e, base=base, m=m, prj=prj: e.reduce_max(
                                csc[0:1, base + m:base + m + 1],
                                prj[0:1, m * 512:m * 512 + 512], axis=AX.X),
                                reads=[r_pool[si_]], writes=[r_csc])
            P.emit("v", lambda e, nq=nq: e.reduce_max(csc[0:1, 12:13], csc[0:1, 0:nq],
                                                      axis=AX.X), reads=[r_csc], writes=[r_csc])
            P.emit("v", lambda e, nq=nq, nk=nk: e.reduce_max(csc[0:1, 13:14],
                                                             csc[0:1, nq:nq + nk], axis=AX.X),
                   reads=[r_csc], writes=[r_csc])
            P.emit("v", lambda e: e.tensor_add(csc[0:1, 14:15], csc[0:1, 12:13],
                                               csc[0:1, 13:14]), reads=[r_csc], writes=[r_csc])
            P.emit("v", lambda e: e.tensor_scalar_mul(csc[0:1, 15:16], csc[0:1, 14:15], -0.51),
                   reads=[r_csc], writes=[r_csc])
            P.emit("v", lambda e: e.tensor_copy(cbf[0:1, 0:1], csc[0:1, 15:16]),
                   reads=[r_csc], writes=[r_cbf])
            si_ = slot()
            prj = ps_pool[si_]
            P.emit("t", lambda e, prj=prj: e.matmul(prj[:, 1022:1023], ones_bf[0:1, :],
                                                    cbf[0:1, 0:1], start=True, stop=True),
                   reads=[r_cbf, r_const], writes=[r_pool[si_]])
            P.emit("s", lambda e, par=par, prj=prj: e.copy(negc[:, par:par + 1],
                                                           prj[:, 1022:1023]),
                   reads=[r_pool[si_]], writes=[r_negc[par]])
        th.append(cchain)
        for idx, j0 in enumerate(range(0, NKV, 2)):
            th.append(proj_group("v", j0, vsl=0))
        if p + 2 < c.NP:
            th.append(lambda: wq_load(p + 2))
        return th

    for t_ in mk_proj_thunks(0):
        t_()

    pend = []
    ilvd = [0]
    norm_pend = [None]
    for p in range(c.NP):
        par = p % 2
        if p + 1 < c.NP:
            pend = mk_proj_thunks(p + 1)
            ilvd[0] = 0

        smap = {}

        def emit_S(qs, kc, p=p, par=par):
            si_ = slot_S()
            smap[kc] = si_
            sab = ps_pool[si_]
            P.emit("t", lambda e, kc=kc, qs=qs, sab=sab: e.matmul(
                sab[:, 0:QSW], k_sb[par][0:64, kc * 128:(kc + 1) * 128],
                q_sb[par][0:64, qs * QSW:(qs + 1) * QSW], start=True, stop=True,
                tile_position=(0, 0)),
                reads=[r_k[par], r_q[par]], writes=[r_pool[si_]])
            P.emit("t", lambda e, kc=kc, qs=qs, sab=sab: e.matmul(
                sab[:, QSW:2 * QSW], k_sb[par][64:128, kc * 128:(kc + 1) * 128],
                q_sb[par][64:128, qs * QSW:(qs + 1) * QSW], start=True, stop=True,
                tile_position=(64, 0)),
                reads=[r_k[par], r_q[par]], writes=[r_pool[si_]])

        def emit_exp(qs, kc, p=p, par=par):
            sp = kc % 4
            si_ = smap.pop(kc)
            pend_slots[si_] -= 1
            sab = ps_pool[si_]
            P.emit("s", lambda e, sab=sab, sp=sp: e.activation(
                p_sb[sp][:, 0:2 * QSW], sab[:, 0:2 * QSW], AF.Exp,
                bias=negc[:, par:par + 1], scale=1.0),
                reads=[r_pool[si_], r_negc[par]], writes=[r_p[sp]])

        def emit_AV(qs, kc, p=p, par=par):
            sp = kc % 4
            P.emit("t", lambda e, kc=kc, sp=sp: e.matmul(
                ps_av[0][:, 0:QSW], vt_sb[par][:, kc * VTW:kc * VTW + 128],
                p_sb[sp][:, 0:QSW], start=(kc == 0), stop=(kc == NKC - 1)),
                reads=[r_p[sp], r_vt[par]], writes=[r_psav[0]])
            P.emit("t", lambda e, kc=kc, sp=sp: e.matmul(
                ps_av[1][:, 0:QSW], vt_sb[par][:, kc * VTW + 33:kc * VTW + 161],
                p_sb[sp][:, QSW:2 * QSW], start=(kc == 0), stop=(kc == NKC - 1)),
                reads=[r_p[sp], r_vt[par]], writes=[r_psav[1]])

        def emit_norm(p, qs):
            P.emit("s", lambda e: e.copy(rden[64:65, 0:QSW], ps_av[0][64:65, 0:QSW]),
                   reads=[r_psav[0]], writes=[r_rden])
            P.emit("s", lambda e: e.copy(rden[32:33, 0:QSW], ps_av[1][32:33, 0:QSW]),
                   reads=[r_psav[1], r_rden], writes=[r_rden])
            si_ = slot()
            prj = ps_pool[si_]
            r_prj = r_pool[si_]
            P.emit("t", lambda e, prj=prj: e.matmul(prj[:, 0:QSW], ones_bf[64:65, :],
                                                    rden[64:65, 0:QSW], start=True, stop=True),
                   reads=[r_rden, r_const], writes=[r_prj])
            P.emit("t", lambda e, prj=prj: e.matmul(prj[:, 512:512 + QSW], ones_bf[32:33, :],
                                                    rden[32:33, 0:QSW], start=True, stop=True),
                   reads=[r_rden, r_const], writes=[r_prj])

            def _recipA(e, prj=prj):
                with nc.allow_low_precision(reason="bf16 softmax denom (honest-bf16 path)"):
                    return e.reciprocal(bc_sb[0][:, 0:QSW], prj[:, 0:QSW])

            def _recipB(e, prj=prj):
                with nc.allow_low_precision(reason="bf16 softmax denom (honest-bf16 path)"):
                    return e.reciprocal(bc_sb[1][:, 0:QSW], prj[:, 512:512 + QSW])
            P.emit("v", _recipA, reads=[r_prj], writes=[r_bc[0]])
            P.emit("v", _recipB, reads=[r_prj], writes=[r_bc[1]])
            P.emit("v", lambda e, p=p, qs=qs: e.tensor_mul(
                O_all[0:64, p * TOK + qs * QSW:p * TOK + (qs + 1) * QSW],
                ps_av[0][0:64, 0:QSW], bc_sb[0][0:64, 0:QSW]),
                reads=[r_psav[0], r_bc[0]], writes=[r_oall])
            P.emit("v", lambda e, p=p, qs=qs: e.tensor_mul(
                O_all[64:128, p * TOK + qs * QSW:p * TOK + (qs + 1) * QSW],
                ps_av[1][64:128, 0:QSW], bc_sb[1][64:128, 0:QSW]),
                reads=[r_psav[1], r_bc[1]], writes=[r_oall])
            if c.debug and p == 0 and qs == 1:
                P.emit("v", lambda e: e.tensor_copy(x_f[:, 0:512], ps_av[0][:, 0:512]),
                       reads=[r_psav[0]], writes=[r_xf]) if False else None
                P.dma("g", DBG, lambda e: e.dma_start(dbg_nrm[:, 0:512], bc_sb[0][:, 0:512]),
                      reads=[r_bc[0]])
                P.dma("g", DBG, lambda e: e.dma_start(dbg_nrm[64:65, 512:1024],
                                                      rden[64:65, 0:512]), reads=[r_rden])
                P.dma("g", DBG, lambda e: e.dma_start(dbg_nrm[32:33, 512:1024],
                                                      rden[32:33, 0:512]), reads=[r_rden])
                P.dma("g", DBG, lambda e: e.dma_start(dbg_nrm[:, 1024:1026], negc[:, 0:2]),
                      reads=[r_negc[0], r_negc[1]])

        n_iters = NQS * (NKC // 2 + 3)
        pop_every = max(1, n_iters // (len(pend) + 2)) if pend else 1
        itc = 0
        for qs in range(NQS):
            for j in range(0, NKC + 6, 2):
                if 2 <= j <= NKC:
                    emit_exp(qs, j - 2)
                    emit_exp(qs, j - 1)
                if j < NKC:
                    emit_S(qs, j)
                    emit_S(qs, j + 1)
                if j == 0 and norm_pend[0] is not None:
                    emit_norm(*norm_pend[0])
                    norm_pend[0] = None
                if 4 <= j - 0 and j - 4 < NKC:
                    emit_AV(qs, j - 4)
                    emit_AV(qs, j - 3)
                if pend and itc % pop_every == 0 and os.environ.get("NOILV") is None:
                    pend.pop(0)()
                itc += 1
            norm_pend[0] = (p, qs)
        while pend:
            pend.pop(0)()
    if norm_pend[0] is not None:
        emit_norm(*norm_pend[0])
        norm_pend[0] = None

    P.flush_block()
    st_a.close()
    pa.close()

    st_bend = ExitStack()
    Ebe = st_bend.enter_context
    x_res = Ebe(nc.sbuf_tensor("x_res", [128, NT * 1024], F32, side="right"))
    r_xres = [R(f"xres{t}") for t in range(NT)]
    r_xscr = R("xscr")
    st_b = ExitStack()
    Eb = st_b.enter_context
    wout_sb = Eb(nc.sbuf_tensor("wout_sb", [128, ND * 1024], BF16))
    b_bc = Eb(nc.sbuf_tensor("b_bc", [128, D], BF16))
    bqv = Eb(nc.sbuf_tensor("bqv", [128, 8], BF16))
    bvw_row = Eb(nc.sbuf_tensor("bvw_row", [1, D], BF16))
    r_bqv, r_bvw = R("bqv"), R("bvw")
    xhi_t = [Eb(nc.sbuf_tensor(f"xhi_t{i}", [128, D], BF16)) for i in range(2)]
    xlo_t = [Eb(nc.sbuf_tensor(f"xlo_t{i}", [128, D], BF16)) for i in range(2)]
    xlo_f = [Eb(nc.sbuf_tensor(f"xlo_f{i}", [128, D], BF16)) for i in range(2)]
    xsq = Eb(nc.sbuf_tensor("xsq", [128, 8 * D], BF16))
    exq = Eb(nc.sbuf_tensor("exq", [128, 64], F32))
    eqb = Eb(nc.sbuf_tensor("eqb", [128, 8], F32))
    mtok = Eb(nc.sbuf_tensor("mtok", [128, 8], BF16))
    r_bbc = R("b_bc")
    r_xhi = [R("xhi0"), R("xhi1")]
    r_xlo = [R("xlo0"), R("xlo1")]
    r_xlof = [R("xlof0"), R("xlof1")]
    r_xsq, r_exq, r_eqb, r_mtok = R("xsq"), R("exq"), R("eqb"), R("mtok")

    pb = ExitStack()
    Ebp = pb.enter_context
    ps_u = [Ebp(nc.psum_tensor(f"ps_u{i}", [128, 1024], F32)) for i in range(2)]
    ps_lg = Ebp(nc.psum_tensor("ps_lg", [128, 512], F32))
    ps_tb = [Ebp(nc.psum_tensor(f"ps_tb{i}", [128, 1024], BF16)) for i in range(2)]
    r_psu = [R("psu0"), R("psu1")]
    r_pslg = R("pslg")
    r_pstb = [R("pstb0"), R("pstb1")]

    a_sbuf_res = ([r_xof, r_sq, r_mst, r_rden] + r_q + r_k + r_vt +
                  r_wq3[0] + r_wq3[1] + r_xst + r_p + r_bc)
    P.dma("y", XRS, lambda e: e.dma_start(
        bass.AP(x_res, 0, [[NT * 1024, 128], [1024, NT], [1, 1024]]),
        bass.AP(xb.tensor, 0, [[1024, 128], [128 * 1024, NT], [1, 1024]])),
        writes=r_xres + a_sbuf_res)
    P.dma("g", WOS, lambda e: e.dma_start(
        wout_sb[:, :].rearrange("p (dc j) -> p dc j", j=1024),
        wout.rearrange("(dc p) j -> p dc j", p=128)),
        writes=[r_wout] + a_sbuf_res)

    a_psum_res = r_pool + r_psav
    P.emit("v", lambda e: e.tensor_copy(bqv[:, 0:8], bq_sb[:, 16:24]),
           reads=[r_bq], writes=[r_bqv])
    for half in range(2):
        P.emit("t", lambda e, half=half: e.matmul(
            ps_u[0][:, half * 512:(half + 1) * 512], ones_bf[0:1, :],
            bout_row[0:1, half * 512:(half + 1) * 512], start=True, stop=False),
            reads=[r_boutrow, r_const], writes=[r_psu[0]] + (a_psum_res if half == 0 else []))
    for pr in range(c.NP):
        for half in range(2):
            P.emit("t", lambda e, pr=pr, half=half: e.matmul(
                ps_u[1][0:1, half * 512:(half + 1) * 512], bqv[:, pr:pr + 1],
                wout_sb[:, pr * 1024 + half * 512:pr * 1024 + (half + 1) * 512],
                start=(pr == 0), stop=(pr == c.NP - 1)),
                reads=[r_bqv, r_wout], writes=[r_psu[1]])
    P.emit("s", lambda e: e.copy(bvw_row[0:1, :], ps_u[1][0:1, :]),
           reads=[r_psu[1]], writes=[r_bvw])
    for half in range(2):
        P.emit("t", lambda e, half=half: e.matmul(
            ps_u[0][:, half * 512:(half + 1) * 512], ones_bf[0:1, :],
            bvw_row[0:1, half * 512:(half + 1) * 512], start=False, stop=True),
            reads=[r_bvw, r_const], writes=[r_psu[0]])
    P.emit("s", lambda e: e.copy(b_bc[:, :], ps_u[0][:, :]), reads=[r_psu[0]], writes=[r_bbc])
    for tt in range(NT):
        P.emit("v", lambda e, tt=tt: e.tensor_add(
            x_res[:, tt * 1024:(tt + 1) * 1024], x_res[:, tt * 1024:(tt + 1) * 1024],
            b_bc[:, :]), reads=[r_bbc, r_xres[tt]], writes=[r_xres[tt]])

    GB = min(8, NT)
    for g0 in range(0, NT, GB):
        gi = g0 // GB
        st = gi % 2
        rs = r_stat[st]
        sb_ = stat[st]
        for g in range(GB):
            tt = g0 + g
            for half in range(2):
                for pr in range(c.NP):
                    P.emit("t", lambda e, pr=pr, half=half, tt=tt: e.matmul(
                        ps_u[0][:, half * 512:(half + 1) * 512],
                        O_all[0:64, pr * TOK + tt * 128:pr * TOK + (tt + 1) * 128],
                        wout_sb[0:64, pr * 1024 + half * 512:pr * 1024 + (half + 1) * 512],
                        start=(pr == 0), stop=(pr == c.NP - 1), tile_position=(0, 0)),
                        reads=[r_oall, r_wout], writes=[r_psu[0]])
                    P.emit("t", lambda e, pr=pr, half=half, tt=tt: e.matmul(
                        ps_u[1][:, half * 512:(half + 1) * 512],
                        O_all[64:128, pr * TOK + tt * 128:pr * TOK + (tt + 1) * 128],
                        wout_sb[64:128, pr * 1024 + half * 512:pr * 1024 + (half + 1) * 512],
                        start=(pr == 0), stop=(pr == c.NP - 1), tile_position=(64, 0)),
                        reads=[r_oall, r_wout], writes=[r_psu[1]])
            P.emit("v", lambda e, tt=tt: e.tensor_add(
                x_res[:, tt * 1024:(tt + 1) * 1024], x_res[:, tt * 1024:(tt + 1) * 1024],
                ps_u[0][:, :]), reads=[r_psu[0], r_xres[tt]], writes=[r_xres[tt]])
            P.emit("v", lambda e, tt=tt: e.tensor_add(
                x_res[:, tt * 1024:(tt + 1) * 1024], x_res[:, tt * 1024:(tt + 1) * 1024],
                ps_u[1][:, :]), reads=[r_psu[1], r_xres[tt]], writes=[r_xres[tt]])
            P.emit("v", lambda e, tt=tt, g=g, sb_=sb_: e.reduce_sum(
                sb_[:, g:g + 1], x_res[:, tt * 1024:(tt + 1) * 1024], axis=AX.X),
                reads=[r_xres[tt]], writes=[rs])
            P.emit("v", lambda e, tt=tt, g=g: e.tensor_mul(
                xsq[:, g * D:(g + 1) * D], x_res[:, tt * 1024:(tt + 1) * 1024],
                x_res[:, tt * 1024:(tt + 1) * 1024]), reads=[r_xres[tt]], writes=[r_xsq])
            P.emit("v", lambda e, g=g, sb_=sb_: e.reduce_sum(
                sb_[:, 8 + g:9 + g], xsq[:, g * D:(g + 1) * D], axis=AX.X),
                reads=[r_xsq], writes=[rs])
        P.emit("v", lambda e, sb_=sb_: e.tensor_scalar_mul(sb_[:, 16:16 + GB], sb_[:, 0:GB],
                                                           1.0 / D), reads=[rs], writes=[rs])
        P.emit("v", lambda e, sb_=sb_: e.tensor_scalar_mul(sb_[:, 24:24 + GB], sb_[:, 8:8 + GB],
                                                           1.0 / D), reads=[rs], writes=[rs])
        P.emit("v", lambda e, sb_=sb_: e.tensor_mul(sb_[:, 32:32 + GB], sb_[:, 16:16 + GB],
                                                    sb_[:, 16:16 + GB]), reads=[rs], writes=[rs])
        P.emit("v", lambda e, sb_=sb_: e.tensor_sub(sb_[:, 40:40 + GB], sb_[:, 24:24 + GB],
                                                    sb_[:, 32:32 + GB]), reads=[rs], writes=[rs])
        P.emit("v", lambda e, sb_=sb_: e.tensor_scalar_add(sb_[:, 40:40 + GB],
                                                           sb_[:, 40:40 + GB], 1e-5),
               reads=[rs], writes=[rs])
        P.emit("s", lambda e, sb_=sb_: e.activation(sb_[:, 48:48 + GB], sb_[:, 40:40 + GB],
                                                    AF.Ln, bias=0.0, scale=1.0),
               reads=[rs], writes=[rs])
        P.emit("s", lambda e, sb_=sb_: e.activation(sb_[:, 56:56 + GB], sb_[:, 48:48 + GB],
                                                    AF.Exp, bias=0.0, scale=-0.5),
               reads=[rs], writes=[rs])
        for g in range(GB):
            tt = g0 + g
            P.emit("v", lambda e, tt=tt, g=g, sb_=sb_: e.tensor_scalar(
                x_res[:, tt * 1024:(tt + 1) * 1024], x_res[:, tt * 1024:(tt + 1) * 1024],
                sb_[:, 16 + g:17 + g], sb_[:, 56 + g:57 + g], ALU.subtract, ALU.mult),
                reads=[rs, r_xres[tt]], writes=[r_xres[tt]])
            sp = tt % 2
            P.emit("s", lambda e, tt=tt, sp=sp: e.copy(xhi_t[sp][:, :],
                                                x_res[:, tt * 1024:(tt + 1) * 1024]),
                   reads=[r_xres[tt]], writes=[r_xhi[sp]])
            P.dma("y", XSC, lambda e, tt=tt, sp=sp: e.dma_start(
                xscr[tt * 128:(tt + 1) * 128, :], xhi_t[sp][:, :]),
                reads=[r_xhi[sp]], writes=[r_xscr])
            P.emit("v", lambda e, tt=tt, sp=sp: e.tensor_sub(xlo_t[sp][:, :],
                                                      x_res[:, tt * 1024:(tt + 1) * 1024],
                                                      xhi_t[sp][:, :]),
                   reads=[r_xres[tt], r_xhi[sp]], writes=[r_xlo[sp]])
            for dc in range(ND):
                P.emit("t", lambda e, dc=dc, sp=sp: e.transpose(
                    ps_tb[sp][:, dc * 128:(dc + 1) * 128], xhi_t[sp][:, dc * 128:(dc + 1) * 128],
                    ident[:, :]), reads=[r_xhi[sp], r_const], writes=[r_pstb[sp]])
            P.emit("s", lambda e, tt=tt, sp=sp: e.copy(
                xfv[:, :, tt * 128:(tt + 1) * 128],
                ps_tb[sp][:, :].rearrange("p (dc t) -> p dc t", t=128)),
                reads=[r_pstb[sp]], writes=[r_xf])
            for dc in range(ND):
                P.emit("t", lambda e, dc=dc, sp=sp: e.transpose(
                    ps_tb[sp][:, dc * 128:(dc + 1) * 128], xlo_t[sp][:, dc * 128:(dc + 1) * 128],
                    ident[:, :]), reads=[r_xlo[sp], r_const], writes=[r_pstb[sp]])
            P.emit("s", lambda e, sp=sp: e.copy(xlo_f[sp][:, :], ps_tb[sp][:, :]),
                   reads=[r_pstb[sp]], writes=[r_xlof[sp]])
            n_mm = 3 * ND
            i_mm = 0
            for dc in range(ND):
                for (xs, ws, rws) in ((0, wrt_hi, r_wrthi), (0, wrt_lo, r_wrtlo),
                                      (1, wrt_hi, r_wrthi)):
                    i_mm += 1
                    P.emit("t", lambda e, dc=dc, xs=xs, ws=ws, tt=tt, g=g, sp=sp,
                           first=(i_mm == 1), last=(i_mm == n_mm): e.matmul(
                        ps_lg[:, g * 8:g * 8 + 8],
                        (xfv[:, dc, tt * 128:(tt + 1) * 128] if xs == 0
                         else xlo_f[sp][:, dc * 128:(dc + 1) * 128]),
                        ws[:, dc * 8:(dc + 1) * 8], start=first, stop=last),
                        reads=[r_xf if xs == 0 else r_xlof[sp], rws], writes=[r_pslg])
        P.emit("v", lambda e, sb_=sb_: e.reduce_max(
            sb_[:, 64:64 + GB],
            ps_lg[:, 0:GB * 8].rearrange("p (g c) -> p g c", c=8), axis=AX.X),
            reads=[r_pslg], writes=[rs])
        P.emit("v", lambda e, sb_=sb_: e.tensor_scalar_mul(sb_[:, 72:72 + GB],
                                                           sb_[:, 64:64 + GB], -1.0),
               reads=[rs], writes=[rs])
        for g in range(GB):
            P.emit("s", lambda e, g=g, sb_=sb_: e.activation(
                exq[:, g * 8:g * 8 + 8], ps_lg[:, g * 8:g * 8 + 8], AF.Exp,
                bias=sb_[:, 72 + g:73 + g], scale=1.0),
                reads=[r_pslg, rs], writes=[r_exq])
        P.emit("v", lambda e, sb_=sb_: e.reduce_sum(
            sb_[:, 80:80 + GB],
            exq[:, 0:GB * 8].rearrange("p (g c) -> p g c", c=8), axis=AX.X),
            reads=[r_exq], writes=[rs])
        P.emit("v", lambda e, sb_=sb_: e.reciprocal(sb_[:, 88:88 + GB], sb_[:, 80:80 + GB]),
               reads=[rs], writes=[rs])
        for g in range(GB):
            tt = g0 + g
            P.emit("v", lambda e, g=g, sb_=sb_: e.tensor_scalar(
                eqb[:, 0:8], ps_lg[:, g * 8:g * 8 + 8], sb_[:, 64 + g:65 + g],
                sb_[:, 88 + g:89 + g], ALU.is_equal, ALU.mult),
                reads=[r_pslg, rs], writes=[r_eqb])
            P.emit("v", lambda e: e.tensor_scalar_mul(mtok[:, 0:8], eqb[:, 0:8], 2.0),
                   reads=[r_eqb], writes=[r_mtok])
            sp = tt % 2
            P.emit("t", lambda e, sp=sp: e.transpose(ps_tb[sp][0:8, 512:640], mtok[:, 0:8],
                                                     ident[:, :]),
                   reads=[r_mtok, r_const], writes=[r_pstb[sp]])
            P.emit("s", lambda e, tt=tt, sp=sp: e.copy(m_f[0:8, tt * 128:(tt + 1) * 128],
                                                       ps_tb[sp][0:8, 512:640]),
                   reads=[r_pstb[sp]], writes=[r_mf])

    P.flush_block()
    st_b.close()
    pb.close()
    st_ab.close()

    CAP, NSC = c.CAP, c.NSC
    st_cA = ExitStack()
    Ec = st_cA.enter_context
    wubA = [Ec(nc.sbuf_tensor(f"wubA{i}", [128, 8 * 512], BF16)) for i in range(2)]
    wgbA = [Ec(nc.sbuf_tensor(f"wgbA{i}", [128, 8 * 512], BF16)) for i in range(2)]
    wdbA = [Ec(nc.sbuf_tensor(f"wdbA{i}", [128, 4 * 1024], BF16)) for i in range(2)]
    GW2 = min(2, NT4) * 512
    hidA = Ec(nc.sbuf_tensor("hidA", [128, 4 * GW2], BF16))
    siluA = Ec(nc.sbuf_tensor("siluA", [128, GW2], BF16))
    r_wubA = [R("wubA0"), R("wubA1")]
    r_wgbA = [R("wgbA0"), R("wgbA1")]
    r_wdbA = [R("wdbA0"), R("wdbA1")]
    r_hidA = [R(f"hidA{i}") for i in range(4)]
    r_siluA = R("siluA")

    pc = ExitStack()
    Ecp = pc.enter_context
    ps_up = Ecp(nc.psum_tensor("ps_up", [128, 1024], F32))
    ps_ga = Ecp(nc.psum_tensor("ps_ga", [128, 1024], F32))
    ps_dn = [Ecp(nc.psum_tensor(f"ps_dn{i}", [128, 1024], F32)) for i in range(2)]
    r_psup = R("psup")
    r_psga = R("psga")
    r_psdn = [R("psdn0"), R("psdn1")]

    r_gateB = R("gateB")
    b_alias = [r_oall, r_wout, r_bbc, r_xsq, r_exq, r_eqb, r_mtok] + \
        r_xhi + r_xlo + r_xlof + a_sbuf_res
    b_psum = [r_psu[0], r_psu[1], r_pslg] + r_pstb
    P.emit("t", lambda e: e.matmul(ps_up[0:1, 0:1], ones_fc[0:1, 0:1], ones_fc[0:1, 0:1],
                                   start=True, stop=True),
           writes=[r_gateB, r_psup] + b_alias + b_psum)

    experts = [-1] + list(range(c.ER))
    steps = [(e_, hg) for e_ in experts for hg in range(NHG)]

    def emit_wload(si):
        e_, hg = steps[si]
        buf = si % 2
        if e_ == -1:
            ups, gas, dns = wsup, wsgate, wsdown
            uo, do = hg * 512, hg * 512 * 1024
            ub, gb, db = wubA[buf], wgbA[buf], wdbA[buf]
            rub, rgb, rdb = r_wubA[buf], r_wgbA[buf], r_wdbA[buf]
            gate = []
        else:
            ups, gas, dns = wup, wgate, wdown
            uo, do = e_ * D * H + hg * 512, e_ * H * D + hg * 512 * 1024
            ub, gb, db = wub2[buf], wgb2[buf], wdb2[buf]
            rub, rgb, rdb = r_wub2[buf], r_wgb2[buf], r_wdb2[buf]
            gate = [r_gateC2] if si < NHG + 2 else []
        extra = (a_sbuf_res + [r_bbc, r_xsq, r_exq, r_eqb, r_mtok, r_wout,
                                r_oall] + r_xhi + r_xlo + r_xlof) if si < 2 else []
        P.dma("g", WUS[buf], lambda e, ups=ups, uo=uo, ub=ub: e.dma_start(
            ub[:, :].rearrange("p (dc j) -> p dc j", j=512),
            bass.AP(ups.tensor, uo, [[H, 128], [128 * H, ND], [1, 512]])),
            reads=gate, writes=[rub] + extra)
        P.dma("g", WGS[buf], lambda e, gas=gas, uo=uo, gb=gb: e.dma_start(
            gb[:, :].rearrange("p (dc j) -> p dc j", j=512),
            bass.AP(gas.tensor, uo, [[H, 128], [128 * H, ND], [1, 512]])),
            reads=gate, writes=[rgb])
        P.dma("g", WDS[buf], lambda e, dns=dns, do=do, db=db: e.dma_start(
            db[:, :].rearrange("p (hc j) -> p hc j", j=1024),
            bass.AP(dns.tensor, do, [[1024, 128], [128 * 1024, 4], [1, 1024]])),
            reads=gate, writes=[rdb])

    emit_wload(0)
    emit_wload(1)

    GS = min(2, NT4)
    groups = [tuple(range(g, min(g + GS, NT4))) for g in range(0, NT4, GS)]
    dn_ctr = [0]

    for si in range(NHG):
        hg = si
        buf = si % 2
        for grp in groups:
            gw = len(grp) * 512
            for hc in range(4):
                for dc in range(ND):
                    for ki, tt4 in enumerate(grp):
                        P.emit("t", lambda e, dc=dc, hc=hc, tt4=tt4, ki=ki, buf=buf: e.matmul(
                            ps_ga[:, ki * 512:ki * 512 + T4W],
                            wgbA[buf][:, dc * 512 + hc * 128:dc * 512 + hc * 128 + 128],
                            xfv[:, dc, tt4 * T4W:(tt4 + 1) * T4W],
                            start=(dc == 0), stop=(dc == ND - 1)),
                            reads=[r_wgbA[buf], r_xf], writes=[r_psga])
                P.emit("s", lambda e, gw=gw: e.activation(
                    siluA[:, 0:gw], ps_ga[:, 0:gw], AF.Silu),
                    reads=[r_psga, r_gateB], writes=[r_siluA])
                for dc in range(ND):
                    for ki, tt4 in enumerate(grp):
                        P.emit("t", lambda e, dc=dc, hc=hc, tt4=tt4, ki=ki, buf=buf: e.matmul(
                            ps_up[:, ki * 512:ki * 512 + T4W],
                            wubA[buf][:, dc * 512 + hc * 128:dc * 512 + hc * 128 + 128],
                            xfv[:, dc, tt4 * T4W:(tt4 + 1) * T4W],
                            start=(dc == 0), stop=(dc == ND - 1)),
                            reads=[r_wubA[buf], r_xf], writes=[r_psup])
                P.emit("v", lambda e, hc=hc, gw=gw: e.tensor_mul(
                    hidA[:, hc * GW2:hc * GW2 + gw], ps_up[:, 0:gw], siluA[:, 0:gw]),
                    reads=[r_psup, r_siluA, r_gateB], writes=[r_hidA[hc]])
            for ki, tt4 in enumerate(grp):
                for it in range(T4W // 128):
                    tt = tt4 * (T4W // 128) + it
                    par = dn_ctr[0] % 2
                    dn_ctr[0] += 1
                    for hc in range(4):
                        for half in range(2):
                            P.emit("t", lambda e, hc=hc, half=half, it=it, ki=ki, par=par,
                                   buf=buf: e.matmul(
                                ps_dn[par][:, half * 512:(half + 1) * 512],
                                hidA[:, hc * GW2 + ki * 512 + it * 128:
                                      hc * GW2 + ki * 512 + (it + 1) * 128],
                                wdbA[buf][:, hc * 1024 + half * 512:hc * 1024 + (half + 1) * 512],
                                start=(hc == 0), stop=(hc == 3)),
                                reads=[r_hidA[hc], r_wdbA[buf]], writes=[r_psdn[par]])
                    P.emit("v", lambda e, tt=tt, par=par: e.tensor_add(
                        x_res[:, tt * 1024:(tt + 1) * 1024],
                        x_res[:, tt * 1024:(tt + 1) * 1024],
                        ps_dn[par][:, :]), reads=[r_psdn[par], r_xres[tt]],
                        writes=[r_xres[tt]])
        if si + 2 < NHG:
            emit_wload(si + 2)

    st_cA.close()
    st_xf.close()
    st_c2 = ExitStack()
    Ec2 = st_c2.enter_context
    wub2 = [Ec2(nc.sbuf_tensor(f"wub2_{i}", [128, 8 * 512], BF16)) for i in range(2)]
    wgb2 = [Ec2(nc.sbuf_tensor(f"wgb2_{i}", [128, 8 * 512], BF16)) for i in range(2)]
    wdb2 = [Ec2(nc.sbuf_tensor(f"wdb2_{i}", [128, 4 * 1024], BF16)) for i in range(2)]
    stripe = [Ec2(nc.sbuf_tensor(f"stripe{i}", [128, TOK], BF16)) for i in range(2)]
    Pv = Ec2(nc.sbuf_tensor("Pv", [128, max(NT * CAP, 4 * D)], BF16))
    PTm = Ec2(nc.sbuf_tensor("PTm", [128, NSC * TOK], BF16))
    MBc = Ec2(nc.sbuf_tensor("MBc", [128, TOK], BF16))
    keyb = Ec2(nc.sbuf_tensor("keyb", [128, TOK], F32))
    xg = Ec2(nc.sbuf_tensor("xg", [128, ND * CAP], BF16))
    dg_bf = Ec2(nc.sbuf_tensor("dg_bf", [128, NSC * 1024], BF16))
    hid3 = Ec2(nc.sbuf_tensor("hid3", [128, 4 * CAP], BF16))
    silu_r = Ec2(nc.sbuf_tensor("silu_r", [128, CAP], BF16))
    mrow = Ec2(nc.sbuf_tensor("mrow", [1, TOK], BF16))
    maskr = Ec2(nc.sbuf_tensor("maskr", [1, TOK], F32))
    keyr = Ec2(nc.sbuf_tensor("keyr", [1, TOK], F32))
    kc_sb = Ec2(nc.sbuf_tensor("kc_sb", [128, NT], F32))
    iota1 = Ec2(nc.sbuf_tensor("iota1", [128, CAP], F32))
    iotaS = Ec2(nc.sbuf_tensor("iotaS", [128, NSC], F32))
    r_wub2 = [R("wub2_0"), R("wub2_1")]
    r_wgb2 = [R("wgb2_0"), R("wgb2_1")]
    r_wdb2 = [R("wdb2_0"), R("wdb2_1")]
    r_stripe = [R("stripe0"), R("stripe1")]
    r_Pv, r_PTm, r_MB, r_keyb = R("Pv"), R("PTm"), R("MB"), R("keyb")
    r_xg, r_dgbf = R("xg"), R("dgbf")
    r_hid3 = [R(f"hid3_{i}") for i in range(4)]
    r_silur, r_mrow, r_maskr, r_keyr = R("silur"), R("mrow"), R("maskr"), R("keyr")
    r_kc, r_iotac = R("kc"), R("iotac")

    r_gateC2 = R("gateC2")
    P.emit("t", lambda e: e.matmul(ps_up[0:1, 0:1], ones_fc[0:1, 0:1], ones_fc[0:1, 0:1],
                                   start=True, stop=True),
           reads=[r_const],
           writes=[r_gateC2, r_psup, r_xf, r_siluA] + r_wubA + r_wgbA + r_wdbA + r_hidA)
    P.emit("g", lambda e: e.iota(iota1[:, :], [[1, CAP]], base=1, channel_multiplier=0,
                                 allow_small_or_imprecise_dtypes=True),
           reads=[r_gateC2], writes=[r_iotac])
    P.emit("g", lambda e: e.iota(iotaS[:, :], [[128, NSC]], base=1, channel_multiplier=1,
                                 allow_small_or_imprecise_dtypes=True),
           reads=[r_gateC2], writes=[r_iotac])
    emit_wload(NHG)
    emit_wload(NHG + 1)

    all8 = [(ps_up, 0, r_psup), (ps_up, 512, r_psup),
            (ps_ga, 0, r_psga), (ps_ga, 512, r_psga),
            (ps_dn[0], 0, r_psdn[0]), (ps_dn[0], 512, r_psdn[0]),
            (ps_dn[1], 0, r_psdn[1]), (ps_dn[1], 512, r_psdn[1])]
    dn4 = all8[4:]
    ug4 = all8[:4]
    dnmap = [[all8[4], all8[5]], [all8[6], all8[7]], [all8[1], all8[3]]]

    def emit_stripe(dc):
        b = dc % 2
        P.dma("y", STR[b], lambda e, dc=dc, b=b: e.dma_start(
            stripe[b][:, :].rearrange("p (tc q) -> p tc q", q=128),
            bass.AP(xscr.tensor, dc * 128, [[1024, 128], [128 * 1024, NT], [1, 128]])),
            reads=[r_xscr, r_gateC2], writes=[r_stripe[b]])

    def emit_pro_mrow(e_):
        nch = TOK // 512
        for j in range(nch):
            pst, off, rp = ug4[j % 4]
            P.emit("t", lambda e, j=j, pst=pst, off=off, e_=e_: e.matmul(
                pst[0:1, off:off + 512], sel[0:8, e_ * 128:e_ * 128 + 1],
                m_f[0:8, j * 512:(j + 1) * 512], start=True, stop=True),
                reads=[r_const, r_mf, r_gateC2], writes=[rp])
            P.emit("s", lambda e, j=j, pst=pst, off=off: e.copy(
                mrow[0:1, j * 512:(j + 1) * 512], pst[0:1, off:off + 512]),
                reads=[rp, r_gateC2], writes=[r_mrow])
        P.emit("v", lambda e: e.tensor_single_scalar(
            maskr[0:1, :], mrow[0:1, :], 0.0, ALU.is_gt),
            reads=[r_mrow, r_gateC2], writes=[r_maskr])
        P.emit("v", lambda e: e.tensor_tensor_scan(
            keyr[0:1, :], maskr[0:1, :], maskr[0:1, :], 0.0, ALU.add, ALU.max),
            reads=[r_maskr, r_gateC2], writes=[r_keyr])
        P.emit("v", lambda e: e.tensor_mul(keyr[0:1, :], keyr[0:1, :], maskr[0:1, :]),
               reads=[r_keyr, r_maskr], writes=[r_keyr])

    def emit_pro_rest(e_):
        nch = TOK // 512
        for j in range(nch):
            pst, off, rp = ug4[j % 4]
            P.emit("t", lambda e, j=j, pst=pst, off=off: e.matmul(
                pst[:, off:off + 512], onesf32[0:1, :], keyr[0:1, j * 512:(j + 1) * 512],
                start=True, stop=True), reads=[r_const, r_keyr], writes=[rp])
            P.emit("v", lambda e, j=j, pst=pst, off=off: e.tensor_copy(
                keyb[:, j * 512:(j + 1) * 512], pst[:, off:off + 512]),
                reads=[rp, r_gateC2], writes=[r_keyb])
        for j in range(nch):
            pst, off, rp = ug4[j % 4]
            P.emit("t", lambda e, j=j, pst=pst, off=off: e.matmul(
                pst[:, off:off + 512], ones_bf[0:1, :], mrow[0:1, j * 512:(j + 1) * 512],
                start=True, stop=True), reads=[r_const, r_mrow], writes=[rp])
            P.emit("s", lambda e, j=j, pst=pst, off=off: e.copy(
                MBc[:, j * 512:(j + 1) * 512], pst[:, off:off + 512]),
                reads=[rp, r_gateC2], writes=[r_MB])
        for tc in range(NT):
            pst, off, rp = ug4[tc % 4]
            P.emit("t", lambda e, tc=tc, pst=pst, off=off: e.matmul(
                pst[:, off:off + 1], keyr[0:1, tc * 128:(tc + 1) * 128],
                onesf32[0:1, 0:1], start=True, stop=True),
                reads=[r_keyr, r_const], writes=[rp])
            P.emit("v", lambda e, tc=tc, pst=pst, off=off: e.tensor_copy(
                kc_sb[:, tc:tc + 1], pst[:, off:off + 1]),
                reads=[rp, r_gateC2], writes=[r_kc])
            P.emit("v", lambda e, tc=tc: e.tensor_single_scalar(
                Pv[:, tc * CAP:(tc + 1) * CAP], iota1[:, :], kc_sb[:, tc:tc + 1],
                ALU.is_equal), reads=[r_kc, r_iotac, r_gateC2], writes=[r_Pv])
        for sc in range(NSC):
            P.emit("v", lambda e, sc=sc: e.scalar_tensor_tensor(
                PTm[:, sc * TOK:(sc + 1) * TOK], keyb[:, :], iotaS[:, sc:sc + 1],
                MBc[:, :], ALU.is_equal, ALU.mult),
                reads=[r_keyb, r_MB, r_iotac, r_gateC2], writes=[r_PTm])

    def emit_gather(e_):
        emit_stripe(0)
        if ND > 1:
            emit_stripe(1)
        for dc in range(ND):
            pst, off, rp = all8[dc]
            b = dc % 2
            for tc in range(NT):
                P.emit("t", lambda e, dc=dc, tc=tc, pst=pst, off=off, b=b: e.matmul(
                    pst[:, off:off + CAP], stripe[b][:, tc * 128:(tc + 1) * 128],
                    Pv[:, tc * CAP:(tc + 1) * CAP], start=(tc == 0), stop=(tc == NT - 1)),
                    reads=[r_stripe[b], r_Pv], writes=[rp])
            P.emit("s", lambda e, dc=dc, pst=pst, off=off: e.copy(
                xg[:, dc * CAP:(dc + 1) * CAP], pst[:, off:off + CAP]),
                reads=[rp, r_gateC2], writes=[r_xg])
            if dc + 2 < ND:
                emit_stripe(dc + 2)

    def emit_rcompute(si):
        e_, hg = steps[si]
        buf = si % 2
        for hc in range(4):
            for dc in range(ND):
                P.emit("t", lambda e, hc=hc, dc=dc, buf=buf: e.matmul(
                    ps_ga[:, 0:CAP],
                    wgb2[buf][:, dc * 512 + hc * 128:dc * 512 + hc * 128 + 128],
                    xg[:, dc * CAP:(dc + 1) * CAP],
                    start=(dc == 0), stop=(dc == ND - 1)),
                    reads=[r_wgb2[buf], r_xg], writes=[r_psga])
            P.emit("s", lambda e: e.activation(silu_r[:, 0:CAP], ps_ga[:, 0:CAP], AF.Silu),
                   reads=[r_psga, r_gateC2], writes=[r_silur])
            for dc in range(ND):
                P.emit("t", lambda e, hc=hc, dc=dc, buf=buf: e.matmul(
                    ps_up[:, 0:CAP],
                    wub2[buf][:, dc * 512 + hc * 128:dc * 512 + hc * 128 + 128],
                    xg[:, dc * CAP:(dc + 1) * CAP],
                    start=(dc == 0), stop=(dc == ND - 1)),
                    reads=[r_wub2[buf], r_xg], writes=[r_psup])
            P.emit("v", lambda e, hc=hc: e.tensor_mul(
                hid3[:, hc * CAP:(hc + 1) * CAP], ps_up[:, 0:CAP], silu_r[:, 0:CAP]),
                reads=[r_psup, r_silur, r_gateC2], writes=[r_hid3[hc]])
            for sc in range(NSC):
                for half in range(2):
                    pst, off, rp = dnmap[sc][half]
                    P.emit("t", lambda e, hc=hc, sc=sc, half=half, buf=buf, pst=pst,
                           off=off, hg=hg: e.matmul(
                        pst[:, off:off + 512],
                        hid3[:, hc * CAP + sc * 128:hc * CAP + (sc + 1) * 128],
                        wdb2[buf][:, hc * 1024 + half * 512:hc * 1024 + (half + 1) * 512],
                        start=(hg == 0 and hc == 0), stop=(hg == NHG - 1 and hc == 3)),
                        reads=[r_hid3[hc], r_wdb2[buf]], writes=[rp])

    def emit_dgevac(e_):
        for sc in range(NSC):
            for half in range(2):
                pst, off, rp = dnmap[sc][half]
                P.emit("s", lambda e, sc=sc, half=half, pst=pst, off=off: e.copy(
                    dg_bf[:, sc * 1024 + half * 512:sc * 1024 + (half + 1) * 512],
                    pst[:, off:off + 512]), reads=[rp, r_gateC2], writes=[r_dgbf])

    def emit_scatter(e_):
        for tc in range(NT):
            par = tc % 2
            for sc in range(NSC):
                for half in range(2):
                    P.emit("t", lambda e, tc=tc, sc=sc, half=half, par=par: e.matmul(
                        ps_dn[par][:, half * 512:(half + 1) * 512],
                        PTm[:, sc * TOK + tc * 128:sc * TOK + (tc + 1) * 128],
                        dg_bf[:, sc * 1024 + half * 512:sc * 1024 + (half + 1) * 512],
                        start=(sc == 0), stop=(sc == NSC - 1)),
                        reads=[r_PTm, r_dgbf], writes=[r_psdn[par]])
            P.emit("v", lambda e, tc=tc, par=par: e.tensor_add(
                x_res[:, tc * 1024:(tc + 1) * 1024], x_res[:, tc * 1024:(tc + 1) * 1024],
                ps_dn[par][:, :]), reads=[r_psdn[par], r_xres[tc]], writes=[r_xres[tc]])

    emit_pro_mrow(0)
    emit_pro_rest(0)
    emit_gather(0)
    for e_ in range(c.ER):
        for hg in range(NHG):
            si = NHG + e_ * NHG + hg
            emit_rcompute(si)
            if si + 2 < len(steps):
                emit_wload(si + 2)
        emit_dgevac(e_)
        if e_ + 1 < c.ER:
            emit_pro_mrow(e_ + 1)
        emit_scatter(e_)
        if e_ + 1 < c.ER:
            emit_pro_rest(e_ + 1)
            emit_gather(e_ + 1)

    GB2 = min(4, NT)
    for g0 in range(0, NT, GB2):
        gi = g0 // GB2
        st = gi % 2
        rs = r_stat[st]
        sb_ = stat[st]
        for g in range(GB2):
            tt = g0 + g
            P.emit("v", lambda e, tt=tt, g=g, sb_=sb_: e.reduce_sum(
                sb_[:, g:g + 1], x_res[:, tt * 1024:(tt + 1) * 1024], axis=AX.X),
                reads=[r_xres[tt]], writes=[rs])
            P.emit("v", lambda e, tt=tt, g=g: e.tensor_mul(
                Pv[:, g * D:(g + 1) * D], x_res[:, tt * 1024:(tt + 1) * 1024],
                x_res[:, tt * 1024:(tt + 1) * 1024]),
                reads=[r_xres[tt], r_gateC2], writes=[r_Pv])
            P.emit("v", lambda e, g=g, sb_=sb_: e.reduce_sum(
                sb_[:, 8 + g:9 + g], Pv[:, g * D:(g + 1) * D], axis=AX.X),
                reads=[r_Pv], writes=[rs])
        P.emit("v", lambda e, sb_=sb_: e.tensor_scalar_mul(sb_[:, 16:16 + GB2], sb_[:, 0:GB2],
                                                           1.0 / D), reads=[rs], writes=[rs])
        P.emit("v", lambda e, sb_=sb_: e.tensor_scalar_mul(sb_[:, 24:24 + GB2],
                                                           sb_[:, 8:8 + GB2],
                                                           1.0 / D), reads=[rs], writes=[rs])
        P.emit("v", lambda e, sb_=sb_: e.tensor_mul(sb_[:, 32:32 + GB2], sb_[:, 16:16 + GB2],
                                                    sb_[:, 16:16 + GB2]), reads=[rs], writes=[rs])
        P.emit("v", lambda e, sb_=sb_: e.tensor_sub(sb_[:, 40:40 + GB2], sb_[:, 24:24 + GB2],
                                                    sb_[:, 32:32 + GB2]), reads=[rs], writes=[rs])
        P.emit("v", lambda e, sb_=sb_: e.tensor_scalar_add(sb_[:, 40:40 + GB2],
                                                           sb_[:, 40:40 + GB2], 1e-5),
               reads=[rs], writes=[rs])
        P.emit("s", lambda e, sb_=sb_: e.activation(sb_[:, 48:48 + GB2], sb_[:, 40:40 + GB2],
                                                    AF.Ln, bias=0.0, scale=1.0),
               reads=[rs], writes=[rs])
        P.emit("s", lambda e, sb_=sb_: e.activation(sb_[:, 56:56 + GB2], sb_[:, 48:48 + GB2],
                                                    AF.Exp, bias=0.0, scale=-0.5),
               reads=[rs], writes=[rs])
        for g in range(GB2):
            tt = g0 + g
            P.emit("v", lambda e, tt=tt, g=g, sb_=sb_: e.tensor_scalar(
                x_res[:, tt * 1024:(tt + 1) * 1024], x_res[:, tt * 1024:(tt + 1) * 1024],
                sb_[:, 16 + g:17 + g], sb_[:, 56 + g:57 + g], ALU.subtract, ALU.mult),
                reads=[rs, r_xres[tt]], writes=[r_xres[tt]])
            P.dma("y", STO, lambda e, tt=tt: e.dma_start(
                out[tt * 128:(tt + 1) * 128, :], x_res[:, tt * 1024:(tt + 1) * 1024]),
                reads=[r_xres[tt]])

    ops = P.finalize(3, [("y", STO)] + ([("g", DBG)] if c.debug else []))

    for ebd in ops:
        with nc.Block() as blk:
            blk.tensor(lambda e, d=ebd: [f(e) for f in d["t"]])
            blk.vector(lambda e, d=ebd: [f(e) for f in d["v"]])
            blk.scalar(lambda e, d=ebd: [f(e) for f in d["s"]])
            blk.gpsimd(lambda e, d=ebd: [f(e) for f in d["g"]])
            blk.sync(lambda e, d=ebd: [f(e) for f in d["y"]])

    pc.close()
    st_c2.close()
    st_bend.close()
    st0.close()
    return nc


def kernel(**inputs):
    c = CFG()
    x = np.asarray(inputs["L0_attn_data"])
    wm = {
        "wqkv": np.asarray(inputs["L0_attn_W_qkv"]),
        "bqkv": np.asarray(inputs["L0_attn_b_qkv"]),
        "wout": np.asarray(inputs["L0_attn_W_out"]),
        "bout": np.asarray(inputs["L0_attn_b_out"]),
        "wrt": np.asarray(inputs["L0_ffn_router_w"]),
        "wup": np.asarray(inputs["L0_ffn_up_w"]),
        "wgate": np.asarray(inputs["L0_ffn_gate_w"]),
        "wdown": np.asarray(inputs["L0_ffn_down_w"]),
        "wsup": np.asarray(inputs["L0_ffn_shared_up_w"]),
        "wsgate": np.asarray(inputs["L0_ffn_shared_gate_w"]),
        "wsdown": np.asarray(inputs["L0_ffn_shared_down_w"]),
    }
    nc = build_nc(c)
    in_maps = []
    for core in range(8):
        b, h = divmod(core, 2)
        own = x[b, h * 2048:(h + 1) * 2048]
        oth = x[b, (1 - h) * 2048:(2 - h) * 2048]
        m = {"xb": np.ascontiguousarray(np.concatenate([own, oth], axis=0))}
        m.update(wm)
        in_maps.append(m)
    res = run_bass_kernel_spmd(nc, in_maps, core_ids=list(range(8)))
    outp = np.empty((4, 4096, 1024), np.float32)
    for core in range(8):
        b, h = divmod(core, 2)
        outp[b, h * 2048:(h + 1) * 2048] = res.results[core]["out"]
    return outp


# version 25
# speedup vs baseline: 1.0339x; 1.0096x over previous
import os
import numpy as np
import concourse.bass as bass
import concourse.mybir as mybir
from concourse.bass_utils import run_bass_kernel_spmd
from contextlib import ExitStack

F32 = mybir.dt.float32
BF16 = mybir.dt.bfloat16
AF = mybir.ActivationFunctionType
ALU = mybir.AluOpType
AX = mybir.AxisListType


class Res:
    __slots__ = ("writer", "writers", "readers", "name")

    def __init__(self, name=""):
        self.writer = None
        self.writers = []
        self.readers = []
        self.name = name


class Ev:
    __slots__ = ("fn", "eng", "idx", "need_inc", "snap", "blk", "sem", "kind", "seq")


class _Eng:
    def __init__(self, name, sem):
        self.name = name
        self.sem = sem
        self.events = []
        self.know = {}
        self.dknow = {}
        self.waits = {}


class DSem:
    def __init__(self, h, mode):
        self.h = h
        self.mode = mode
        self.count = 0
        self.events = []


class Prog:
    def __init__(self, nc):
        self.nc = nc
        self.blk = 0
        self.engs = {}
        self.csems = {}

    def add_engine(self, name, sem_handle):
        s = {"h": sem_handle, "events": []}
        self.csems[name] = s
        self.engs[name] = _Eng(name, s)

    def flush_block(self):
        self.blk += 1

    def _deps(self, eng, ev, reads, writes, wdisj=()):
        deps = []
        for r in reads:
            if r.writer is not None:
                deps.append(r.writer)
            deps.extend(r.writers)
        for w in writes:
            if w.writer is not None:
                deps.append(w.writer)
            deps.extend(w.writers)
            deps.extend(w.readers)
        for w in wdisj:
            if w.writer is not None:
                deps.append(w.writer)
            deps.extend(w.readers)
        seen = set()
        for d in deps:
            if id(d) in seen or d is ev:
                continue
            seen.add(id(d))
            if d.kind == "c":
                if d.eng is eng and eng.name == "t" and d.eng.name == "t":
                    continue
                if eng.know.get(d.eng.name, -1) >= d.idx:
                    continue
                d.need_inc = True
                eng.waits.setdefault(ev.idx, []).append(("c", d))
                eng.know[d.eng.name] = max(eng.know.get(d.eng.name, -1), d.idx)
                for k, v in d.snap.items():
                    eng.know[k] = max(eng.know.get(k, -1), v)
            else:
                key = "d%d" % id(d.sem)
                if d.sem.mode == "slot":
                    if eng.know.get(key, -1) >= d.seq:
                        continue
                    eng.know[key] = d.seq
                else:
                    if eng.know.get(key, -1) >= 0:
                        continue
                    eng.know[key] = 0
                eng.waits.setdefault(ev.idx, []).append(("d", d))
                for k, v in d.snap.items():
                    eng.know[k] = max(eng.know.get(k, -1), v)

    @staticmethod
    def _prune(lst, ev):
        nr = []
        for r0 in lst:
            if r0.eng is ev.eng and r0.idx <= ev.idx and r0.kind == "c" and ev.kind == "c":
                continue
            if r0.kind == "c" and ev.snap.get(r0.eng.name, -1) >= r0.idx:
                continue
            nr.append(r0)
        return nr

    def _register(self, ev, reads, writes, wdisj=()):
        for r in reads:
            r.readers = self._prune(r.readers, ev)
            r.readers.append(ev)
        for w in writes:
            w.writer = ev
            w.writers = []
            w.readers = []
        for w in wdisj:
            w.writers = self._prune(w.writers, ev)
            w.writers.append(ev)

    def emit(self, ename, fn, reads=(), writes=(), wdisj=()):
        eng = self.engs[ename]
        ev = Ev()
        ev.fn = fn
        ev.eng = eng
        ev.idx = len(eng.events)
        ev.need_inc = False
        ev.blk = self.blk
        ev.kind = "c"
        ev.sem = None
        ev.seq = 0
        ev.snap = {}
        self._deps(eng, ev, reads, writes, wdisj)
        ev.snap = dict(eng.know)
        ev.snap[eng.name] = ev.idx
        self._register(ev, reads, writes, wdisj)
        eng.events.append(ev)
        return ev

    def dma(self, ename, dsem, fn, reads=(), writes=(), wdisj=()):
        eng = self.engs[ename]
        ev = Ev()
        ev.fn = fn
        ev.eng = eng
        ev.idx = len(eng.events)
        ev.need_inc = True
        ev.blk = self.blk
        ev.kind = "d"
        ev.sem = dsem
        dsem.count += 1
        ev.seq = dsem.count
        dsem.events.append(ev)
        ev.snap = {}
        self._deps(eng, ev, reads, writes, wdisj)
        ev.snap = dict(eng.know)
        self._register(ev, reads, writes, wdisj)
        eng.events.append(ev)
        return ev

    def finalize(self, nblocks, final_waits):
        for eng in self.engs.values():
            cnt = 0
            for ev in eng.events:
                if ev.kind == "c" and ev.need_inc:
                    cnt += 1
                    ev.seq = cnt
        out = [{k: [] for k in self.engs} for _ in range(nblocks)]
        for name, eng in self.engs.items():
            hw = {}
            for ev in eng.events:
                ops = out[ev.blk][name]
                best = {}
                for kind, d in eng.waits.get(ev.idx, ()):
                    if kind == "c":
                        sem = d.eng.sem["h"]
                        val = d.seq
                    else:
                        sem = d.sem.h
                        val = 16 * (d.sem.count if d.sem.mode == "sum" else d.seq)
                    key = id(sem)
                    if val > best.get(key, (0, None))[0]:
                        best[key] = (val, sem)
                for key, (val, sem) in best.items():
                    if hw.get(key, 0) >= val:
                        continue
                    hw[key] = val
                    ops.append(lambda e, s=sem, v=val: e.wait_ge(s, v))
                if ev.kind == "d":
                    ops.append(lambda e, f=ev.fn, s=ev.sem.h: f(e).then_inc(s, 16))
                elif ev.need_inc:
                    def _mk(f, s, ev=ev):
                        def go(e):
                            r = f(e)
                            if r is None:
                                raise RuntimeError("event fn returned None: %s blk=%d idx=%d"
                                                   % (ev.eng.name, ev.blk, ev.idx))
                            return r.then_inc(s, 1)
                        return go
                    ops.append(_mk(ev.fn, eng.sem["h"]))
                else:
                    ops.append(ev.fn)
        for ename, dsem in final_waits:
            out[nblocks - 1][ename].append(
                lambda e, s=dsem.h, v=16 * dsem.count: e.wait_ge(s, v))
        return out


class CFG:
    def __init__(self, TOK=2048, SEQ=4096, H=4096, debug=False):
        self.TOK, self.SEQ, self.H = TOK, SEQ, H
        self.debug = debug
        self.D = 1024
        self.NP = 8
        self.ER = 7
        self.ND = self.D // 128
        self.NT = TOK // 128
        self.NTS = SEQ // 128
        self.QSW = min(512, TOK)
        self.NQS = TOK // self.QSW
        self.NKC = SEQ // 128
        self.NKV = SEQ // 512
        self.NHG = H // 512
        self.T4W = min(512, TOK)
        self.NT4 = TOK // self.T4W
        self.CAP = 128 * ((3 * TOK + 2047) // 2048)
        self.NSC = self.CAP // 128
        assert self.NSC <= 3


def build_nc(c: CFG):
    nc = bass.Bass("TRN2", target_bir_lowering=False, detect_race_conditions=True)
    TOK, SEQ, D, H = c.TOK, c.SEQ, c.D, c.H
    ND, NT, QSW, NQS, NKC, NKV, NHG, NT4, T4W = (
        c.ND, c.NT, c.QSW, c.NQS, c.NKC, c.NKV, c.NHG, c.NT4, c.T4W)
    TOKO = SEQ - TOK
    NTO = TOKO // 128

    xb = nc.dram_tensor("xb", [SEQ, D], F32, kind="ExternalInput").ap()
    wqkv = nc.dram_tensor("wqkv", [D, 3 * D], F32, kind="ExternalInput").ap()
    bqkv = nc.dram_tensor("bqkv", [3 * D], F32, kind="ExternalInput").ap()
    wout = nc.dram_tensor("wout", [D, D], F32, kind="ExternalInput").ap()
    bout = nc.dram_tensor("bout", [D], F32, kind="ExternalInput").ap()
    wrt = nc.dram_tensor("wrt", [D, 8], F32, kind="ExternalInput").ap()
    wup = nc.dram_tensor("wup", [c.ER, D, H], F32, kind="ExternalInput").ap()
    wgate = nc.dram_tensor("wgate", [c.ER, D, H], F32, kind="ExternalInput").ap()
    wdown = nc.dram_tensor("wdown", [c.ER, H, D], F32, kind="ExternalInput").ap()
    wsup = nc.dram_tensor("wsup", [D, H], F32, kind="ExternalInput").ap()
    wsgate = nc.dram_tensor("wsgate", [D, H], F32, kind="ExternalInput").ap()
    wsdown = nc.dram_tensor("wsdown", [H, D], F32, kind="ExternalInput").ap()
    out = nc.dram_tensor("out", [TOK, D], F32, kind="ExternalOutput").ap()
    xscr = nc.dram_tensor("xscr", [TOK, D], BF16, kind="Internal").ap()
    if c.debug:
        dbg_xf = nc.dram_tensor("dbg_xf", [128, ND * TOK], F32, kind="ExternalOutput").ap()
        dbg_xof = nc.dram_tensor("dbg_xof", [128, ND * TOKO], F32, kind="ExternalOutput").ap()
        dbg_nrm = nc.dram_tensor("dbg_nrm", [128, 3 * 512], F32, kind="ExternalOutput").ap()

    st0 = ExitStack()
    E = st0.enter_context

    P = Prog(nc)
    for en in "tvsg":
        P.add_engine(en, E(nc.semaphore("sem_" + en)))
    P.add_engine("y", E(nc.semaphore("sem_y")))
    XST = [DSem(E(nc.semaphore(f"xst{i}")), "slot") for i in range(4)]
    TPX = [DSem(E(nc.semaphore(f"tpx{i}")), "slot") for i in range(4)]
    TPV = [DSem(E(nc.semaphore(f"tpv{i}")), "slot") for i in range(2)]
    TPS = DSem(E(nc.semaphore("tps")), "slot")
    WQS = [[DSem(E(nc.semaphore(f"wqs{i}_{r}")), "slot") for r in range(3)] for i in range(2)]
    WUS = [DSem(E(nc.semaphore(f"wus{i}")), "slot") for i in range(2)]
    WGS = [DSem(E(nc.semaphore(f"wgs{i}")), "slot") for i in range(2)]
    WDS = [DSem(E(nc.semaphore(f"wds{i}")), "slot") for i in range(2)]
    MIG = DSem(E(nc.semaphore("mig")), "sum")
    MIY = DSem(E(nc.semaphore("miy")), "sum")
    XRS = DSem(E(nc.semaphore("xrs")), "slot")
    WOS = DSem(E(nc.semaphore("wos")), "slot")
    XSC = DSem(E(nc.semaphore("xsc")), "slot")
    STR = [DSem(E(nc.semaphore(f"str{i}")), "slot") for i in range(2)]
    STO = DSem(E(nc.semaphore("sto")), "sum")
    DBG = DSem(E(nc.semaphore("dbgs")), "sum")

    ident = E(nc.sbuf_tensor("ident", [128, 128], BF16))
    ones_bf = E(nc.sbuf_tensor("ones_bf", [128, 128], BF16))
    ones_fc = E(nc.sbuf_tensor("ones_fc", [128, 1], BF16))
    sel = E(nc.sbuf_tensor("sel", [8, 8 * 128], BF16))
    bq_sb = E(nc.sbuf_tensor("bq_sb", [128, 24], F32))
    bout_row = E(nc.sbuf_tensor("bout_row", [1, D], BF16))
    wrt_hi = E(nc.sbuf_tensor("wrt_hi", [128, ND * 8], BF16))
    wrt_lo = E(nc.sbuf_tensor("wrt_lo", [128, ND * 8], BF16))
    wrt32 = E(nc.sbuf_tensor("wrt32", [128, ND * 8], F32))
    m_f = E(nc.sbuf_tensor("m_f", [8, TOK], BF16))
    stat = [E(nc.sbuf_tensor(f"stat{i}", [128, 96], F32)) for i in range(2)]
    negc = E(nc.sbuf_tensor("negc", [128, 2], F32))
    csc = E(nc.sbuf_tensor("csc", [1, 16], F32))
    cbf = E(nc.sbuf_tensor("cbf", [1, 4], BF16))
    onesf32 = E(nc.sbuf_tensor("onesf32", [1, 128], F32))

    st_xf = ExitStack()
    x_f = st_xf.enter_context(nc.sbuf_tensor("x_f", [128, ND * TOK], BF16))

    R = lambda n: Res(n)
    r_xf = R("x_f")
    r_const = R("const")
    r_bq = R("bq")
    r_boutrow = R("boutrow")
    r_wrthi, r_wrtlo, r_wrt32 = R("wrthi"), R("wrtlo"), R("wrt32")
    r_mf = R("m_f")
    r_stat = [R("stat0"), R("stat1")]
    r_wub = [R("wub0"), R("wub1")]
    r_wgb = [R("wgb0"), R("wgb1")]
    r_wdb = [R("wdb0"), R("wdb1")]
    r_negc = [R("negc0"), R("negc1")]
    r_csc = R("csc")
    r_cbf = R("cbf")

    st_ab = ExitStack()
    O_all = st_ab.enter_context(nc.sbuf_tensor("O_all", [128, c.NP * TOK], BF16))
    r_oall, r_wout = R("O_all"), R("wout")

    VTW = 161
    st_a = ExitStack()
    Ea = st_a.enter_context
    xo_f = Ea(nc.sbuf_tensor("xo_f", [128, max(ND * TOKO, 2)], BF16))
    iota_s = Ea(nc.sbuf_tensor("iota_s", [128, 1280], F32))
    wq_sb = [Ea(nc.sbuf_tensor(f"wq_sb{i}", [128, ND * 384], BF16)) for i in range(2)]
    xstage = [Ea(nc.sbuf_tensor(f"xstage{i}", [128, D], BF16)) for i in range(4)]
    q_sb = [Ea(nc.sbuf_tensor(f"q_sb{i}", [128, TOK], BF16)) for i in range(2)]
    k_sb = [Ea(nc.sbuf_tensor(f"k_sb{i}", [128, SEQ], BF16)) for i in range(2)]
    vt_sb = [Ea(nc.sbuf_tensor(f"vt_sb{i}", [128, NKC * VTW], BF16)) for i in range(2)]
    p_sb = [Ea(nc.sbuf_tensor(f"p_sb{i}", [128, 2 * QSW], BF16)) for i in range(4)]
    sqbuf = Ea(nc.sbuf_tensor("sqbuf", [128, 512], BF16))
    sqw = Ea(nc.sbuf_tensor("sqw", [128, 2048], BF16))
    mstat = Ea(nc.sbuf_tensor("mstat", [128, 24], F32))
    rden = Ea(nc.sbuf_tensor("rden", [128, QSW], BF16))
    bc_sb = [Ea(nc.sbuf_tensor(f"bc_sb{i}", [128, QSW], BF16)) for i in range(2)]
    r_xof = R("xo_f")
    r_wq3 = [[R(f"wq{b}_{r}") for r in range(3)] for b in range(2)]
    r_xst = [R(f"xst{i}") for i in range(4)]
    r_q = [R("q0"), R("q1")]
    r_k = [R("k0"), R("k1")]
    r_vt = [R("vt0"), R("vt1")]
    r_p = [R(f"p{i}") for i in range(4)]
    r_sq, r_mst, r_rden = R("sq"), R("mst"), R("rden")
    r_sqw = R("sqw")
    r_bc = [R("bc0"), R("bc1")]

    pa0 = ExitStack()
    ps_tp = [pa0.enter_context(nc.psum_tensor(f"ps_tp{i}", [128, 1024], BF16))
             for i in range(2)]
    r_pstp = [R("pstp0"), R("pstp1")]

    xfv = x_f[:, :].rearrange("p (dc t) -> p dc t", t=TOK)
    xov = xo_f[:, :].rearrange("p (dc t) -> p dc t", t=TOKO) if TOKO else None

    P.emit("g", lambda e: e.iota(iota_s[:, 0:128], [[1, 128]], channel_multiplier=-1,
                                 allow_small_or_imprecise_dtypes=True), writes=[r_const])
    P.emit("g", lambda e: e.iota(
        iota_s[:, 128:128 + 8 * 128].rearrange("p (a b) -> p a b", b=128),
        [[1, 8], [0, 128]], channel_multiplier=-1,
        allow_small_or_imprecise_dtypes=True), reads=[r_const], writes=[r_const])
    P.emit("v", lambda e: e.tensor_single_scalar(ident[:, :], iota_s[:, 0:128], 0.0,
                                                 ALU.is_equal), reads=[r_const], writes=[r_const])
    P.emit("v", lambda e: e.tensor_single_scalar(sel[0:8, :], iota_s[0:8, 128:128 + 8 * 128],
                                                 0.0, ALU.is_equal), reads=[r_const], writes=[r_const])
    P.emit("v", lambda e: e.memset(ones_bf[:, :], 1.0), reads=[r_const], writes=[r_const])
    P.emit("v", lambda e: e.memset(ones_fc[:, :], 1.0), reads=[r_const], writes=[r_const])
    P.emit("v", lambda e: e.memset(onesf32[0:1, :], 1.0), reads=[r_const], writes=[r_const])
    for par in range(2):
        vtr = vt_sb[par][:, :].rearrange("p (kc w) -> p kc w", w=VTW)
        P.emit("v", lambda e, par=par: e.memset(vt_sb[par][:, :], 0.0), writes=[r_vt[par]])
        P.emit("v", lambda e, vtr=vtr, par=par: e.memset(vtr[:, :, 64:66], 1.0),
               reads=[r_vt[par]], writes=[r_vt[par]])

    P.dma("g", MIG, lambda e: e.dma_start(bq_sb[:, :].rearrange("p (r j) -> p r j", r=3),
                                          bqkv.rearrange("(r j p) -> p r j", p=128, r=3),
                                          allow_slow_non_contiguous=True),
          writes=[r_bq])
    P.dma("g", MIG, lambda e: e.dma_start(bout_row[0:1, :],
                                          bout.rearrange("(a d) -> a d", a=1)),
          writes=[r_boutrow])
    P.dma("g", MIG, lambda e: e.dma_start(wrt_hi[:, :].rearrange("p (dc j) -> p dc j", j=8),
                                          wrt.rearrange("(dc p) j -> p dc j", p=128)),
          writes=[r_wrthi])
    P.dma("y", MIY, lambda e: e.dma_start(wrt32[:, :].rearrange("p (dc j) -> p dc j", j=8),
                                          wrt.rearrange("(dc p) j -> p dc j", p=128)),
          writes=[r_wrt32])
    P.emit("v", lambda e: e.tensor_scalar_mul(bq_sb[:, 0:8], bq_sb[:, 0:8], 0.25),
           reads=[r_bq], writes=[r_bq])
    P.emit("v", lambda e: e.tensor_scalar_mul(bq_sb[:, 8:16], bq_sb[:, 8:16], 0.5),
           reads=[r_bq], writes=[r_bq])
    P.emit("v", lambda e: e.tensor_sub(wrt_lo[:, :], wrt32[:, :], wrt_hi[:, :]),
           reads=[r_wrt32, r_wrthi], writes=[r_wrtlo])

    def wq_load(p):
        buf = p % 2
        for r in range(3):
            P.dma("g", WQS[buf][r], lambda e, p=p, buf=buf, r=r: e.dma_start(
                wq_sb[buf][:, :].rearrange("p (dc r cc) -> p dc r cc", r=3, cc=128)[:, :, r, :],
                bass.AP(wqkv.tensor, r * 1024 + p * 128,
                        [[3072, 128], [128 * 3072, ND], [1, 128]])),
                writes=[r_wq3[buf][r]])

    wq_load(0)
    if c.NP > 1:
        wq_load(1)

    def emit_xtile(j):
        for dc in range(ND):
            P.emit("t", lambda e, j=j, dc=dc: e.transpose(
                ps_tp[j % 2][:, dc * 128:(dc + 1) * 128],
                xstage[j % 4][:, dc * 128:(dc + 1) * 128], ident[:, :]),
                reads=[r_xst[j % 4], r_const], writes=[r_pstp[j % 2]])
        if j < NT:
            P.emit("v", lambda e, j=j: e.tensor_copy(
                xfv[:, :, j * 128:(j + 1) * 128],
                ps_tp[j % 2][:, :].rearrange("p (dc t) -> p dc t", t=128)),
                reads=[r_pstp[j % 2]], writes=[r_xf])
        else:
            P.emit("v", lambda e, j=j: e.tensor_copy(
                xov[:, :, (j - NT) * 128:(j - NT + 1) * 128],
                ps_tp[j % 2][:, :].rearrange("p (dc t) -> p dc t", t=128)),
                reads=[r_pstp[j % 2]], writes=[r_xof])

    for i in range(c.NTS):
        P.dma("g", XST[i % 4], lambda e, i=i: e.dma_start(
            xstage[i % 4][:, :], xb[i * 128:(i + 1) * 128, :]), writes=[r_xst[i % 4]])
        if i >= 2:
            emit_xtile(i - 2)
    for j in range(max(0, c.NTS - 2), c.NTS):
        emit_xtile(j)

    if c.debug:
        P.dma("g", DBG, lambda e: e.dma_start(dbg_xf, x_f[:, :]), reads=[r_xf])
        P.dma("g", DBG, lambda e: e.dma_start(dbg_xof, xo_f[:, :]), reads=[r_xof])

    pa0.close()
    pa = ExitStack()
    Eap = pa.enter_context
    ps_pool = [Eap(nc.psum_tensor(f"ps_pool{i}", [128, 1024], F32)) for i in range(3)]
    ps_av = [Eap(nc.psum_tensor(f"ps_av{i}", [128, 512], F32)) for i in range(2)]
    r_pool = [R(f"pool{i}") for i in range(3)]
    r_psav = [R("psav0"), R("psav1")]
    pctr = [0]
    pend_slots = {}

    def slot():
        for _ in range(4):
            i = pctr[0] % 3
            pctr[0] += 1
            if pend_slots.get(i, 0) == 0:
                return i
        raise RuntimeError("no free psum slot")

    def slot_S():
        i = slot()
        pend_slots[i] = pend_slots.get(i, 0) + 1
        return i
    P.emit("t", lambda e: e.matmul(ps_pool[0][0:1, 0:1], ones_fc[0:1, 0:1], ones_fc[0:1, 0:1],
                                   start=True, stop=True),
           reads=[r_const], writes=r_pool + r_pstp + r_psav)

    vtview = [bass.AP(vt_sb[i], 0, [[NKC * VTW, 128], [VTW, 4], [97, 2], [1, 64]])
              for i in range(2)]

    def mk_proj_thunks(p):
        par = p % 2
        buf = p % 2
        th = []

        def proj_group(kind, j0, vsl=None):
            r = {"q": 0, "k": 1, "v": 2}[kind]
            lim = TOK if kind == "q" else SEQ

            def go():
                si_ = slot()
                prj = ps_pool[si_]
                r_prj = r_pool[si_]
                if kind == "v":
                    for k2 in range(2):
                        for kq in range(4):
                            t0 = (j0 + k2) * 512 + kq * 128
                            for dc in range(ND):
                                def mmv(e, dc=dc, k2=k2, kq=kq, t0=t0, prj=prj):
                                    lh = (xfv[:, dc, t0:t0 + 128] if t0 < TOK
                                          else xov[:, dc, t0 - TOK:t0 - TOK + 128])
                                    return e.matmul(
                                        prj[:, k2 * 512 + kq * 128:k2 * 512 + (kq + 1) * 128],
                                        lh, wq_sb[buf][:, dc * 384 + 256:dc * 384 + 384],
                                        start=(dc == 0), stop=(dc == ND - 1))
                                rsrc = r_xf if t0 < TOK else r_xof
                                P.emit("t", mmv, reads=[r_wq3[buf][r], rsrc], writes=[r_prj])
                    for k2 in range(2):
                        kcb = (j0 + k2) * 4
                        P.emit("v", lambda e, kcb=kcb, k2=k2, prj=prj, par=par: e.tensor_copy(
                            bass.AP(vt_sb[par], kcb * VTW,
                                    [[NKC * VTW, 128], [VTW, 4], [97, 2], [1, 64]]),
                            prj[:, k2 * 512:(k2 + 1) * 512].rearrange(
                                "p (kc h w) -> p kc h w", h=2, w=64)),
                            reads=[r_prj], wdisj=[r_vt[par]])
                    return
                for dc in range(ND):
                    for k2 in range(2):
                        if (j0 + k2) * 512 >= lim:
                            continue
                        def mm(e, dc=dc, k2=k2, j0=j0, r=r, prj=prj):
                            t0 = (j0 + k2) * 512
                            if kind == "q":
                                rhs = xfv[:, dc, t0:t0 + 512]
                            else:
                                rhs = (xfv[:, dc, t0:t0 + 512] if t0 < TOK
                                       else xov[:, dc, t0 - TOK:t0 - TOK + 512])
                            return e.matmul(
                                prj[:, k2 * 512:(k2 + 1) * 512],
                                wq_sb[buf][:, dc * 384 + r * 128:dc * 384 + (r + 1) * 128],
                                rhs, start=(dc == 0), stop=(dc == ND - 1))
                        rsrc = r_xf if (kind == "q" or (j0 + k2) * 512 < TOK) else r_xof
                        P.emit("t", mm, reads=[r_wq3[buf][r], rsrc], writes=[r_prj])
                for k2 in range(2):
                    if (j0 + k2) * 512 >= lim:
                        continue
                    if kind == "q":
                        P.emit("v", lambda e, k2=k2, j0=j0, prj=prj: e.tensor_scalar(
                            q_sb[par][:, (j0 + k2) * 512:(j0 + k2 + 1) * 512],
                            prj[:, k2 * 512:(k2 + 1) * 512], 0.25, bq_sb[:, p:p + 1],
                            ALU.mult, ALU.add),
                            reads=[r_prj, r_bq], writes=[r_q[par]])
                    else:
                        P.emit("v", lambda e, k2=k2, j0=j0, prj=prj: e.tensor_scalar(
                            k_sb[par][:, (j0 + k2) * 512:(j0 + k2 + 1) * 512],
                            prj[:, k2 * 512:(k2 + 1) * 512], 0.5, bq_sb[:, 8 + p:9 + p],
                            ALU.mult, ALU.add),
                            reads=[r_prj, r_bq], writes=[r_k[par]])
            return go

        for j0 in range(0, NQS, 2):
            th.append(proj_group("q", j0))
        for j0 in range(0, NKV, 2):
            th.append(proj_group("k", j0))

        def cchain():
            nq = max(1, TOK // 512)
            nk = SEQ // 512
            nall = nq + nk
            for j in range(nall):
                src = q_sb[par] if j < nq else k_sb[par]
                rr = r_q[par] if j < nq else r_k[par]
                jj = j if j < nq else j - nq
                P.emit("v", lambda e, src=src, jj=jj, j=j: e.tensor_mul(
                    sqw[:, (j % 4) * 512:(j % 4) * 512 + 512],
                    src[:, jj * 512:(jj + 1) * 512], src[:, jj * 512:(jj + 1) * 512]),
                    reads=[rr], writes=[r_sqw])
                if j % 4 == 3 or j == nall - 1:
                    base = j - (j % 4)
                    si_ = slot()
                    prj = ps_pool[si_]
                    for m in range(j % 4 + 1):
                        P.emit("t", lambda e, m=m, prj=prj: e.matmul(
                            prj[0:1, m * 256:m * 256 + 256],
                            ones_fc[:, 0:1],
                            sqw[:, m * 512:m * 512 + 512].rearrange(
                                "p (a b) -> p a b", a=2)[:, 0, :]
                            if False else sqw[:, m * 512:m * 512 + 256],
                            start=True, stop=True),
                            reads=[r_sqw, r_const], writes=[r_pool[si_]]) if False else None
                        P.emit("t", lambda e, m=m, prj=prj: e.matmul(
                            prj[0:1, m * 512:m * 512 + 512], ones_fc[:, 0:1],
                            sqw[:, m * 512:m * 512 + 512], start=True, stop=True),
                            reads=[r_sqw, r_const], writes=[r_pool[si_]]) if m < 2 else None
                        if m >= 2:
                            si2 = slot()
                            prj2 = ps_pool[si2]
                            P.emit("t", lambda e, m=m, prj2=prj2: e.matmul(
                                prj2[0:1, (m - 2) * 512:(m - 2) * 512 + 512], ones_fc[:, 0:1],
                                sqw[:, m * 512:m * 512 + 512], start=True, stop=True),
                                reads=[r_sqw, r_const], writes=[r_pool[si2]])
                            P.emit("v", lambda e, base=base, m=m, prj2=prj2: e.reduce_max(
                                csc[0:1, base + m:base + m + 1],
                                prj2[0:1, (m - 2) * 512:(m - 2) * 512 + 512], axis=AX.X),
                                reads=[r_pool[si2]], writes=[r_csc])
                        else:
                            P.emit("v", lambda e, base=base, m=m, prj=prj: e.reduce_max(
                                csc[0:1, base + m:base + m + 1],
                                prj[0:1, m * 512:m * 512 + 512], axis=AX.X),
                                reads=[r_pool[si_]], writes=[r_csc])
            P.emit("v", lambda e, nq=nq: e.reduce_max(csc[0:1, 12:13], csc[0:1, 0:nq],
                                                      axis=AX.X), reads=[r_csc], writes=[r_csc])
            P.emit("v", lambda e, nq=nq, nk=nk: e.reduce_max(csc[0:1, 13:14],
                                                             csc[0:1, nq:nq + nk], axis=AX.X),
                   reads=[r_csc], writes=[r_csc])
            P.emit("v", lambda e: e.tensor_add(csc[0:1, 14:15], csc[0:1, 12:13],
                                               csc[0:1, 13:14]), reads=[r_csc], writes=[r_csc])
            P.emit("v", lambda e: e.tensor_scalar_mul(csc[0:1, 15:16], csc[0:1, 14:15], -0.51),
                   reads=[r_csc], writes=[r_csc])
            P.emit("v", lambda e: e.tensor_copy(cbf[0:1, 0:1], csc[0:1, 15:16]),
                   reads=[r_csc], writes=[r_cbf])
            si_ = slot()
            prj = ps_pool[si_]
            P.emit("t", lambda e, prj=prj: e.matmul(prj[:, 1022:1023], ones_bf[0:1, :],
                                                    cbf[0:1, 0:1], start=True, stop=True),
                   reads=[r_cbf, r_const], writes=[r_pool[si_]])
            P.emit("s", lambda e, par=par, prj=prj: e.copy(negc[:, par:par + 1],
                                                           prj[:, 1022:1023]),
                   reads=[r_pool[si_]], writes=[r_negc[par]])
        th.append(cchain)
        for idx, j0 in enumerate(range(0, NKV, 2)):
            th.append(proj_group("v", j0, vsl=0))
        if p + 2 < c.NP:
            th.append(lambda: wq_load(p + 2))
        return th

    for t_ in mk_proj_thunks(0):
        t_()

    pend = []
    ilvd = [0]
    norm_pend = [None]
    for p in range(c.NP):
        par = p % 2
        if p + 1 < c.NP:
            pend = mk_proj_thunks(p + 1)
            ilvd[0] = 0

        smap = {}

        def emit_S(qs, kc, p=p, par=par):
            si_ = slot_S()
            smap[kc] = si_
            sab = ps_pool[si_]
            P.emit("t", lambda e, kc=kc, qs=qs, sab=sab: e.matmul(
                sab[:, 0:QSW], k_sb[par][0:64, kc * 128:(kc + 1) * 128],
                q_sb[par][0:64, qs * QSW:(qs + 1) * QSW], start=True, stop=True,
                tile_position=(0, 0)),
                reads=[r_k[par], r_q[par]], writes=[r_pool[si_]])
            P.emit("t", lambda e, kc=kc, qs=qs, sab=sab: e.matmul(
                sab[:, QSW:2 * QSW], k_sb[par][64:128, kc * 128:(kc + 1) * 128],
                q_sb[par][64:128, qs * QSW:(qs + 1) * QSW], start=True, stop=True,
                tile_position=(64, 0)),
                reads=[r_k[par], r_q[par]], writes=[r_pool[si_]])

        def emit_exp(qs, kc, p=p, par=par):
            sp = kc % 4
            si_ = smap.pop(kc)
            pend_slots[si_] -= 1
            sab = ps_pool[si_]
            P.emit("s", lambda e, sab=sab, sp=sp: e.activation(
                p_sb[sp][:, 0:2 * QSW], sab[:, 0:2 * QSW], AF.Exp,
                bias=negc[:, par:par + 1], scale=1.0),
                reads=[r_pool[si_], r_negc[par]], writes=[r_p[sp]])

        def emit_AV(qs, kc, p=p, par=par):
            sp = kc % 4
            P.emit("t", lambda e, kc=kc, sp=sp: e.matmul(
                ps_av[0][:, 0:QSW], vt_sb[par][:, kc * VTW:kc * VTW + 128],
                p_sb[sp][:, 0:QSW], start=(kc == 0), stop=(kc == NKC - 1)),
                reads=[r_p[sp], r_vt[par]], writes=[r_psav[0]])
            P.emit("t", lambda e, kc=kc, sp=sp: e.matmul(
                ps_av[1][:, 0:QSW], vt_sb[par][:, kc * VTW + 33:kc * VTW + 161],
                p_sb[sp][:, QSW:2 * QSW], start=(kc == 0), stop=(kc == NKC - 1)),
                reads=[r_p[sp], r_vt[par]], writes=[r_psav[1]])

        def emit_norm(p, qs):
            P.emit("s", lambda e: e.copy(rden[64:65, 0:QSW], ps_av[0][64:65, 0:QSW]),
                   reads=[r_psav[0]], writes=[r_rden])
            P.emit("s", lambda e: e.copy(rden[32:33, 0:QSW], ps_av[1][32:33, 0:QSW]),
                   reads=[r_psav[1], r_rden], writes=[r_rden])
            si_ = slot()
            prj = ps_pool[si_]
            r_prj = r_pool[si_]
            P.emit("t", lambda e, prj=prj: e.matmul(prj[:, 0:QSW], ones_bf[64:65, :],
                                                    rden[64:65, 0:QSW], start=True, stop=True),
                   reads=[r_rden, r_const], writes=[r_prj])
            P.emit("t", lambda e, prj=prj: e.matmul(prj[:, 512:512 + QSW], ones_bf[32:33, :],
                                                    rden[32:33, 0:QSW], start=True, stop=True),
                   reads=[r_rden, r_const], writes=[r_prj])

            def _recipA(e, prj=prj):
                with nc.allow_low_precision(reason="bf16 softmax denom (honest-bf16 path)"):
                    return e.reciprocal(bc_sb[0][:, 0:QSW], prj[:, 0:QSW])

            def _recipB(e, prj=prj):
                with nc.allow_low_precision(reason="bf16 softmax denom (honest-bf16 path)"):
                    return e.reciprocal(bc_sb[1][:, 0:QSW], prj[:, 512:512 + QSW])
            P.emit("v", _recipA, reads=[r_prj], writes=[r_bc[0]])
            P.emit("v", _recipB, reads=[r_prj], writes=[r_bc[1]])
            P.emit("v", lambda e, p=p, qs=qs: e.tensor_mul(
                O_all[0:64, p * TOK + qs * QSW:p * TOK + (qs + 1) * QSW],
                ps_av[0][0:64, 0:QSW], bc_sb[0][0:64, 0:QSW]),
                reads=[r_psav[0], r_bc[0]], writes=[r_oall])
            P.emit("v", lambda e, p=p, qs=qs: e.tensor_mul(
                O_all[64:128, p * TOK + qs * QSW:p * TOK + (qs + 1) * QSW],
                ps_av[1][64:128, 0:QSW], bc_sb[1][64:128, 0:QSW]),
                reads=[r_psav[1], r_bc[1]], writes=[r_oall])
            if c.debug and p == 0 and qs == 1:
                P.emit("v", lambda e: e.tensor_copy(x_f[:, 0:512], ps_av[0][:, 0:512]),
                       reads=[r_psav[0]], writes=[r_xf]) if False else None
                P.dma("g", DBG, lambda e: e.dma_start(dbg_nrm[:, 0:512], bc_sb[0][:, 0:512]),
                      reads=[r_bc[0]])
                P.dma("g", DBG, lambda e: e.dma_start(dbg_nrm[64:65, 512:1024],
                                                      rden[64:65, 0:512]), reads=[r_rden])
                P.dma("g", DBG, lambda e: e.dma_start(dbg_nrm[32:33, 512:1024],
                                                      rden[32:33, 0:512]), reads=[r_rden])
                P.dma("g", DBG, lambda e: e.dma_start(dbg_nrm[:, 1024:1026], negc[:, 0:2]),
                      reads=[r_negc[0], r_negc[1]])

        n_iters = NQS * (NKC // 2 + 3)
        pop_every = max(1, n_iters // (len(pend) + 5)) if pend else 1
        itc = 0
        for qs in range(NQS):
            for j in range(0, NKC + 6, 2):
                if 2 <= j <= NKC:
                    emit_exp(qs, j - 2)
                    emit_exp(qs, j - 1)
                if j < NKC:
                    emit_S(qs, j)
                    emit_S(qs, j + 1)
                if j == 0 and norm_pend[0] is not None:
                    emit_norm(*norm_pend[0])
                    norm_pend[0] = None
                if 4 <= j - 0 and j - 4 < NKC:
                    emit_AV(qs, j - 4)
                    emit_AV(qs, j - 3)
                if pend and itc % pop_every == 0 and os.environ.get("NOILV") is None:
                    pend.pop(0)()
                itc += 1
            norm_pend[0] = (p, qs)
        while pend:
            pend.pop(0)()
    if norm_pend[0] is not None:
        emit_norm(*norm_pend[0])
        norm_pend[0] = None

    P.flush_block()
    st_a.close()
    pa.close()

    st_bend = ExitStack()
    Ebe = st_bend.enter_context
    x_res = Ebe(nc.sbuf_tensor("x_res", [128, NT * 1024], F32, side="right"))
    r_xres = [R(f"xres{t}") for t in range(NT)]
    r_xscr = R("xscr")
    st_b = ExitStack()
    Eb = st_b.enter_context
    wout_sb = Eb(nc.sbuf_tensor("wout_sb", [128, ND * 1024], BF16))
    b_bc = Eb(nc.sbuf_tensor("b_bc", [128, D], BF16))
    bqv = Eb(nc.sbuf_tensor("bqv", [128, 8], BF16))
    bvw_row = Eb(nc.sbuf_tensor("bvw_row", [1, D], BF16))
    r_bqv, r_bvw = R("bqv"), R("bvw")
    xhi_t = [Eb(nc.sbuf_tensor("xhi_t0", [128, D], BF16))]
    xlo_t = [Eb(nc.sbuf_tensor("xlo_t0", [128, D], BF16))]
    xlo_f = [Eb(nc.sbuf_tensor("xlo_f0", [128, D], BF16))]
    xsq = Eb(nc.sbuf_tensor("xsq", [128, 8 * D], BF16))
    exq = Eb(nc.sbuf_tensor("exq", [128, 64], F32))
    eqb = Eb(nc.sbuf_tensor("eqb", [128, 8], F32))
    mtok = Eb(nc.sbuf_tensor("mtok", [128, 8], BF16))
    r_bbc = R("b_bc")
    r_xhi = [R("xhi0")]
    r_xlo = [R("xlo0")]
    r_xlof = [R("xlof0")]
    r_xsq, r_exq, r_eqb, r_mtok = R("xsq"), R("exq"), R("eqb"), R("mtok")

    pb = ExitStack()
    Ebp = pb.enter_context
    ps_u = [Ebp(nc.psum_tensor(f"ps_u{i}", [128, 1024], F32)) for i in range(2)]
    ps_lg = Ebp(nc.psum_tensor("ps_lg", [128, 512], F32))
    ps_tb = [Ebp(nc.psum_tensor(f"ps_tb{i}", [128, 1024], BF16)) for i in range(2)]
    r_psu = [R("psu0"), R("psu1")]
    r_pslg = R("pslg")
    r_pstb = [R("pstb0"), R("pstb1")]

    a_sbuf_res = ([r_xof, r_sq, r_mst, r_rden] + r_q + r_k + r_vt +
                  r_wq3[0] + r_wq3[1] + r_xst + r_p + r_bc)
    P.dma("y", XRS, lambda e: e.dma_start(
        bass.AP(x_res, 0, [[NT * 1024, 128], [1024, NT], [1, 1024]]),
        bass.AP(xb.tensor, 0, [[1024, 128], [128 * 1024, NT], [1, 1024]])),
        writes=r_xres + a_sbuf_res)
    P.dma("g", WOS, lambda e: e.dma_start(
        wout_sb[:, :].rearrange("p (dc j) -> p dc j", j=1024),
        wout.rearrange("(dc p) j -> p dc j", p=128)),
        writes=[r_wout] + a_sbuf_res)

    a_psum_res = r_pool + r_psav
    P.emit("v", lambda e: e.tensor_copy(bqv[:, 0:8], bq_sb[:, 16:24]),
           reads=[r_bq], writes=[r_bqv])
    for half in range(2):
        P.emit("t", lambda e, half=half: e.matmul(
            ps_u[0][:, half * 512:(half + 1) * 512], ones_bf[0:1, :],
            bout_row[0:1, half * 512:(half + 1) * 512], start=True, stop=False),
            reads=[r_boutrow, r_const], writes=[r_psu[0]] + (a_psum_res if half == 0 else []))
    for pr in range(c.NP):
        for half in range(2):
            P.emit("t", lambda e, pr=pr, half=half: e.matmul(
                ps_u[1][0:1, half * 512:(half + 1) * 512], bqv[:, pr:pr + 1],
                wout_sb[:, pr * 1024 + half * 512:pr * 1024 + (half + 1) * 512],
                start=(pr == 0), stop=(pr == c.NP - 1)),
                reads=[r_bqv, r_wout], writes=[r_psu[1]])
    P.emit("s", lambda e: e.copy(bvw_row[0:1, :], ps_u[1][0:1, :]),
           reads=[r_psu[1]], writes=[r_bvw])
    for half in range(2):
        P.emit("t", lambda e, half=half: e.matmul(
            ps_u[0][:, half * 512:(half + 1) * 512], ones_bf[0:1, :],
            bvw_row[0:1, half * 512:(half + 1) * 512], start=False, stop=True),
            reads=[r_bvw, r_const], writes=[r_psu[0]])
    P.emit("s", lambda e: e.copy(b_bc[:, :], ps_u[0][:, :]), reads=[r_psu[0]], writes=[r_bbc])
    for tt in range(NT):
        P.emit("v", lambda e, tt=tt: e.tensor_add(
            x_res[:, tt * 1024:(tt + 1) * 1024], x_res[:, tt * 1024:(tt + 1) * 1024],
            b_bc[:, :]), reads=[r_bbc, r_xres[tt]], writes=[r_xres[tt]])

    GB = min(8, NT)
    for g0 in range(0, NT, GB):
        gi = g0 // GB
        st = gi % 2
        rs = r_stat[st]
        sb_ = stat[st]
        for g in range(GB):
            tt = g0 + g
            for half in range(2):
                for pr in range(c.NP):
                    P.emit("t", lambda e, pr=pr, half=half, tt=tt: e.matmul(
                        ps_u[0][:, half * 512:(half + 1) * 512],
                        O_all[0:64, pr * TOK + tt * 128:pr * TOK + (tt + 1) * 128],
                        wout_sb[0:64, pr * 1024 + half * 512:pr * 1024 + (half + 1) * 512],
                        start=(pr == 0), stop=(pr == c.NP - 1), tile_position=(0, 0)),
                        reads=[r_oall, r_wout], writes=[r_psu[0]])
                    P.emit("t", lambda e, pr=pr, half=half, tt=tt: e.matmul(
                        ps_u[1][:, half * 512:(half + 1) * 512],
                        O_all[64:128, pr * TOK + tt * 128:pr * TOK + (tt + 1) * 128],
                        wout_sb[64:128, pr * 1024 + half * 512:pr * 1024 + (half + 1) * 512],
                        start=(pr == 0), stop=(pr == c.NP - 1), tile_position=(64, 0)),
                        reads=[r_oall, r_wout], writes=[r_psu[1]])
            P.emit("v", lambda e, tt=tt: e.tensor_add(
                x_res[:, tt * 1024:(tt + 1) * 1024], x_res[:, tt * 1024:(tt + 1) * 1024],
                ps_u[0][:, :]), reads=[r_psu[0], r_xres[tt]], writes=[r_xres[tt]])
            P.emit("v", lambda e, tt=tt: e.tensor_add(
                x_res[:, tt * 1024:(tt + 1) * 1024], x_res[:, tt * 1024:(tt + 1) * 1024],
                ps_u[1][:, :]), reads=[r_psu[1], r_xres[tt]], writes=[r_xres[tt]])
            P.emit("v", lambda e, tt=tt, g=g, sb_=sb_: e.reduce_sum(
                sb_[:, g:g + 1], x_res[:, tt * 1024:(tt + 1) * 1024], axis=AX.X),
                reads=[r_xres[tt]], writes=[rs])
            P.emit("v", lambda e, tt=tt, g=g: e.tensor_mul(
                xsq[:, g * D:(g + 1) * D], x_res[:, tt * 1024:(tt + 1) * 1024],
                x_res[:, tt * 1024:(tt + 1) * 1024]), reads=[r_xres[tt]], writes=[r_xsq])
            P.emit("v", lambda e, g=g, sb_=sb_: e.reduce_sum(
                sb_[:, 8 + g:9 + g], xsq[:, g * D:(g + 1) * D], axis=AX.X),
                reads=[r_xsq], writes=[rs])
        P.emit("v", lambda e, sb_=sb_: e.tensor_scalar_mul(sb_[:, 16:16 + GB], sb_[:, 0:GB],
                                                           1.0 / D), reads=[rs], writes=[rs])
        P.emit("v", lambda e, sb_=sb_: e.tensor_scalar_mul(sb_[:, 24:24 + GB], sb_[:, 8:8 + GB],
                                                           1.0 / D), reads=[rs], writes=[rs])
        P.emit("v", lambda e, sb_=sb_: e.tensor_mul(sb_[:, 32:32 + GB], sb_[:, 16:16 + GB],
                                                    sb_[:, 16:16 + GB]), reads=[rs], writes=[rs])
        P.emit("v", lambda e, sb_=sb_: e.tensor_sub(sb_[:, 40:40 + GB], sb_[:, 24:24 + GB],
                                                    sb_[:, 32:32 + GB]), reads=[rs], writes=[rs])
        P.emit("v", lambda e, sb_=sb_: e.tensor_scalar_add(sb_[:, 40:40 + GB],
                                                           sb_[:, 40:40 + GB], 1e-5),
               reads=[rs], writes=[rs])
        P.emit("s", lambda e, sb_=sb_: e.activation(sb_[:, 48:48 + GB], sb_[:, 40:40 + GB],
                                                    AF.Ln, bias=0.0, scale=1.0),
               reads=[rs], writes=[rs])
        P.emit("s", lambda e, sb_=sb_: e.activation(sb_[:, 56:56 + GB], sb_[:, 48:48 + GB],
                                                    AF.Exp, bias=0.0, scale=-0.5),
               reads=[rs], writes=[rs])
        for g in range(GB):
            tt = g0 + g
            P.emit("v", lambda e, tt=tt, g=g, sb_=sb_: e.tensor_scalar(
                x_res[:, tt * 1024:(tt + 1) * 1024], x_res[:, tt * 1024:(tt + 1) * 1024],
                sb_[:, 16 + g:17 + g], sb_[:, 56 + g:57 + g], ALU.subtract, ALU.mult),
                reads=[rs, r_xres[tt]], writes=[r_xres[tt]])
            P.emit("s", lambda e, tt=tt: e.copy(xhi_t[0][:, :],
                                                x_res[:, tt * 1024:(tt + 1) * 1024]),
                   reads=[r_xres[tt]], writes=[r_xhi[0]])
            P.dma("y", XSC, lambda e, tt=tt: e.dma_start(
                xscr[tt * 128:(tt + 1) * 128, :], xhi_t[0][:, :]),
                reads=[r_xhi[0]], writes=[r_xscr])
            P.emit("v", lambda e, tt=tt: e.tensor_sub(xlo_t[0][:, :],
                                                      x_res[:, tt * 1024:(tt + 1) * 1024],
                                                      xhi_t[0][:, :]),
                   reads=[r_xres[tt], r_xhi[0]], writes=[r_xlo[0]])
            for dc in range(ND):
                P.emit("t", lambda e, dc=dc, st=st: e.transpose(
                    ps_tb[st][:, dc * 128:(dc + 1) * 128], xhi_t[0][:, dc * 128:(dc + 1) * 128],
                    ident[:, :]), reads=[r_xhi[0], r_const], writes=[r_pstb[st]])
            P.emit("s", lambda e, tt=tt, st=st: e.copy(
                xfv[:, :, tt * 128:(tt + 1) * 128],
                ps_tb[st][:, :].rearrange("p (dc t) -> p dc t", t=128)),
                reads=[r_pstb[st]], writes=[r_xf])
            for dc in range(ND):
                P.emit("t", lambda e, dc=dc, st=st: e.transpose(
                    ps_tb[st][:, dc * 128:(dc + 1) * 128], xlo_t[0][:, dc * 128:(dc + 1) * 128],
                    ident[:, :]), reads=[r_xlo[0], r_const], writes=[r_pstb[st]])
            P.emit("s", lambda e, st=st: e.copy(xlo_f[0][:, :], ps_tb[st][:, :]),
                   reads=[r_pstb[st]], writes=[r_xlof[0]])
            n_mm = 3 * ND
            i_mm = 0
            for dc in range(ND):
                for (xs, ws, rws) in ((0, wrt_hi, r_wrthi), (0, wrt_lo, r_wrtlo),
                                      (1, wrt_hi, r_wrthi)):
                    i_mm += 1
                    P.emit("t", lambda e, dc=dc, xs=xs, ws=ws, tt=tt, g=g,
                           first=(i_mm == 1), last=(i_mm == n_mm): e.matmul(
                        ps_lg[:, g * 8:g * 8 + 8],
                        (xfv[:, dc, tt * 128:(tt + 1) * 128] if xs == 0
                         else xlo_f[0][:, dc * 128:(dc + 1) * 128]),
                        ws[:, dc * 8:(dc + 1) * 8], start=first, stop=last),
                        reads=[r_xf if xs == 0 else r_xlof[0], rws], writes=[r_pslg])
        P.emit("v", lambda e, sb_=sb_: e.reduce_max(
            sb_[:, 64:64 + GB],
            ps_lg[:, 0:GB * 8].rearrange("p (g c) -> p g c", c=8), axis=AX.X),
            reads=[r_pslg], writes=[rs])
        P.emit("v", lambda e, sb_=sb_: e.tensor_scalar_mul(sb_[:, 72:72 + GB],
                                                           sb_[:, 64:64 + GB], -1.0),
               reads=[rs], writes=[rs])
        for g in range(GB):
            P.emit("s", lambda e, g=g, sb_=sb_: e.activation(
                exq[:, g * 8:g * 8 + 8], ps_lg[:, g * 8:g * 8 + 8], AF.Exp,
                bias=sb_[:, 72 + g:73 + g], scale=1.0),
                reads=[r_pslg, rs], writes=[r_exq])
        P.emit("v", lambda e, sb_=sb_: e.reduce_sum(
            sb_[:, 80:80 + GB],
            exq[:, 0:GB * 8].rearrange("p (g c) -> p g c", c=8), axis=AX.X),
            reads=[r_exq], writes=[rs])
        P.emit("v", lambda e, sb_=sb_: e.reciprocal(sb_[:, 88:88 + GB], sb_[:, 80:80 + GB]),
               reads=[rs], writes=[rs])
        for g in range(GB):
            tt = g0 + g
            P.emit("v", lambda e, g=g, sb_=sb_: e.tensor_scalar(
                eqb[:, 0:8], ps_lg[:, g * 8:g * 8 + 8], sb_[:, 64 + g:65 + g],
                sb_[:, 88 + g:89 + g], ALU.is_equal, ALU.mult),
                reads=[r_pslg, rs], writes=[r_eqb])
            P.emit("v", lambda e: e.tensor_scalar_mul(mtok[:, 0:8], eqb[:, 0:8], 2.0),
                   reads=[r_eqb], writes=[r_mtok])
            P.emit("t", lambda e, st=st: e.transpose(ps_tb[st][0:8, 512:640], mtok[:, 0:8],
                                                     ident[:, :]),
                   reads=[r_mtok, r_const], writes=[r_pstb[st]])
            P.emit("s", lambda e, tt=tt, st=st: e.copy(m_f[0:8, tt * 128:(tt + 1) * 128],
                                                       ps_tb[st][0:8, 512:640]),
                   reads=[r_pstb[st]], writes=[r_mf])

    P.flush_block()
    st_b.close()
    pb.close()
    st_ab.close()

    CAP, NSC = c.CAP, c.NSC
    st_cA = ExitStack()
    Ec = st_cA.enter_context
    wubA = [Ec(nc.sbuf_tensor(f"wubA{i}", [128, 8 * 512], BF16)) for i in range(2)]
    wgbA = [Ec(nc.sbuf_tensor(f"wgbA{i}", [128, 8 * 512], BF16)) for i in range(2)]
    wdbA = [Ec(nc.sbuf_tensor(f"wdbA{i}", [128, 4 * 1024], BF16)) for i in range(2)]
    GW2 = min(2, NT4) * 512
    hidA = Ec(nc.sbuf_tensor("hidA", [128, 4 * GW2], BF16))
    siluA = Ec(nc.sbuf_tensor("siluA", [128, GW2], BF16))
    r_wubA = [R("wubA0"), R("wubA1")]
    r_wgbA = [R("wgbA0"), R("wgbA1")]
    r_wdbA = [R("wdbA0"), R("wdbA1")]
    r_hidA = [R(f"hidA{i}") for i in range(4)]
    r_siluA = R("siluA")

    pc = ExitStack()
    Ecp = pc.enter_context
    ps_up = Ecp(nc.psum_tensor("ps_up", [128, 1024], F32))
    ps_ga = Ecp(nc.psum_tensor("ps_ga", [128, 1024], F32))
    ps_dn = [Ecp(nc.psum_tensor(f"ps_dn{i}", [128, 1024], F32)) for i in range(2)]
    r_psup = R("psup")
    r_psga = R("psga")
    r_psdn = [R("psdn0"), R("psdn1")]

    r_gateB = R("gateB")
    b_alias = [r_oall, r_wout, r_bbc, r_xsq, r_exq, r_eqb, r_mtok] + \
        r_xhi + r_xlo + r_xlof + a_sbuf_res
    b_psum = [r_psu[0], r_psu[1], r_pslg] + r_pstb
    P.emit("t", lambda e: e.matmul(ps_up[0:1, 0:1], ones_fc[0:1, 0:1], ones_fc[0:1, 0:1],
                                   start=True, stop=True),
           writes=[r_gateB, r_psup] + b_alias + b_psum)

    experts = [-1] + list(range(c.ER))
    steps = [(e_, hg) for e_ in experts for hg in range(NHG)]

    def emit_wload(si):
        e_, hg = steps[si]
        buf = si % 2
        if e_ == -1:
            ups, gas, dns = wsup, wsgate, wsdown
            uo, do = hg * 512, hg * 512 * 1024
            ub, gb, db = wubA[buf], wgbA[buf], wdbA[buf]
            rub, rgb, rdb = r_wubA[buf], r_wgbA[buf], r_wdbA[buf]
            gate = []
        else:
            ups, gas, dns = wup, wgate, wdown
            uo, do = e_ * D * H + hg * 512, e_ * H * D + hg * 512 * 1024
            ub, gb, db = wub2[buf], wgb2[buf], wdb2[buf]
            rub, rgb, rdb = r_wub2[buf], r_wgb2[buf], r_wdb2[buf]
            gate = [r_gateC2] if si < NHG + 2 else []
        extra = (a_sbuf_res + [r_bbc, r_xhi[0], r_xlo[0], r_xlof[0], r_xsq,
                                r_exq, r_eqb, r_mtok, r_wout, r_oall]) if si < 2 else []
        P.dma("g", WUS[buf], lambda e, ups=ups, uo=uo, ub=ub: e.dma_start(
            ub[:, :].rearrange("p (dc j) -> p dc j", j=512),
            bass.AP(ups.tensor, uo, [[H, 128], [128 * H, ND], [1, 512]])),
            reads=gate, writes=[rub] + extra)
        P.dma("g", WGS[buf], lambda e, gas=gas, uo=uo, gb=gb: e.dma_start(
            gb[:, :].rearrange("p (dc j) -> p dc j", j=512),
            bass.AP(gas.tensor, uo, [[H, 128], [128 * H, ND], [1, 512]])),
            reads=gate, writes=[rgb])
        P.dma("g", WDS[buf], lambda e, dns=dns, do=do, db=db: e.dma_start(
            db[:, :].rearrange("p (hc j) -> p hc j", j=1024),
            bass.AP(dns.tensor, do, [[1024, 128], [128 * 1024, 4], [1, 1024]])),
            reads=gate, writes=[rdb])

    emit_wload(0)
    emit_wload(1)

    GS = min(2, NT4)
    groups = [tuple(range(g, min(g + GS, NT4))) for g in range(0, NT4, GS)]
    dn_ctr = [0]

    for si in range(NHG):
        hg = si
        buf = si % 2
        for grp in groups:
            gw = len(grp) * 512
            for hc in range(4):
                for dc in range(ND):
                    for ki, tt4 in enumerate(grp):
                        P.emit("t", lambda e, dc=dc, hc=hc, tt4=tt4, ki=ki, buf=buf: e.matmul(
                            ps_ga[:, ki * 512:ki * 512 + T4W],
                            wgbA[buf][:, dc * 512 + hc * 128:dc * 512 + hc * 128 + 128],
                            xfv[:, dc, tt4 * T4W:(tt4 + 1) * T4W],
                            start=(dc == 0), stop=(dc == ND - 1)),
                            reads=[r_wgbA[buf], r_xf], writes=[r_psga])
                P.emit("s", lambda e, gw=gw: e.activation(
                    siluA[:, 0:gw], ps_ga[:, 0:gw], AF.Silu),
                    reads=[r_psga, r_gateB], writes=[r_siluA])
                for dc in range(ND):
                    for ki, tt4 in enumerate(grp):
                        P.emit("t", lambda e, dc=dc, hc=hc, tt4=tt4, ki=ki, buf=buf: e.matmul(
                            ps_up[:, ki * 512:ki * 512 + T4W],
                            wubA[buf][:, dc * 512 + hc * 128:dc * 512 + hc * 128 + 128],
                            xfv[:, dc, tt4 * T4W:(tt4 + 1) * T4W],
                            start=(dc == 0), stop=(dc == ND - 1)),
                            reads=[r_wubA[buf], r_xf], writes=[r_psup])
                P.emit("v", lambda e, hc=hc, gw=gw: e.tensor_mul(
                    hidA[:, hc * GW2:hc * GW2 + gw], ps_up[:, 0:gw], siluA[:, 0:gw]),
                    reads=[r_psup, r_siluA, r_gateB], writes=[r_hidA[hc]])
            for ki, tt4 in enumerate(grp):
                for it in range(T4W // 128):
                    tt = tt4 * (T4W // 128) + it
                    par = dn_ctr[0] % 2
                    dn_ctr[0] += 1
                    for hc in range(4):
                        for half in range(2):
                            P.emit("t", lambda e, hc=hc, half=half, it=it, ki=ki, par=par,
                                   buf=buf: e.matmul(
                                ps_dn[par][:, half * 512:(half + 1) * 512],
                                hidA[:, hc * GW2 + ki * 512 + it * 128:
                                      hc * GW2 + ki * 512 + (it + 1) * 128],
                                wdbA[buf][:, hc * 1024 + half * 512:hc * 1024 + (half + 1) * 512],
                                start=(hc == 0), stop=(hc == 3)),
                                reads=[r_hidA[hc], r_wdbA[buf]], writes=[r_psdn[par]])
                    P.emit("v", lambda e, tt=tt, par=par: e.tensor_add(
                        x_res[:, tt * 1024:(tt + 1) * 1024],
                        x_res[:, tt * 1024:(tt + 1) * 1024],
                        ps_dn[par][:, :]), reads=[r_psdn[par], r_xres[tt]],
                        writes=[r_xres[tt]])
        if si + 2 < NHG:
            emit_wload(si + 2)

    st_cA.close()
    st_xf.close()
    st_c2 = ExitStack()
    Ec2 = st_c2.enter_context
    wub2 = [Ec2(nc.sbuf_tensor(f"wub2_{i}", [128, 8 * 512], BF16)) for i in range(2)]
    wgb2 = [Ec2(nc.sbuf_tensor(f"wgb2_{i}", [128, 8 * 512], BF16)) for i in range(2)]
    wdb2 = [Ec2(nc.sbuf_tensor(f"wdb2_{i}", [128, 4 * 1024], BF16)) for i in range(2)]
    stripe = [Ec2(nc.sbuf_tensor(f"stripe{i}", [128, TOK], BF16)) for i in range(2)]
    Pv = Ec2(nc.sbuf_tensor("Pv", [128, max(NT * CAP, 4 * D)], BF16))
    PTm = Ec2(nc.sbuf_tensor("PTm", [128, NSC * TOK], BF16))
    MBc = Ec2(nc.sbuf_tensor("MBc", [128, TOK], BF16))
    keyb = Ec2(nc.sbuf_tensor("keyb", [128, TOK], F32))
    xg = Ec2(nc.sbuf_tensor("xg", [128, ND * CAP], BF16))
    dg_bf = Ec2(nc.sbuf_tensor("dg_bf", [128, NSC * 1024], BF16))
    hid3 = Ec2(nc.sbuf_tensor("hid3", [128, 4 * CAP], BF16))
    silu_r = Ec2(nc.sbuf_tensor("silu_r", [128, CAP], BF16))
    mrow = Ec2(nc.sbuf_tensor("mrow", [1, TOK], BF16))
    maskr = Ec2(nc.sbuf_tensor("maskr", [1, TOK], F32))
    keyr = Ec2(nc.sbuf_tensor("keyr", [1, TOK], F32))
    kc_sb = Ec2(nc.sbuf_tensor("kc_sb", [128, NT], F32))
    iota1 = Ec2(nc.sbuf_tensor("iota1", [128, CAP], F32))
    iotaS = Ec2(nc.sbuf_tensor("iotaS", [128, NSC], F32))
    r_wub2 = [R("wub2_0"), R("wub2_1")]
    r_wgb2 = [R("wgb2_0"), R("wgb2_1")]
    r_wdb2 = [R("wdb2_0"), R("wdb2_1")]
    r_stripe = [R("stripe0"), R("stripe1")]
    r_Pv, r_PTm, r_MB, r_keyb = R("Pv"), R("PTm"), R("MB"), R("keyb")
    r_xg, r_dgbf = R("xg"), R("dgbf")
    r_hid3 = [R(f"hid3_{i}") for i in range(4)]
    r_silur, r_mrow, r_maskr, r_keyr = R("silur"), R("mrow"), R("maskr"), R("keyr")
    r_kc, r_iotac = R("kc"), R("iotac")

    r_gateC2 = R("gateC2")
    P.emit("t", lambda e: e.matmul(ps_up[0:1, 0:1], ones_fc[0:1, 0:1], ones_fc[0:1, 0:1],
                                   start=True, stop=True),
           reads=[r_const],
           writes=[r_gateC2, r_psup, r_xf, r_siluA] + r_wubA + r_wgbA + r_wdbA + r_hidA)
    P.emit("g", lambda e: e.iota(iota1[:, :], [[1, CAP]], base=1, channel_multiplier=0,
                                 allow_small_or_imprecise_dtypes=True),
           reads=[r_gateC2], writes=[r_iotac])
    P.emit("g", lambda e: e.iota(iotaS[:, :], [[128, NSC]], base=1, channel_multiplier=1,
                                 allow_small_or_imprecise_dtypes=True),
           reads=[r_gateC2], writes=[r_iotac])
    emit_wload(NHG)
    emit_wload(NHG + 1)

    all8 = [(ps_up, 0, r_psup), (ps_up, 512, r_psup),
            (ps_ga, 0, r_psga), (ps_ga, 512, r_psga),
            (ps_dn[0], 0, r_psdn[0]), (ps_dn[0], 512, r_psdn[0]),
            (ps_dn[1], 0, r_psdn[1]), (ps_dn[1], 512, r_psdn[1])]
    dn4 = all8[4:]
    ug4 = all8[:4]
    dnmap = [[all8[4], all8[5]], [all8[6], all8[7]], [all8[1], all8[3]]]

    def emit_stripe(dc):
        b = dc % 2
        P.dma("y", STR[b], lambda e, dc=dc, b=b: e.dma_start(
            stripe[b][:, :].rearrange("p (tc q) -> p tc q", q=128),
            bass.AP(xscr.tensor, dc * 128, [[1024, 128], [128 * 1024, NT], [1, 128]])),
            reads=[r_xscr, r_gateC2], writes=[r_stripe[b]])

    def emit_pro_mrow(e_):
        nch = TOK // 512
        for j in range(nch):
            pst, off, rp = ug4[j % 4]
            P.emit("t", lambda e, j=j, pst=pst, off=off, e_=e_: e.matmul(
                pst[0:1, off:off + 512], sel[0:8, e_ * 128:e_ * 128 + 1],
                m_f[0:8, j * 512:(j + 1) * 512], start=True, stop=True),
                reads=[r_const, r_mf, r_gateC2], writes=[rp])
            P.emit("s", lambda e, j=j, pst=pst, off=off: e.copy(
                mrow[0:1, j * 512:(j + 1) * 512], pst[0:1, off:off + 512]),
                reads=[rp, r_gateC2], writes=[r_mrow])
        P.emit("v", lambda e: e.tensor_single_scalar(
            maskr[0:1, :], mrow[0:1, :], 0.0, ALU.is_gt),
            reads=[r_mrow, r_gateC2], writes=[r_maskr])
        P.emit("v", lambda e: e.tensor_tensor_scan(
            keyr[0:1, :], maskr[0:1, :], maskr[0:1, :], 0.0, ALU.add, ALU.max),
            reads=[r_maskr, r_gateC2], writes=[r_keyr])
        P.emit("v", lambda e: e.tensor_mul(keyr[0:1, :], keyr[0:1, :], maskr[0:1, :]),
               reads=[r_keyr, r_maskr], writes=[r_keyr])

    def emit_pro_rest(e_):
        nch = TOK // 512
        for j in range(nch):
            pst, off, rp = ug4[j % 4]
            P.emit("t", lambda e, j=j, pst=pst, off=off: e.matmul(
                pst[:, off:off + 512], onesf32[0:1, :], keyr[0:1, j * 512:(j + 1) * 512],
                start=True, stop=True), reads=[r_const, r_keyr], writes=[rp])
            P.emit("v", lambda e, j=j, pst=pst, off=off: e.tensor_copy(
                keyb[:, j * 512:(j + 1) * 512], pst[:, off:off + 512]),
                reads=[rp, r_gateC2], writes=[r_keyb])
        for j in range(nch):
            pst, off, rp = ug4[j % 4]
            P.emit("t", lambda e, j=j, pst=pst, off=off: e.matmul(
                pst[:, off:off + 512], ones_bf[0:1, :], mrow[0:1, j * 512:(j + 1) * 512],
                start=True, stop=True), reads=[r_const, r_mrow], writes=[rp])
            P.emit("s", lambda e, j=j, pst=pst, off=off: e.copy(
                MBc[:, j * 512:(j + 1) * 512], pst[:, off:off + 512]),
                reads=[rp, r_gateC2], writes=[r_MB])
        for tc in range(NT):
            pst, off, rp = ug4[tc % 4]
            P.emit("t", lambda e, tc=tc, pst=pst, off=off: e.matmul(
                pst[:, off:off + 1], keyr[0:1, tc * 128:(tc + 1) * 128],
                onesf32[0:1, 0:1], start=True, stop=True),
                reads=[r_keyr, r_const], writes=[rp])
            P.emit("v", lambda e, tc=tc, pst=pst, off=off: e.tensor_copy(
                kc_sb[:, tc:tc + 1], pst[:, off:off + 1]),
                reads=[rp, r_gateC2], writes=[r_kc])
            P.emit("v", lambda e, tc=tc: e.tensor_single_scalar(
                Pv[:, tc * CAP:(tc + 1) * CAP], iota1[:, :], kc_sb[:, tc:tc + 1],
                ALU.is_equal), reads=[r_kc, r_iotac, r_gateC2], writes=[r_Pv])
        for sc in range(NSC):
            P.emit("v", lambda e, sc=sc: e.scalar_tensor_tensor(
                PTm[:, sc * TOK:(sc + 1) * TOK], keyb[:, :], iotaS[:, sc:sc + 1],
                MBc[:, :], ALU.is_equal, ALU.mult),
                reads=[r_keyb, r_MB, r_iotac, r_gateC2], writes=[r_PTm])

    def emit_gather(e_):
        emit_stripe(0)
        if ND > 1:
            emit_stripe(1)
        for dc in range(ND):
            pst, off, rp = all8[dc]
            b = dc % 2
            for tc in range(NT):
                P.emit("t", lambda e, dc=dc, tc=tc, pst=pst, off=off, b=b: e.matmul(
                    pst[:, off:off + CAP], stripe[b][:, tc * 128:(tc + 1) * 128],
                    Pv[:, tc * CAP:(tc + 1) * CAP], start=(tc == 0), stop=(tc == NT - 1)),
                    reads=[r_stripe[b], r_Pv], writes=[rp])
            P.emit("s", lambda e, dc=dc, pst=pst, off=off: e.copy(
                xg[:, dc * CAP:(dc + 1) * CAP], pst[:, off:off + CAP]),
                reads=[rp, r_gateC2], writes=[r_xg])
            if dc + 2 < ND:
                emit_stripe(dc + 2)

    def emit_rcompute(si):
        e_, hg = steps[si]
        buf = si % 2
        for hc in range(4):
            for dc in range(ND):
                P.emit("t", lambda e, hc=hc, dc=dc, buf=buf: e.matmul(
                    ps_ga[:, 0:CAP],
                    wgb2[buf][:, dc * 512 + hc * 128:dc * 512 + hc * 128 + 128],
                    xg[:, dc * CAP:(dc + 1) * CAP],
                    start=(dc == 0), stop=(dc == ND - 1)),
                    reads=[r_wgb2[buf], r_xg], writes=[r_psga])
            P.emit("s", lambda e: e.activation(silu_r[:, 0:CAP], ps_ga[:, 0:CAP], AF.Silu),
                   reads=[r_psga, r_gateC2], writes=[r_silur])
            for dc in range(ND):
                P.emit("t", lambda e, hc=hc, dc=dc, buf=buf: e.matmul(
                    ps_up[:, 0:CAP],
                    wub2[buf][:, dc * 512 + hc * 128:dc * 512 + hc * 128 + 128],
                    xg[:, dc * CAP:(dc + 1) * CAP],
                    start=(dc == 0), stop=(dc == ND - 1)),
                    reads=[r_wub2[buf], r_xg], writes=[r_psup])
            P.emit("v", lambda e, hc=hc: e.tensor_mul(
                hid3[:, hc * CAP:(hc + 1) * CAP], ps_up[:, 0:CAP], silu_r[:, 0:CAP]),
                reads=[r_psup, r_silur, r_gateC2], writes=[r_hid3[hc]])
            for sc in range(NSC):
                for half in range(2):
                    pst, off, rp = dnmap[sc][half]
                    P.emit("t", lambda e, hc=hc, sc=sc, half=half, buf=buf, pst=pst,
                           off=off, hg=hg: e.matmul(
                        pst[:, off:off + 512],
                        hid3[:, hc * CAP + sc * 128:hc * CAP + (sc + 1) * 128],
                        wdb2[buf][:, hc * 1024 + half * 512:hc * 1024 + (half + 1) * 512],
                        start=(hg == 0 and hc == 0), stop=(hg == NHG - 1 and hc == 3)),
                        reads=[r_hid3[hc], r_wdb2[buf]], writes=[rp])

    def emit_dgevac(e_):
        for sc in range(NSC):
            for half in range(2):
                pst, off, rp = dnmap[sc][half]
                P.emit("s", lambda e, sc=sc, half=half, pst=pst, off=off: e.copy(
                    dg_bf[:, sc * 1024 + half * 512:sc * 1024 + (half + 1) * 512],
                    pst[:, off:off + 512]), reads=[rp, r_gateC2], writes=[r_dgbf])

    def emit_scatter(e_):
        for tc in range(NT):
            par = tc % 2
            for sc in range(NSC):
                for half in range(2):
                    P.emit("t", lambda e, tc=tc, sc=sc, half=half, par=par: e.matmul(
                        ps_dn[par][:, half * 512:(half + 1) * 512],
                        PTm[:, sc * TOK + tc * 128:sc * TOK + (tc + 1) * 128],
                        dg_bf[:, sc * 1024 + half * 512:sc * 1024 + (half + 1) * 512],
                        start=(sc == 0), stop=(sc == NSC - 1)),
                        reads=[r_PTm, r_dgbf], writes=[r_psdn[par]])
            P.emit("v", lambda e, tc=tc, par=par: e.tensor_add(
                x_res[:, tc * 1024:(tc + 1) * 1024], x_res[:, tc * 1024:(tc + 1) * 1024],
                ps_dn[par][:, :]), reads=[r_psdn[par], r_xres[tc]], writes=[r_xres[tc]])

    emit_pro_mrow(0)
    emit_pro_rest(0)
    emit_gather(0)
    for e_ in range(c.ER):
        for hg in range(NHG):
            si = NHG + e_ * NHG + hg
            emit_rcompute(si)
            if si + 2 < len(steps):
                emit_wload(si + 2)
        emit_dgevac(e_)
        if e_ + 1 < c.ER:
            emit_pro_mrow(e_ + 1)
        emit_scatter(e_)
        if e_ + 1 < c.ER:
            emit_pro_rest(e_ + 1)
            emit_gather(e_ + 1)

    GB2 = min(4, NT)
    for g0 in range(0, NT, GB2):
        gi = g0 // GB2
        st = gi % 2
        rs = r_stat[st]
        sb_ = stat[st]
        for g in range(GB2):
            tt = g0 + g
            P.emit("v", lambda e, tt=tt, g=g, sb_=sb_: e.reduce_sum(
                sb_[:, g:g + 1], x_res[:, tt * 1024:(tt + 1) * 1024], axis=AX.X),
                reads=[r_xres[tt]], writes=[rs])
            P.emit("v", lambda e, tt=tt, g=g: e.tensor_mul(
                Pv[:, g * D:(g + 1) * D], x_res[:, tt * 1024:(tt + 1) * 1024],
                x_res[:, tt * 1024:(tt + 1) * 1024]),
                reads=[r_xres[tt], r_gateC2], writes=[r_Pv])
            P.emit("v", lambda e, g=g, sb_=sb_: e.reduce_sum(
                sb_[:, 8 + g:9 + g], Pv[:, g * D:(g + 1) * D], axis=AX.X),
                reads=[r_Pv], writes=[rs])
        P.emit("v", lambda e, sb_=sb_: e.tensor_scalar_mul(sb_[:, 16:16 + GB2], sb_[:, 0:GB2],
                                                           1.0 / D), reads=[rs], writes=[rs])
        P.emit("v", lambda e, sb_=sb_: e.tensor_scalar_mul(sb_[:, 24:24 + GB2],
                                                           sb_[:, 8:8 + GB2],
                                                           1.0 / D), reads=[rs], writes=[rs])
        P.emit("v", lambda e, sb_=sb_: e.tensor_mul(sb_[:, 32:32 + GB2], sb_[:, 16:16 + GB2],
                                                    sb_[:, 16:16 + GB2]), reads=[rs], writes=[rs])
        P.emit("v", lambda e, sb_=sb_: e.tensor_sub(sb_[:, 40:40 + GB2], sb_[:, 24:24 + GB2],
                                                    sb_[:, 32:32 + GB2]), reads=[rs], writes=[rs])
        P.emit("v", lambda e, sb_=sb_: e.tensor_scalar_add(sb_[:, 40:40 + GB2],
                                                           sb_[:, 40:40 + GB2], 1e-5),
               reads=[rs], writes=[rs])
        P.emit("s", lambda e, sb_=sb_: e.activation(sb_[:, 48:48 + GB2], sb_[:, 40:40 + GB2],
                                                    AF.Ln, bias=0.0, scale=1.0),
               reads=[rs], writes=[rs])
        P.emit("s", lambda e, sb_=sb_: e.activation(sb_[:, 56:56 + GB2], sb_[:, 48:48 + GB2],
                                                    AF.Exp, bias=0.0, scale=-0.5),
               reads=[rs], writes=[rs])
        for g in range(GB2):
            tt = g0 + g
            P.emit("v", lambda e, tt=tt, g=g, sb_=sb_: e.tensor_scalar(
                x_res[:, tt * 1024:(tt + 1) * 1024], x_res[:, tt * 1024:(tt + 1) * 1024],
                sb_[:, 16 + g:17 + g], sb_[:, 56 + g:57 + g], ALU.subtract, ALU.mult),
                reads=[rs, r_xres[tt]], writes=[r_xres[tt]])
            P.dma("y", STO, lambda e, tt=tt: e.dma_start(
                out[tt * 128:(tt + 1) * 128, :], x_res[:, tt * 1024:(tt + 1) * 1024]),
                reads=[r_xres[tt]])

    ops = P.finalize(3, [("y", STO)] + ([("g", DBG)] if c.debug else []))

    for ebd in ops:
        with nc.Block() as blk:
            blk.tensor(lambda e, d=ebd: [f(e) for f in d["t"]])
            blk.vector(lambda e, d=ebd: [f(e) for f in d["v"]])
            blk.scalar(lambda e, d=ebd: [f(e) for f in d["s"]])
            blk.gpsimd(lambda e, d=ebd: [f(e) for f in d["g"]])
            blk.sync(lambda e, d=ebd: [f(e) for f in d["y"]])

    pc.close()
    st_c2.close()
    st_bend.close()
    st0.close()
    return nc


def kernel(**inputs):
    c = CFG()
    x = np.asarray(inputs["L0_attn_data"])
    wm = {
        "wqkv": np.asarray(inputs["L0_attn_W_qkv"]),
        "bqkv": np.asarray(inputs["L0_attn_b_qkv"]),
        "wout": np.asarray(inputs["L0_attn_W_out"]),
        "bout": np.asarray(inputs["L0_attn_b_out"]),
        "wrt": np.asarray(inputs["L0_ffn_router_w"]),
        "wup": np.asarray(inputs["L0_ffn_up_w"]),
        "wgate": np.asarray(inputs["L0_ffn_gate_w"]),
        "wdown": np.asarray(inputs["L0_ffn_down_w"]),
        "wsup": np.asarray(inputs["L0_ffn_shared_up_w"]),
        "wsgate": np.asarray(inputs["L0_ffn_shared_gate_w"]),
        "wsdown": np.asarray(inputs["L0_ffn_shared_down_w"]),
    }
    nc = build_nc(c)
    in_maps = []
    for core in range(8):
        b, h = divmod(core, 2)
        own = x[b, h * 2048:(h + 1) * 2048]
        oth = x[b, (1 - h) * 2048:(2 - h) * 2048]
        m = {"xb": np.ascontiguousarray(np.concatenate([own, oth], axis=0))}
        m.update(wm)
        in_maps.append(m)
    res = run_bass_kernel_spmd(nc, in_maps, core_ids=list(range(8)))
    outp = np.empty((4, 4096, 1024), np.float32)
    for core in range(8):
        b, h = divmod(core, 2)
        outp[b, h * 2048:(h + 1) * 2048] = res.results[core]["out"]
    return outp
